# version 22
# baseline (speedup 1.0000x reference)
import numpy as np
import concourse.bass as bass
import concourse.mybir as mybir
from concourse.bass_utils import run_bass_kernel_spmd

F32 = mybir.dt.float32
BF16 = mybir.dt.bfloat16
AF = mybir.ActivationFunctionType
ALU = mybir.AluOpType
AX = mybir.AxisListType

T = 2048
NSEQ = 4
NCH = 8
NG = 4
GT = 512
NL = 64
EPS = 1e-5


class Ctx:

    def __init__(self, nc, active, sems, h):
        self.nc = nc
        self.active = active
        self.sems = sems
        self.h = h
        self.guard = {}
        self.pre_ident = None
        self._toks = {}

    def tok(self, name):
        return self._toks.get(name)

    def set_tok(self, name, tok):
        self._toks[name] = tok

    def op(self, engine, fn, *args, wait=(), inc=(), **kwargs):
        if self.active == engine:
            eng = getattr(self.nc, engine)
            wmax = {}
            for tk in wait:
                if tk is None:
                    continue
                nm, v = tk
                wmax[nm] = max(wmax.get(nm, 0), v)
            for nm, v in wmax.items():
                eng.wait_ge(self.sems[nm][0], v)
            instr = getattr(eng, fn)(*args, **kwargs)
            for nm, k in inc:
                instr.then_inc(self.sems[nm][0], k)
        toks = []
        for nm, k in inc:
            self.sems[nm][1] += k
            toks.append((nm, self.sems[nm][1]))
        return toks[0] if len(toks) == 1 else (toks if toks else None)

    def mm(self, out, lhsT, rhs, start, stop, wait=(), inc=()):
        return self.op("tensor", "matmul", out, lhsT, rhs,
                       start=start, stop=stop, wait=wait, inc=inc)

    def mmacc(self, bank, out, lhsT_list, rhs_list, wait_first=()):
        n = len(lhsT_list)
        tok = None
        for i in range(n):
            w = (list(wait_first) + [self.guard.get(bank)]) if i == 0 else ()
            tok = self.mm(out, lhsT_list[i], rhs_list[i],
                          start=(i == 0), stop=(i == n - 1),
                          wait=w, inc=((("T", 1),) if i == n - 1 else ()))
        return tok

    def bank_read(self, bank, tok):
        self.guard[bank] = tok
        return tok

    def tr(self, out, in_, wait=(), inc=()):
        return self.op("tensor", "transpose", out, in_, self.h["ident"][:, :],
                       wait=wait, inc=inc)

    def act(self, out, in_, func, wait=(), inc=(("S", 1),), **kw):
        return self.op("scalar", "activation", out, in_, func,
                       wait=wait, inc=inc, **kw)

    def vcopy(self, out, in_, wait=(), inc=(("V", 1),)):
        return self.op("vector", "tensor_copy", out, in_, wait=wait, inc=inc)

    def tt(self, out, in0, in1, alu, wait=(), inc=(("V", 1),)):
        return self.op("vector", "tensor_tensor", out, in0, in1, alu,
                       wait=wait, inc=inc)

    def ts(self, out, in0, scalar, alu, wait=(), inc=(("V", 1),)):
        return self.op("vector", "tensor_single_scalar", out, in0, scalar, alu,
                       wait=wait, inc=inc)

    def stt(self, out, in0, scalar, in1, op0, op1, wait=(), inc=(("V", 1),)):
        return self.op("vector", "scalar_tensor_tensor", out, in0, scalar, in1,
                       op0, op1, wait=wait, inc=inc)

    def red(self, out, in_, alu, wait=(), inc=(("V", 1),), **kw):
        return self.op("vector", "tensor_reduce", out, in_, AX.X, alu,
                       wait=wait, inc=inc, **kw)

    def gcopy(self, out, in_, wait=(), inc=(("G", 1),)):
        return self.op("gpsimd", "tensor_copy", out, in_, wait=wait, inc=inc)

    def gtt(self, out, in0, in1, alu, wait=(), inc=(("G", 1),)):
        return self.op("gpsimd", "tensor_tensor", out, in0, in1, alu,
                       wait=wait, inc=inc)

    def gstt(self, out, in0, scalar, in1, op0, op1, wait=(), inc=(("G", 1),)):
        return self.op("gpsimd", "scalar_tensor_tensor", out, in0, scalar, in1,
                       op0, op1, wait=wait, inc=inc)

    def dma(self, queue, out, in_, wait=(), sem="XO", **kw):
        return self.op(queue, "dma_start", out=out, in_=in_,
                       wait=wait, inc=((sem, 16),), **kw)


def c3(ap2d, n=GT):
    return ap2d.rearrange("p (c n) -> p c n", c=NCH)



def chain_p01(ctx, g, row_a, row_b, gates, cast_eng="vector"):
    h = ctx.h
    gsl = slice(g * GT, (g + 1) * GT)
    st = dict(g=g, row_a=row_a, row_b=row_b)
    s1_tok = s2_tok = None
    for c in range(NCH):
        cp = slice((c % 2) * GT, (c % 2 + 1) * GT)
        cwait = list(gates) + [ctx.tok("cs%d" % (c % 2)),
                               ctx.tok("csq%d" % (c % 2))]
        if cast_eng == "vector":
            cast_t = ctx.vcopy(h["castb"][:, cp],
                               c3(h["x"][:, :], T)[:, c, gsl], wait=cwait)
        else:
            cast_t = ctx.act(h["castb"][:, cp],
                             c3(h["x"][:, :], T)[:, c, gsl], AF.Copy,
                             wait=cwait)
        sq_t = ctx.act(h["sqb"][:, cp], h["castb"][:, cp], AF.Square,
                       wait=[cast_t, ctx.tok("sq%d" % (c % 2))])
        ctx.set_tok("csq%d" % (c % 2), sq_t)
        s1_tok = ctx.mm(h["G2"][0:1, 0:GT], h["ones_col"][:, :],
                        h["castb"][:, cp], start=(c == 0), stop=(c == NCH - 1),
                        wait=[cast_t] +
                             ([ctx.guard.get("G2")] if c == 0 else []),
                        inc=(("T", 1),))
        ctx.set_tok("cs%d" % (c % 2), s1_tok)
        s2_tok = ctx.mm(h["G3"][0:1, 0:GT], h["ones_col"][:, :],
                        h["sqb"][:, cp], start=(c == 0), stop=(c == NCH - 1),
                        wait=[sq_t] +
                             ([ctx.guard.get("G3")] if c == 0 else []),
                        inc=(("T", 1),))
        ctx.set_tok("sq%d" % (c % 2), s2_tok)
    ev1 = ctx.bank_read("G2", ctx.act(h["m_row"][:, gsl], h["G2"][0:1, 0:GT],
                                      AF.Copy, scale=1.0 / 1024.0,
                                      wait=[s1_tok, ctx.tok("mrow")]))
    ev2 = ctx.bank_read("G3", ctx.act(h["s2_row"][:, gsl], h["G3"][0:1, 0:GT],
                                      AF.Copy, scale=1.0 / 1024.0,
                                      wait=[s2_tok]))
    m = h["m_row"][:, gsl]
    s2m = h["s2_row"][:, gsl]
    t_m2 = ctx.tt(h["stat_m2"], m, m, ALU.mult, wait=[ev1, ctx.tok("m2")])
    t_var = ctx.tt(s2m, s2m, h["stat_m2"], ALU.subtract, wait=[ev2, t_m2])
    ctx.set_tok("m2", t_var)
    t_ln = ctx.act(h["stat_ln"], s2m, AF.Ln, bias=h["cst_eps"][0:1, 0:1],
                   wait=[t_var, ctx.tok("lnx")])
    t_a = ctx.act(h["ab_r"][row_a][:, gsl], h["stat_ln"], AF.Exp, scale=-0.5,
                  wait=[t_ln])
    ctx.set_tok("lnx", t_a)
    t_b = ctx.tt(h["ab_r"][row_b][:, gsl], m, h["ab_r"][row_a][:, gsl],
                 ALU.mult, wait=[t_a])
    ctx.set_tok("mrow", t_b)
    st["t_a"] = t_a
    st["t_b"] = t_b
    return st


def chain_p2(ctx, st):
    h = ctx.h
    g, row_a, row_b = st["g"], st["row_a"], st["row_b"]
    gsl = slice(g * GT, (g + 1) * GT)
    par = g % 2
    psl = slice(par * GT, (par + 1) * GT)
    mm_a = ctx.mm(h["G2"][:, 0:GT], h["onesr_at"][h["ab_bp"][row_a]],
                  h["ab_r"][row_a][:, gsl], start=True, stop=True,
                  wait=[st["t_a"], ctx.guard.get("G2")], inc=(("T", 1),))
    mm_b = ctx.mm(h["G3"][:, 0:GT], h["onesr_at"][h["ab_bp"][row_b]],
                  h["ab_r"][row_b][:, gsl], start=True, stop=True,
                  wait=[st["t_b"], ctx.guard.get("G3")], inc=(("T", 1),))
    ev_a = ctx.bank_read("G2", ctx.act(h["ab_a"][:, psl], h["G2"][:, 0:GT],
                                       AF.Copy,
                                       wait=[mm_a, ctx.tok("aba%d" % par)]))
    ev_b = ctx.bank_read("G3", ctx.act(h["ab_b"][:, psl], h["G3"][:, 0:GT],
                                       AF.Copy,
                                       wait=[mm_b, ctx.tok("abb%d" % par)]))
    return dict(ev_a=ev_a, ev_b=ev_b, par=par, b_tok=st["t_b"])


def emit_ln_chain(ctx, g, row_a, row_b, gates):
    return chain_p2(ctx, chain_p01(ctx, g, row_a, row_b, gates))


def emit_apply(ctx, g, chain, extra_wait=(), dst=None):
    h = ctx.h
    dst = dst if dst is not None else h["ln"]
    gsl = slice(g * GT, (g + 1) * GT)
    par = chain["par"]
    psl = slice(par * GT, (par + 1) * GT)
    d3 = c3(dst[:, :])
    t1 = ctx.tt(d3, c3(h["x"][:, :], T)[:, :, gsl],
                h["ab_a"][:, psl].unsqueeze(1).broadcast_to([128, NCH, GT]),
                ALU.mult, wait=[chain["ev_a"]] + list(extra_wait))
    tok = ctx.tt(d3, d3,
                 h["ab_b"][:, psl].unsqueeze(1).broadcast_to([128, NCH, GT]),
                 ALU.subtract, wait=[chain["ev_b"], t1])
    ctx.set_tok("aba%d" % par, tok)
    ctx.set_tok("abb%d" % par, tok)
    return tok


def load_weight(ctx, slot, bf_name, wait=()):
    h = ctx.h
    return ctx.dma("sync", out=h[slot][:, :], in_=h[bf_name][:, :],
                   wait=wait, sem="W" + slot[-1])


def cast_weight_sbuf(ctx, slot, hbm_name, wait=()):
    h = ctx.h
    return ctx.dma("gpsimd",
                   out=h[slot][:, :].rearrange("p (c d) -> p c d", c=NCH),
                   in_=h[hbm_name][:, :, :].rearrange("c p d -> p c d"),
                   wait=wait, sem="W" + slot[-1])


def cast_weight_hbm(ctx, bf_name, hbm_name, wait=()):
    h = ctx.h
    return ctx.dma("gpsimd",
                   out=h[bf_name][:, :].rearrange("p (c d) -> p c d", c=NCH),
                   in_=h[hbm_name][:, :, :].rearrange("c p d -> p c d"),
                   wait=wait, sem="WC")


def wtile(ctx, slot, c, dd):
    return ctx.h[slot][:, c * 1024 + dd * 128: c * 1024 + (dd + 1) * 128]


def proj_mms(ctx, slot, w_tok, rhs_buf, rhs_tok, extra_first=()):
    h = ctx.h
    for dd in range(NCH):
        bank = "G0" if dd % 2 == 0 else "G1"
        tok = ctx.mmacc(bank, h[bank][:, 0:GT],
                        [wtile(ctx, slot, c, dd) for c in range(NCH)],
                        [c3(rhs_buf[:, :])[:, c, :] for c in range(NCH)],
                        wait_first=[rhs_tok, w_tok] + list(extra_first))
        yield dd, bank, tok



def new_ffn_state():
    return dict(chains={}, applied={}, diff={}, lnprev={}, mix_r=None,
                mix_k=None)


def ffn_prep(ctx, g, st, extra_wait=()):
    h = ctx.h
    if -1 not in st["lnprev"]:
        st["lnprev"][-1] = ctx.op("vector", "memset",
                                  h["lnprev"][:, NCH:2 * NCH], 0.0,
                                  inc=(("V", 1),))
    st["applied"][g] = emit_apply(ctx, g, st["chains"][g],
                                  extra_wait=list(extra_wait))
    lp_w = slice((g % 2) * NCH, (g % 2) * NCH + NCH)
    st["lnprev"][g] = ctx.vcopy(h["lnprev"][:, lp_w].unsqueeze(2),
                                c3(h["ln"][:, :])[:, :, GT - 1:GT],
                                wait=[st["applied"][g]])
    d1 = ctx.tt(c3(h["diff"][:, :])[:, :, 1:GT],
                c3(h["ln"][:, :])[:, :, 1:GT],
                c3(h["ln"][:, :])[:, :, 0:GT - 1], ALU.subtract,
                wait=[st["applied"][g], ctx.tok("difffree")])
    lp_r = slice(((g - 1) % 2) * NCH, ((g - 1) % 2) * NCH + NCH)
    st["diff"][g] = ctx.tt(c3(h["diff"][:, :])[:, :, 0:1],
                           c3(h["ln"][:, :])[:, :, 0:1],
                           h["lnprev"][:, lp_r].unsqueeze(2), ALU.subtract,
                           wait=[d1, st["lnprev"][g - 1]])


def ffn_mixes(ctx, g, which, guard_tok, st):
    h = ctx.h
    mu = h["mur"] if which == "r" else h["muk"]
    dst = h["xr"] if which == "r" else h["xk"]
    lp_r = slice(((g - 1) % 2) * NCH, ((g - 1) % 2) * NCH + NCH)
    tok = None
    for c in range(NCH):
        ctx.stt(c3(dst[:, :])[:, c, 1:GT],
                c3(h["diff"][:, :])[:, c, 1:GT], mu[:, c:c + 1],
                c3(h["ln"][:, :])[:, c, 0:GT - 1], ALU.mult, ALU.add,
                wait=[st["diff"][g], guard_tok] if c == 0 else ())
        tok = ctx.stt(c3(dst[:, :])[:, c, 0:1],
                      c3(h["diff"][:, :])[:, c, 0:1], mu[:, c:c + 1],
                      h["lnprev"][:, lp_r][:, c:c + 1].unsqueeze(2),
                      ALU.mult, ALU.add, wait=())
    return tok


def emit_ffn(ctx, s, slots, w_toks, out_dma, gates, st=None, tails=None,
             out_xload=None):
    h = ctx.h
    st = st or new_ffn_state()
    tails = tails or {}
    pend = {}

    def chain_full(g):
        if g not in st["chains"]:
            st["chains"][g] = emit_ln_chain(ctx, g, 0, 1, gates[g])
    chain_full(0)
    if "pend1" not in st:
        chain_full(1)
    if 0 not in st["applied"]:
        ffn_prep(ctx, 0, st, extra_wait=[ctx.tok("lnfree")])
    mix_r = st["mix_r"] or ffn_mixes(ctx, 0, "r", ctx.tok("xrfree"), st)
    mix_k = st["mix_k"] or ffn_mixes(ctx, 0, "k", ctx.tok("xkfree"), st)
    res_toks = [None] * NG
    out_toks = [None] * NG
    xload_toks = [None] * NG
    prod_t = ctx.tok("rgfree")
    kact_free = ctx.tok("kactfree")
    last_r = last_k = last_v = None

    def run_tails(g, point):
        for cb in tails.get((g, point), ()):
            cb(res_toks, xload_toks)

    for g in range(NG):
        gsl = slice(g * GT, (g + 1) * GT)
        sig = {}
        for dd, bank, tok in proj_mms(ctx, slots[0], w_toks[0], h["xr"],
                                      mix_r):
            sig[dd] = ctx.bank_read(bank, ctx.act(
                c3(h["rg"][:, :])[:, dd, :], h[bank][:, 0:GT], AF.Sigmoid,
                wait=[tok, prod_t]))
            last_r = tok
        if g == NG - 1:
            ctx.set_tok("xrfree", last_r)
        if g == 0 and "pend1" in st:
            st["chains"][1] = chain_p2(ctx, st.pop("pend1"))
        if g + 1 < NG:
            ffn_prep(ctx, g + 1, st, extra_wait=[mix_k])
            mix_r = ffn_mixes(ctx, g + 1, "r", last_r, st)
        run_tails(g, "mid")
        sq_t = None
        for dd, bank, tok in proj_mms(ctx, slots[1], w_toks[1], h["xk"],
                                      mix_k):
            rtok = ctx.bank_read(bank, ctx.act(
                c3(h["krelu"][:, :])[:, dd, :], h[bank][:, 0:GT], AF.Relu,
                wait=[tok, prod_t]))
            sq_t = ctx.act(c3(h["kact"][:, :])[:, dd, :],
                           c3(h["krelu"][:, :])[:, dd, :], AF.Square,
                           wait=[rtok, kact_free if dd == 0 else None,
                                 last_v if dd == 0 else None])
            last_k = tok
        kact_free = None
        if g == NG - 1:
            ctx.set_tok("xkfree", last_k)
        if g + 2 < NG and g + 2 not in st["chains"]:
            pend[g + 2] = chain_p01(ctx, g + 2, 0, 1, gates[g + 2])
        else:
            run_tails(g, "mid2")
        if g + 1 < NG:
            mix_k = ffn_mixes(ctx, g + 1, "k", last_k, st)
            if g + 1 == NG - 1:
                ctx.set_tok("lnfree", mix_k)
                ctx.set_tok("difffree", mix_k)
        pres = {}
        for dd, bank, tok in proj_mms(ctx, slots[2], w_toks[2], h["kact"],
                                      sq_t):
            vev = ctx.bank_read(bank, ctx.act(
                c3(h["krelu"][:, :])[:, dd, :], h[bank][:, 0:GT], AF.Copy,
                scale=0.5, wait=[tok, sq_t]))
            psl2 = slice((dd % 2) * GT, (dd % 2) * GT + GT)
            ptok = ctx.tt(h["stg"][:, psl2], c3(h["rg"][:, :])[:, dd, :],
                          c3(h["krelu"][:, :])[:, dd, :], ALU.mult,
                          wait=[vev] + ([sig[NCH - 1], prod_t]
                                        if dd == 0 else []) +
                               ([pres[dd - 2]] if dd >= 2 else []))
            pres[dd] = ctx.tt(c3(h["x"][:, :], T)[:, dd, gsl],
                              c3(h["x"][:, :], T)[:, dd, gsl],
                              h["stg"][:, psl2], ALU.add, wait=[ptok])
            last_v = tok
        res_toks[g] = pres[NCH - 1]
        prod_t = res_toks[g]
        if g + 2 in pend:
            st["chains"][g + 2] = chain_p2(ctx, pend.pop(g + 2))
        run_tails(g, "post")
        if out_dma:
            out_toks[g] = ctx.dma(
                "sync",
                out=h["out_hbm"][s, :, :, gsl].rearrange("c p t -> p c t"),
                in_=c3(h["x"][:, :], T)[:, :, gsl],
                wait=[res_toks[g]], sem="XO%d" % g)
            if out_xload is not None:
                ns = out_xload[0]
                xload_toks[g] = ctx.dma(
                    "sync",
                    out=c3(h["x"][:, :], T)[:, :, gsl],
                    in_=h["x_hbm"][ns, :, :, gsl].rearrange("c p t -> p c t"),
                    wait=[out_toks[g]], sem="XI%d" % g)
        run_tails(g, "post2")
    ctx.set_tok("rgfree", prod_t)
    ctx.set_tok("kactfree", last_v)
    return dict(slot_last=(last_r, last_k, last_v), res=res_toks,
                out=out_toks, xloads=xload_toks)



def emit_attn(ctx, s, slots, w_toks, gates, pre_chains=None, pre_applied=None,
              tail_emits=None):
    h = ctx.h
    expk_buf = h["rg"]
    v_buf = h["krelu"]
    kern3_buf = h["kact"]
    ksm_buf = h["ksm_g"]
    k3t_buf = h["k3t_ev"]
    lnb1 = (h["ln"], h["xk"])
    lnbufs = (h["ln"], h["ksm_g"])
    rg_free = ctx.tok("rgfree")

    chains = [None] * NG
    if pre_chains:
        chains[0] = pre_chains[0]
        c1 = pre_chains[1]
        chains[1] = c1 if "ev_a" in c1 else chain_p2(ctx, c1)
    else:
        chains[0] = emit_ln_chain(ctx, 0, 2, 3, gates[0])
        chains[1] = emit_ln_chain(ctx, 1, 2, 3, gates[1])
    applied = pre_applied or emit_apply(ctx, 0, chains[0],
                                        extra_wait=[ctx.tok("lnfree")])
    pend_a = None

    ksm_t = None
    kern3_t = None
    pool_t = ctx.tok("poolfree")
    ksm_dma = [None] * NG
    k3t_dma = [None] * NG
    tpose_done = [None] * NG
    vmm_by = [None] * NG
    last_kmm = None
    last_exp = last_vev = None
    pend_tpose = None
    ln3_t = None

    def do_tpose(gp, k3_tok, spill=True):
        last = None
        ev = None
        evs = []
        prev_spill = k3t_dma[gp - 1] if gp >= 1 else ctx.tok("k3evfree")
        for tl in range(4):
            half = "trA0" if tl % 2 == 0 else "trA1"
            hap = h["trA"][:, (tl % 2) * 1024:(tl % 2) * 1024 + 1024]
            tk = None
            for c in range(NCH):
                tk = ctx.tr(hap[:, c * 128:(c + 1) * 128],
                            c3(kern3_buf[:, :])[:, c, tl * 128:(tl + 1) * 128],
                            wait=([k3_tok, ctx.guard.get(half),
                                   ctx.pre_ident] if c == 0 else ()),
                            inc=((("T", 1),) if c == NCH - 1 else ()))
            ev = ctx.bank_read(half, ctx.act(
                k3t_buf[:, tl * 1024:(tl + 1) * 1024], hap, AF.Copy,
                wait=[tk, prev_spill]))
            evs.append(ev)
            last = tk
        dma = None
        if spill:
            dma = ctx.dma("sync",
                          out=h["k3t_scr"][gp * 4:(gp + 1) * 4, :, :]
                              .rearrange("t p d -> p t d"),
                          in_=k3t_buf[:, :].rearrange("p (t d) -> p t d", t=4),
                          wait=[ev, k3t_dma[gp - 1] if gp >= 1 else None],
                          sem="K3")
        return last, dma, evs

    def rebcast(g):
        par = g % 2
        psl = slice(par * GT, (par + 1) * GT)
        gsl = slice(g * GT, (g + 1) * GT)
        mm_a = ctx.mm(h["G2"][:, 0:GT], h["onesr_at"][h["ab_bp"][2]],
                      h["ab_r"][2][:, gsl], start=True, stop=True,
                      wait=[chains[g]["b_tok"], ctx.guard.get("G2")],
                      inc=(("T", 1),))
        mm_b = ctx.mm(h["G3"][:, 0:GT], h["onesr_at"][h["ab_bp"][3]],
                      h["ab_r"][3][:, gsl], start=True, stop=True,
                      wait=[chains[g]["b_tok"], ctx.guard.get("G3")],
                      inc=(("T", 1),))
        ev_a = ctx.bank_read("G2", ctx.act(h["ab_a"][:, psl], h["G2"][:, 0:GT],
                                           AF.Copy,
                                           wait=[mm_a, ctx.tok("aba%d" % par)]))
        ev_b = ctx.bank_read("G3", ctx.act(h["ab_b"][:, psl], h["G3"][:, 0:GT],
                                           AF.Copy,
                                           wait=[mm_b, ctx.tok("abb%d" % par)]))
        return dict(ev_a=ev_a, ev_b=ev_b, par=par)

    for g in range(NG):
        gsl = slice(g * GT, (g + 1) * GT)
        ln_t = applied
        exp = {}
        for dd, bank, tok in proj_mms(ctx, slots[1], w_toks[1], lnb1[g % 2],
                                      ln_t):
            exp[dd] = ctx.bank_read(bank, ctx.act(
                c3(expk_buf[:, :])[:, dd, :], h[bank][:, 0:GT], AF.Exp,
                bias=h["biask"][:, dd:dd + 1],
                wait=[tok, ksm_t, rg_free if g == 0 else None]))
            last_kmm = tok
        last_exp = exp[NCH - 1]
        if g + 2 < NG:
            pend_a = chain_p01(ctx, g + 2, 2, 3, gates[g + 2])
        vev = {}
        for dd, bank, tok in proj_mms(ctx, slots[2], w_toks[2], lnb1[g % 2],
                                      ln_t):
            vev[dd] = ctx.bank_read(bank, ctx.act(
                c3(v_buf[:, :])[:, dd, :], h[bank][:, 0:GT], AF.Identity,
                bias=h["bv"][:, dd:dd + 1],
                wait=[tok, kern3_t, rg_free if g == 0 else None]))
            vmm_by[g] = tok
        last_vev = vev[NCH - 1]
        sum_mm = ctx.mmacc("G2", h["G2"][0:1, 0:GT],
                           [h["ones_col"][:, :]] * NCH,
                           [c3(expk_buf[:, :])[:, c, :] for c in range(NCH)],
                           wait_first=[last_exp])
        ln_s = ctx.bank_read("G2", ctx.act(h["inv_f0"], h["G2"][0:1, 0:GT],
                                           AF.Ln, wait=[sum_mm]))
        inv_t = ctx.act(h["invb0"], h["inv_f0"], AF.Exp, scale=-1.0,
                        wait=[ln_s])
        invbc_mm = ctx.mm(h["G2"][:, 0:GT], h["onesr_at"][0], h["invb0"],
                          start=True, stop=True,
                          wait=[inv_t, ctx.guard.get("G2")], inc=(("T", 1),))
        invbc_ev = ctx.bank_read("G2", ctx.act(h["invb"][:, :],
                                               h["G2"][:, 0:GT], AF.Copy,
                                               wait=[invbc_mm, ksm_t]))
        if pend_a is not None:
            chains[pend_a["g"]] = chain_p2(ctx, pend_a)
            pend_a = None
        if g == NG - 1:
            ch0 = rebcast(0)
            ln3_t = emit_apply(ctx, 0, ch0, extra_wait=[vmm_by[NG - 2]],
                               dst=lnbufs[0])
        if g + 1 < NG:
            applied = emit_apply(ctx, g + 1, chains[g + 1],
                                 extra_wait=[vmm_by[g - 1] if g >= 1
                                             else ctx.tok("xkfree")],
                                 dst=lnb1[(g + 1) % 2])
        if pend_tpose is not None:
            gp, tokp = pend_tpose
            tpose_done[gp] = do_tpose(gp, tokp)
            k3t_dma[gp] = tpose_done[gp][1]
        ksm_t = ctx.tt(c3(ksm_buf[:, :]), c3(expk_buf[:, :]),
                       h["invb"][:, :].unsqueeze(1).broadcast_to([128, NCH, GT]),
                       ALU.mult,
                       wait=[invbc_ev, last_exp,
                             ksm_dma[g - 1] if g else ctx.tok("ksmfree")])
        kern3_t = ctx.tt(c3(kern3_buf[:, :]), c3(ksm_buf[:, :]),
                         c3(v_buf[:, :]), ALU.mult,
                         wait=[ksm_t, last_vev,
                               tpose_done[g - 1][0] if g
                               else ctx.tok("kactfree")])
        poolr = ctx.red(h["kland_f"][:, :].rearrange("p (c l) -> p c l", c=NCH),
                        ksm_buf[:, :].rearrange("p (c o i) -> p c o i",
                                                c=NCH, i=32),
                        ALU.add, wait=[ksm_t, pool_t])
        pool_t = ctx.vcopy(h["kland"][:, :].rearrange(
                               "p (c l) -> p c l", c=NCH)[:, :, g * 16:(g + 1) * 16],
                           h["kland_f"][:, :].rearrange("p (c l) -> p c l",
                                                        c=NCH),
                           wait=[poolr])
        ksm_dma[g] = ctx.dma("sync",
                             out=h["ksm_scr"][:, :, gsl]
                                 .rearrange("c p t -> p c t"),
                             in_=c3(ksm_buf[:, :]), wait=[ksm_t], sem="KS")
        pend_tpose = (g, kern3_t)
    ctx.set_tok("poolfree", None)
    ctx.set_tok("rgfree", None)

    expq_buf = h["rg"]
    qsm_buf = h["krelu"]
    k1_evs = [None] * NG
    k1tok_by = [None] * NG
    qmm_by = [None] * NG
    qsm_t = None
    rd_toks = [None] * NG
    mid_mm = None

    def emit_k1(gq, qsm_tok):
        gsl = slice(gq * GT, (gq + 1) * GT)
        k1tok_by[gq] = ctx.mmacc(
            "G3", h["G3"][0:NL, 0:GT],
            [h["kland"][:, :].rearrange("p (c l) -> p c l", c=NCH)[:, c, :]
             for c in range(NCH)],
            [c3(qsm_buf[:, :])[:, c, :] for c in range(NCH)],
            wait_first=[qsm_tok, pool_t])
        k1_evs[gq] = ctx.bank_read("G3", ctx.vcopy(h["kern1T"][0:NL, gsl],
                                                   h["G3"][0:NL, 0:GT],
                                                   wait=[k1tok_by[gq]]))

    def emit_rd1(q, extra):
        return ctx.dma("sync",
                       out=c3(h["xk"][:, :]),
                       in_=h["ksm_scr"][:, :, q * GT:(q + 1) * GT]
                           .rearrange("c p t -> p c t"),
                       wait=[ksm_dma[NG - 1]] + extra, sem="RK0")

    def emit_rd2(q, extra):
        return ctx.dma("sync",
                       out=h["kact"][:, :].rearrange("p (t d) -> p t d", t=4),
                       in_=h["k3t_scr"][q * 4:(q + 1) * 4, :, :]
                           .rearrange("t p d -> p t d"),
                       wait=[k3t_dma[q]] + extra, sem="RK1")

    nt_ctr = [0]

    def emit_chunk(q, rd1, rd2, xr_evs=None):
        nonlocal mid_mm
        for tl in range(4):
            nt = nt_ctr[0]
            nt_ctr[0] += 1
            bank = "G0" if tl % 2 == 0 else "G1"
            ktok = ctx.mmacc(bank, h[bank][:, 0:NL],
                             [c3(h["xk"][:, :])[:, c, tl * 128:(tl + 1) * 128]
                              for c in range(NCH)],
                             [h["kland"][:, :].rearrange("p (c l) -> p c l",
                                                         c=NCH)[:, c, :]
                              for c in range(NCH)],
                             wait_first=[rd1, rd2, pool_t])
            k2_ev = ctx.bank_read(bank, ctx.vcopy(
                h["kern2T"][:, (nt % 2) * NL:(nt % 2 + 1) * NL],
                h[bank][:, 0:NL], wait=[ktok]))
            rhs_buf = k3t_buf if xr_evs is not None else h["kact"]
            for hh in range(2):
                mid_mm = ctx.mm(
                    h["M"][:, hh * 512:(hh + 1) * 512],
                    h["kern2T"][:, (nt % 2) * NL:(nt % 2 + 1) * NL],
                    rhs_buf[:, tl * 1024 + hh * 512:tl * 1024 + (hh + 1) * 512],
                    start=(nt == 0), stop=(nt == 15),
                    wait=([k2_ev, ctx.guard.get("M")] +
                          ([xr_evs[tl]] if xr_evs is not None else [])
                          if hh == 0 else ()),
                    inc=(("T", 1),) if hh == 1 else ())

    ln3_next = None
    for g in range(NG):
        if g + 1 < NG:
            ch = rebcast(g + 1)
            ln3_next = emit_apply(ctx, g + 1, ch,
                                  extra_wait=[qmm_by[g - 1] if g >= 1
                                              else ksm_dma[NG - 1]],
                                  dst=lnbufs[(g + 1) % 2])
        exq = {}
        for dd, bank, tok in proj_mms(
                ctx, slots[0], w_toks[0], lnbufs[g % 2], ln3_t,
                extra_first=([last_exp, ksm_t] if g == 0 else ())):
            exq[dd] = ctx.bank_read(bank, ctx.act(
                c3(expq_buf[:, :])[:, dd, :], h[bank][:, 0:GT], AF.Exp,
                bias=h["biasq"][:, dd:dd + 1], wait=[tok, qsm_t]))
            qmm_by[g] = tok
        if g == 0:
            gp, tokp = pend_tpose
            tpose_done[gp] = do_tpose(gp, tokp, spill=False)
            rd1 = emit_rd1(3, [vmm_by[NG - 1]])
            emit_chunk(3, rd1, None, xr_evs=tpose_done[gp][2])
            rd_toks[0] = (emit_rd1(0, [mid_mm]),
                          emit_rd2(0, [tpose_done[gp][0]]))
        else:
            emit_k1(g - 1, qsm_t)
            emit_chunk(g - 1, *rd_toks[g - 1])
            if g < NG - 1:
                rd_toks[g] = (emit_rd1(g, [mid_mm]), emit_rd2(g, [mid_mm]))
        ln3_t = ln3_next
        sum_mm = ctx.mmacc("G2", h["G2"][0:1, 0:GT],
                           [h["ones_col"][:, :]] * NCH,
                           [c3(expq_buf[:, :])[:, c, :] for c in range(NCH)],
                           wait_first=[exq[NCH - 1]])
        ln_s = ctx.bank_read("G2", ctx.act(h["inv_f1"], h["G2"][0:1, 0:GT],
                                           AF.Ln, wait=[sum_mm]))
        inv3 = ctx.act(h["invb1"], h["inv_f1"], AF.Exp, scale=-1.0,
                       wait=[ln_s])
        invbc_mm = ctx.mm(h["G2"][:, 0:GT], h["onesr_at"][0], h["invb1"],
                          start=True, stop=True,
                          wait=[inv3, ctx.guard.get("G2")], inc=(("T", 1),))
        invbc_ev = ctx.bank_read("G2", ctx.act(
            h["invb"][:, :], h["G2"][:, 0:GT], AF.Copy,
            wait=[invbc_mm, qsm_t, ksm_t if g == 0 else None]))
        qsm_t = ctx.tt(c3(qsm_buf[:, :]), c3(expq_buf[:, :]),
                       h["invb"][:, :].unsqueeze(1).broadcast_to([128, NCH, GT]),
                       ALU.mult,
                       wait=[invbc_ev, exq[NCH - 1],
                             k1tok_by[g - 1] if g >= 1 else None,
                             kern3_t if g == 0 else None])
    mid_ev = ctx.bank_read("M", ctx.act(h["mid"][:, :], h["M"][:, :], AF.Copy,
                                        scale=1.0 / 1024.0, wait=[mid_mm]))

    ctx.set_tok("xrfree", mid_mm)
    ctx.set_tok("xkfree", mid_mm)
    ctx.set_tok("lnfree", qmm_by[2])
    ctx.set_tok("difffree", qmm_by[3])
    ctx.set_tok("rgfree", qsm_t)
    ctx.set_tok("kactfree", mid_mm)
    ctx.set_tok("ksmfree", None)
    ctx.set_tok("k3evfree", None)

    res_toks = [None] * NG
    for g in range(NG):
        gsl = slice(g * GT, (g + 1) * GT)
        rtok = None
        for dd in range(NCH):
            bank = "G0" if dd % 2 == 0 else "G1"
            omm = ctx.mm(h[bank][:, 0:GT],
                         h["mid"][:, dd * 128:(dd + 1) * 128],
                         h["kern1T"][0:NL, gsl], start=True, stop=True,
                         wait=[k1_evs[g], mid_ev, ctx.guard.get(bank)],
                         inc=(("T", 1),))
            rtok = ctx.bank_read(bank, ctx.tt(
                c3(h["x"][:, :], T)[:, dd, gsl],
                c3(h["x"][:, :], T)[:, dd, gsl],
                h[bank][:, 0:GT], ALU.add, wait=[omm]))
        res_toks[g] = rtok
        if g == 0:
            emit_k1(NG - 1, qsm_t)
        if tail_emits and g < len(tail_emits):
            for cb in tail_emits[g]:
                cb(res_toks)
    return dict(slot_last=(qmm_by[NG - 1], last_kmm, vmm_by[NG - 1]),
                res=res_toks)



def emit_preamble(ctx):
    h = ctx.h
    ones_t = ctx.op("vector", "memset", h["ones_col"][:, :], 1.0,
                    inc=(("V", 1),))
    ctx.op("vector", "memset", h["ones_row"][:, :], 1.0, inc=(("V", 1),))
    eps_t = ctx.op("vector", "memset", h["cst_eps"][:, :], EPS,
                   inc=(("V", 1),))
    id0 = ctx.op("gpsimd", "memset", h["ident"][:, :], 0.0, inc=(("G", 1),))
    id_t = ctx.op("gpsimd", "affine_select", out=h["ident"][:, :],
                  in_=h["ident"][:, :], compare_op=ALU.not_equal, fill=1.0,
                  base=0, pattern=[[-1, 128]], channel_multiplier=1,
                  wait=[id0], inc=(("G", 1),))
    ctx.pre_ident = id_t
    ctx.dma("gpsimd", out=h["vecs_sb"][:, :], in_=h["vecs_hbm"][:, :],
            sem="PV")
    bfl_t = None
    for i, dst in enumerate((h["bflat0"], h["bflat1"])):
        bfl_t = ctx.dma("gpsimd",
                        out=dst.rearrange("o (c p) -> o c p", c=8),
                        in_=h["vecs_hbm"][:, i * 8:(i + 1) * 8]
                            .rearrange("p (o c) -> o c p", o=1),
                        sem="PV", allow_slow_non_contiguous=True)
    vec_t = bfl_t
    wq_t = cast_weight_sbuf(ctx, "w0", "wq_hbm")
    wk_t = cast_weight_sbuf(ctx, "w1", "wk_hbm")
    spill_q = ctx.dma("sync", out=h["wq_bf"][:, :], in_=h["w0"][:, :],
                      wait=[wq_t], sem="W0")
    spill_k = ctx.dma("sync", out=h["wk_bf"][:, :], in_=h["w1"][:, :],
                      wait=[wk_t], sem="W1")
    c_wr = cast_weight_hbm(ctx, "wr_bf", "wr_hbm")
    c_wkf = cast_weight_hbm(ctx, "wkf_bf", "wkf_hbm", wait=[c_wr])
    c_wvf = cast_weight_hbm(ctx, "wvf_bf", "wvf_hbm", wait=[c_wkf])
    c_wv = cast_weight_hbm(ctx, "wv_bf", "wv_hbm", wait=[c_wvf])

    b_toks = {}
    last_sq = None
    for i, (slot, wt, brow) in enumerate((("w0", wq_t, 0), ("w1", wk_t, 1))):
        sq0 = ctx.act(h["xr"][:, :], h[slot][:, 0:4096], AF.Square,
                      wait=[wt, last_sq])
        sq1 = ctx.act(h["xk"][:, :], h[slot][:, 4096:8192], AF.Square,
                      wait=[wt])
        cn_mm = {}
        for dg in range(2):
            bank = "G2" if dg == 0 else "G3"
            lhs, rhs = [], []
            for c in range(NCH):
                buf = h["xr"] if c < 4 else h["xk"]
                lhs.append(h["ones_col"][:, :])
                rhs.append(buf[:, (c % 4) * 1024 + dg * 512:
                               (c % 4) * 1024 + dg * 512 + 512])
            cn_mm[dg] = ctx.mmacc(bank, h[bank][0:1, 0:512], lhs, rhs,
                                  wait_first=[sq0, sq1, ones_t])
        last_sq = cn_mm[1]
        ev0 = ctx.bank_read("G2", ctx.act(h["m_row"][:, 0:512],
                                          h["G2"][0:1, 0:512], AF.Copy,
                                          wait=[cn_mm[0]]))
        ev1 = ctx.bank_read("G3", ctx.act(h["m_row"][:, 512:1024],
                                          h["G3"][0:1, 0:512], AF.Copy,
                                          wait=[cn_mm[1]]))
        lnt = ctx.act(h["s2_row"][:, 0:1024], h["m_row"][:, 0:1024], AF.Ln,
                      bias=h["cst_eps"][0:1, 0:1], wait=[ev0, ev1, eps_t])
        cnt = ctx.act(h["m_row"][:, 0:1024], h["s2_row"][:, 0:1024], AF.Exp,
                      scale=0.5, wait=[lnt])
        m1 = ctx.red(h["stat_m2"][:, 0:1], h["m_row"][:, 0:1024], ALU.max,
                     wait=[cnt])
        m2 = ctx.red(h["stat_m2"][:, 1:2],
                     h["bflat0"] if brow == 0 else h["bflat1"],
                     ALU.max, apply_absolute_value=True, wait=[bfl_t])
        b1 = ctx.ts(h["stat_m2"][:, 2:3], h["stat_m2"][:, 0:1], 32.0,
                    ALU.mult, wait=[m1])
        b2 = ctx.tt(h["stat_m2"][:, 2:3], h["stat_m2"][:, 2:3],
                    h["stat_m2"][:, 1:2], ALU.add, wait=[b1, m2])
        b3 = ctx.ts(h["stat_m2"][:, 2:3], h["stat_m2"][:, 2:3], 2.0,
                    ALU.add, wait=[b2])
        bf_c = ctx.vcopy(h["stat_bf"][0:1, 0:1], h["stat_m2"][:, 2:3],
                         wait=[b3])
        bmm = ctx.mm(h["G2"][:, 0:1], h["onesr_at"][0],
                     h["stat_bf"][0:1, 0:1], start=True, stop=True,
                     wait=[bf_c, ctx.guard.get("G2")], inc=(("T", 1),))
        bev = ctx.bank_read("G2", ctx.act(h["Bcol"][:, i:i + 1],
                                          h["G2"][:, 0:1], AF.Copy,
                                          wait=[bmm]))
        boff = 0 if i == 0 else 8
        b_toks[i] = ctx.op("vector", "tensor_scalar",
                           h["biasqk"][:, i * 8:(i + 1) * 8],
                           h["vecs_sb"][:, boff:boff + 8],
                           h["Bcol"][:, i:i + 1], None, ALU.subtract,
                           wait=[bev, vec_t], inc=(("V", 1),))
    ctx.set_tok("m2", b_toks[1])
    ctx.set_tok("lnx", b_toks[1])
    ctx.set_tok("mrow", b_toks[1])
    ctx.set_tok("xrfree", last_sq)
    ctx.set_tok("xkfree", last_sq)
    return dict(bias_done=(b_toks[0], b_toks[1]),
                bcomp_last=(last_sq, b_toks[0], b_toks[1]),
                vecs=vec_t, spills=(spill_q, spill_k),
                casts=dict(wr=c_wr, wkf=c_wkf, wvf=c_wvf, wv=c_wv))



def program(ctx):
    h = ctx.h
    pre = emit_preamble(ctx)
    spill_q, spill_k = pre["spills"]
    casts = pre["casts"]

    wr_t = load_weight(ctx, "w0", "wr_bf",
                       wait=list(pre["bcomp_last"]) + [spill_q, casts["wr"]])
    wkf_t = load_weight(ctx, "w1", "wkf_bf",
                        wait=list(pre["bcomp_last"]) + [spill_k, casts["wkf"]])
    wvf_t = load_weight(ctx, "w2", "wvf_bf", wait=[casts["wvf"]])
    ffn_slots = ("w0", "w1", "w2")
    ffn_toks = (wr_t, wkf_t, wvf_t)

    x_loads = None
    f1st = None
    f2 = None
    for s in range(NSEQ):
        if s == 0:
            x_loads = [ctx.dma(
                "sync",
                out=c3(h["x"][:, :], T)[:, :, g * GT:(g + 1) * GT],
                in_=h["x_hbm"][s, :, :, g * GT:(g + 1) * GT]
                    .rearrange("c p t -> p c t"),
                sem="XI%d" % g) for g in range(NG)]
        gates1 = [[x_loads[g]] + (list(pre["bias_done"]) if s == 0 else [])
                  for g in range(NG)]
        attn_pre = {}
        ap = {}

        def a0_p01(res, xl):
            ap["c0"] = chain_p01(ctx, 0, 2, 3, [res[0]])

        def a0_p2(res, xl):
            attn_pre[0] = chain_p2(ctx, ap["c0"])

        def a1_p01(res, xl):
            attn_pre[1] = chain_p01(ctx, 1, 2, 3, [res[1]])

        def a_ap0(res, xl):
            attn_pre["ap0"] = emit_apply(ctx, 0, attn_pre[0],
                                         extra_wait=[ctx.tok("lnfree")])
        tails1 = {(2, "mid2"): [a0_p01],
                  (2, "post"): [a0_p2],
                  (3, "mid2"): [a1_p01, a_ap0]}
        f1 = emit_ffn(ctx, s, ffn_slots, ffn_toks, False, gates1, st=f1st,
                      tails=tails1)
        sl = f1["slot_last"]
        wq_t = load_weight(ctx, ffn_slots[0], "wq_bf", wait=[sl[0], spill_q])
        wka_t = load_weight(ctx, ffn_slots[1], "wk_bf", wait=[sl[1], spill_k])
        wva_t = load_weight(ctx, ffn_slots[2], "wv_bf",
                            wait=[sl[2], casts["wv"]])
        gates2 = [[f1["res"][g]] for g in range(NG)]
        f2st = new_ffn_state()
        f2p = {}

        def f20_p01(res):
            f2p["c0"] = chain_p01(ctx, 0, 0, 1, [res[0]], cast_eng="scalar")

        def f20_p2(res):
            f2st["chains"][0] = chain_p2(ctx, f2p["c0"])

        def f21_p01(res):
            f2st["pend1"] = chain_p01(ctx, 1, 0, 1, [res[1]],
                                      cast_eng="scalar")

        def f2_prep(res):
            ffn_prep(ctx, 0, f2st, extra_wait=[ctx.tok("lnfree")])

        def f2_mix(res):
            f2st["mix_r"] = ffn_mixes(ctx, 0, "r", ctx.tok("xrfree"), f2st)
            f2st["mix_k"] = ffn_mixes(ctx, 0, "k", ctx.tok("xkfree"), f2st)
        tails3b = [[f20_p01], [f20_p2, f2_prep, f21_p01], [f2_mix], []]
        a1 = emit_attn(ctx, s, ffn_slots, (wq_t, wka_t, wva_t), gates2,
                       pre_chains=attn_pre, pre_applied=attn_pre["ap0"],
                       tail_emits=tails3b)
        asl = a1["slot_last"]
        new_slots = (ffn_slots[1], ffn_slots[2], ffn_slots[0])
        wr_t = load_weight(ctx, new_slots[0], "wr_bf",
                           wait=[asl[1], casts["wr"]])
        wkf_t = load_weight(ctx, new_slots[1], "wkf_bf",
                            wait=[asl[2], casts["wkf"]])
        wvf_t = load_weight(ctx, new_slots[2], "wvf_bf",
                            wait=[asl[0], casts["wvf"]])
        ffn_slots = new_slots
        ffn_toks = (wr_t, wkf_t, wvf_t)
        gates3 = [[a1["res"][g]] for g in range(NG)]
        if s + 1 < NSEQ:
            f1st = new_ffn_state()
            nst = f1st
            np_ = {}

            def n0_p01(res, xl):
                np_["c0"] = chain_p01(ctx, 0, 0, 1, [xl[0]])

            def n0_p2(res, xl):
                nst["chains"][0] = chain_p2(ctx, np_["c0"])

            def n_prep(res, xl):
                ffn_prep(ctx, 0, nst, extra_wait=[ctx.tok("lnfree")])

            def n1_p01(res, xl):
                nst["pend1"] = chain_p01(ctx, 1, 0, 1, [xl[1]])

            def n_mix(res, xl):
                nst["mix_r"] = ffn_mixes(ctx, 0, "r", ctx.tok("xrfree"), nst)
                nst["mix_k"] = ffn_mixes(ctx, 0, "k", ctx.tok("xkfree"), nst)
            tails2 = {(2, "mid2"): [n0_p01],
                      (2, "post"): [n0_p2],
                      (3, "mid"): [n_prep],
                      (3, "mid2"): [n1_p01, n_mix]}
            oxl = (s + 1,)
        else:
            f1st = None
            tails2 = None
            oxl = None
        f2 = emit_ffn(ctx, s, ffn_slots, ffn_toks, True, gates3, st=f2st,
                      tails=tails2, out_xload=oxl)
        x_loads = f2["xloads"]

    ctx.op("sync", "nop", wait=list(f2["out"]))



def build_nc():
    import contextlib
    nc = bass.Bass("TRN2", target_bir_lowering=False, debug=False,
                   detect_race_conditions=False)
    h = {}
    h["x_hbm"] = nc.dram_tensor("x_in", [NSEQ, NCH, 128, T], F32,
                                kind="ExternalInput").ap()
    h["out_hbm"] = nc.dram_tensor("out", [NSEQ, NCH, 128, T], F32,
                                  kind="ExternalOutput").ap()
    for nm in ("wq", "wk", "wv", "wr", "wkf", "wvf"):
        h[nm + "_hbm"] = nc.dram_tensor(nm, [NCH, 128, 1024], F32,
                                        kind="ExternalInput").ap()
    h["vecs_hbm"] = nc.dram_tensor("vecs", [128, 40], F32,
                                   kind="ExternalInput").ap()
    h["ksm_scr"] = nc.dram_tensor("ksm_scr", [NCH, 128, T], BF16,
                                  kind="Internal").ap()
    h["k3t_scr"] = nc.dram_tensor("k3t_scr", [16, 128, 1024], BF16,
                                  kind="Internal").ap()
    for nm in ("wq", "wk", "wv", "wr", "wkf", "wvf"):
        h[nm + "_bf"] = nc.dram_tensor(nm + "_bf", [128, NCH * 1024], BF16,
                                       kind="Internal").ap()

    stack = contextlib.ExitStack()

    def sb(name, shape, dtype):
        return stack.enter_context(nc.sbuf_tensor(name, shape, dtype))

    def ps(name, shape, dtype):
        return stack.enter_context(nc.psum_tensor(name, shape, dtype))

    h["x"] = sb("x", [128, NCH * T], F32)
    for w in ("w0", "w1", "w2"):
        h[w] = sb(w, [128, NCH * 1024], BF16)
    for nm in ("ln", "diff", "xr", "xk", "rg", "krelu", "kact"):
        h[nm] = sb(nm, [128, NCH * GT], BF16)
    h["ksm_g"] = h["diff"]
    h["k3t_ev"] = h["xr"]
    h["castb"] = sb("castb", [128, 2 * GT], BF16)
    h["sqb"] = sb("sqb", [128, 2 * GT], BF16)
    h["stg"] = sb("stg", [128, 2 * GT], BF16)
    h["ab_a"] = sb("ab_a", [128, 2 * GT], BF16)
    h["ab_b"] = sb("ab_b", [128, 2 * GT], BF16)
    h["invb"] = sb("invb", [128, GT], BF16)
    h["sums"] = sb("sums", [33, T], F32)
    h["ab"] = sb("ab", [65, 2 * T], BF16)
    h["stat_tmp"] = sb("stat_tmp", [33, GT], F32)
    h["stat_bf"] = sb("stat_bf", [1, 4], BF16)
    h["inv_bb"] = sb("inv_bb", [1, GT], BF16)
    h["kern2T"] = sb("kern2T", [128, 2 * NL], BF16)
    h["mid"] = sb("mid", [NL, 1024], BF16)
    h["kern1T"] = sb("kern1T", [NL, T], BF16)
    h["kland"] = sb("kland", [128, NCH * NL], BF16)
    h["kland_f"] = sb("kland_f", [128, NCH * 16], F32)
    h["vecs_sb"] = sb("vecs_sb", [128, 40], F32)
    h["biasqk"] = sb("biasqk", [128, 16], F32)
    h["Bcol"] = sb("Bcol", [128, 2], F32)
    h["ones_col"] = sb("ones_col", [128, 1], BF16)
    h["ones_row"] = sb("ones_row", [65, 128], BF16)
    h["ident"] = sb("ident", [128, 128], BF16)
    h["lnprev"] = sb("lnprev", [128, 2 * NCH], BF16)
    h["cst_eps"] = sb("cst_eps", [1, 1], F32)

    h["G0"] = ps("G0", [128, 512], F32)
    h["G1"] = ps("G1", [128, 512], F32)
    h["G2"] = ps("G2", [128, 512], F32)
    h["G3"] = ps("G3", [128, 512], F32)
    h["M"] = ps("M", [NL, 1024], F32)
    h["trA"] = ps("trA", [128, 2048], BF16)

    h["m_row"] = h["sums"][0:1, 0:T]
    h["s2_row"] = h["sums"][32:33, 0:T]
    h["ab_r"] = [h["ab"][0:1, 0:T], h["ab"][32:33, 0:T],
                 h["ab"][0:1, T:2 * T], h["ab"][64:65, 0:T]]
    h["ab_bp"] = [0, 32, 0, 64]
    h["bflat0"] = h["sums"][0:1, 1024:2048]
    h["bflat1"] = h["sums"][32:33, 1024:2048]
    h["invb0"] = h["inv_bb"][0:1, 0:GT]
    h["invb1"] = h["inv_bb"][0:1, 0:GT]
    h["inv_f0"] = h["stat_tmp"][0:1, 0:GT]
    h["inv_f1"] = h["stat_tmp"][0:1, 0:GT]
    h["stat_m2"] = h["stat_tmp"][32:33, 0:GT]
    h["stat_ln"] = h["stat_tmp"][0:1, 0:GT]
    h["onesr_at"] = {0: h["ones_row"][0:1, :], 32: h["ones_row"][32:33, :],
                     64: h["ones_row"][64:65, :]}
    h["mur"] = h["vecs_sb"][:, 24:32]
    h["muk"] = h["vecs_sb"][:, 32:40]
    h["bv"] = h["vecs_sb"][:, 16:24]
    h["biasq"] = h["biasqk"][:, 0:8]
    h["biask"] = h["biasqk"][:, 8:16]

    sem_names = ("T", "V", "S", "G", "W0", "W1", "W2", "PV", "WC",
                 "XI0", "XI1", "XI2", "XI3", "XO0", "XO1", "XO2", "XO3",
                 "KS", "K3", "RK0", "RK1")
    sems = {nm: stack.enter_context(nc.semaphore("sem_" + nm))
            for nm in sem_names}

    with nc.Block() as block:
        def run(engine):
            def f(_eng):
                local = {nm: [sems[nm], 0] for nm in sem_names}
                ctx = Ctx(nc, engine, local, h)
                program(ctx)
            return f
        block.sync(run("sync"))
        block.gpsimd(run("gpsimd"))
        block.tensor(run("tensor"))
        block.scalar(run("scalar"))
        block.vector(run("vector"))

    stack.close()
    return nc



_NC_CACHE = {}


def kernel(**inputs):
    x = np.asarray(inputs["L0_attn_hidden_states"], dtype=np.float32)
    B, TT_, D = x.shape
    n_cores = 8
    per = B // n_cores

    xfm = np.ascontiguousarray(
        x.reshape(B, TT_, NCH, 128).transpose(0, 2, 3, 1))

    def warr(w):
        return np.ascontiguousarray(
            np.asarray(w, dtype=np.float32).reshape(NCH, 128, 1024))

    wq = warr(inputs["L0_attn_Wq"])
    wk = warr(inputs["L0_attn_Wk"])
    wv = warr(inputs["L0_attn_Wv"])
    wr = warr(np.asarray(inputs["L0_ffn_W_r"], dtype=np.float32).T)
    wkf = warr(np.asarray(inputs["L0_ffn_W_k"], dtype=np.float32).T)
    wvf = warr(np.asarray(inputs["L0_ffn_W_v"], dtype=np.float32).T)

    def pcol(v):
        return np.ascontiguousarray(
            np.asarray(v, dtype=np.float32).reshape(NCH, 128).T)

    vecs = np.ascontiguousarray(np.concatenate(
        [pcol(inputs["L0_attn_bq"]), pcol(inputs["L0_attn_bk"]),
         pcol(inputs["L0_attn_bv"]), pcol(inputs["L0_ffn_mu_r"]),
         pcol(inputs["L0_ffn_mu_k"])], axis=1))

    if "nc" not in _NC_CACHE:
        _NC_CACHE["nc"] = build_nc()
    nc = _NC_CACHE["nc"]

    in_maps = []
    for c in range(n_cores):
        in_maps.append({
            "x_in": np.ascontiguousarray(xfm[c * per:(c + 1) * per]),
            "wq": wq, "wk": wk, "wv": wv,
            "wr": wr, "wkf": wkf, "wvf": wvf,
            "vecs": vecs,
        })
    res = run_bass_kernel_spmd(nc, in_maps, core_ids=list(range(n_cores)))
    outs = [res.results[c]["out"] for c in range(n_cores)]
    full = np.concatenate(outs, axis=0).reshape(B, NCH, 128, T)
    out = np.ascontiguousarray(full.transpose(0, 3, 1, 2).reshape(B, TT_, D))
    return out


if __name__ == "__main__":
    build_nc()
    print("build OK")


# version 23
# speedup vs baseline: 1.0390x; 1.0390x over previous
import numpy as np
import concourse.bass as bass
import concourse.mybir as mybir
from concourse.bass_utils import run_bass_kernel_spmd

F32 = mybir.dt.float32
BF16 = mybir.dt.bfloat16
AF = mybir.ActivationFunctionType
ALU = mybir.AluOpType
AX = mybir.AxisListType

T = 2048
NSEQ = 4
NCH = 8
NG = 4
GT = 512
NL = 64
EPS = 1e-5


class Ctx:

    def __init__(self, nc, active, sems, h):
        self.nc = nc
        self.active = active
        self.sems = sems
        self.h = h
        self.guard = {}
        self.pre_ident = None
        self._toks = {}

    def tok(self, name):
        return self._toks.get(name)

    def set_tok(self, name, tok):
        self._toks[name] = tok

    def op(self, engine, fn, *args, wait=(), inc=(), **kwargs):
        if self.active == engine:
            eng = getattr(self.nc, engine)
            wmax = {}
            for tk in wait:
                if tk is None:
                    continue
                nm, v = tk
                wmax[nm] = max(wmax.get(nm, 0), v)
            for nm, v in wmax.items():
                eng.wait_ge(self.sems[nm][0], v)
            instr = getattr(eng, fn)(*args, **kwargs)
            for nm, k in inc:
                instr.then_inc(self.sems[nm][0], k)
        toks = []
        for nm, k in inc:
            self.sems[nm][1] += k
            toks.append((nm, self.sems[nm][1]))
        return toks[0] if len(toks) == 1 else (toks if toks else None)

    def mm(self, out, lhsT, rhs, start, stop, wait=(), inc=()):
        return self.op("tensor", "matmul", out, lhsT, rhs,
                       start=start, stop=stop, wait=wait, inc=inc)

    def mmacc(self, bank, out, lhsT_list, rhs_list, wait_first=()):
        n = len(lhsT_list)
        tok = None
        for i in range(n):
            w = (list(wait_first) + [self.guard.get(bank)]) if i == 0 else ()
            tok = self.mm(out, lhsT_list[i], rhs_list[i],
                          start=(i == 0), stop=(i == n - 1),
                          wait=w, inc=((("T", 1),) if i == n - 1 else ()))
        return tok

    def bank_read(self, bank, tok):
        self.guard[bank] = tok
        return tok

    def tr(self, out, in_, wait=(), inc=()):
        return self.op("tensor", "transpose", out, in_, self.h["ident"][:, :],
                       wait=wait, inc=inc)

    def act(self, out, in_, func, wait=(), inc=(("S", 1),), **kw):
        return self.op("scalar", "activation", out, in_, func,
                       wait=wait, inc=inc, **kw)

    def vcopy(self, out, in_, wait=(), inc=(("V", 1),)):
        return self.op("vector", "tensor_copy", out, in_, wait=wait, inc=inc)

    def tt(self, out, in0, in1, alu, wait=(), inc=(("V", 1),)):
        return self.op("vector", "tensor_tensor", out, in0, in1, alu,
                       wait=wait, inc=inc)

    def ts(self, out, in0, scalar, alu, wait=(), inc=(("V", 1),)):
        return self.op("vector", "tensor_single_scalar", out, in0, scalar, alu,
                       wait=wait, inc=inc)

    def stt(self, out, in0, scalar, in1, op0, op1, wait=(), inc=(("V", 1),)):
        return self.op("vector", "scalar_tensor_tensor", out, in0, scalar, in1,
                       op0, op1, wait=wait, inc=inc)

    def red(self, out, in_, alu, wait=(), inc=(("V", 1),), **kw):
        return self.op("vector", "tensor_reduce", out, in_, AX.X, alu,
                       wait=wait, inc=inc, **kw)

    def gcopy(self, out, in_, wait=(), inc=(("G", 1),)):
        return self.op("gpsimd", "tensor_copy", out, in_, wait=wait, inc=inc)

    def gtt(self, out, in0, in1, alu, wait=(), inc=(("G", 1),)):
        return self.op("gpsimd", "tensor_tensor", out, in0, in1, alu,
                       wait=wait, inc=inc)

    def gstt(self, out, in0, scalar, in1, op0, op1, wait=(), inc=(("G", 1),)):
        return self.op("gpsimd", "scalar_tensor_tensor", out, in0, scalar, in1,
                       op0, op1, wait=wait, inc=inc)

    def dma(self, queue, out, in_, wait=(), sem="XO", **kw):
        return self.op(queue, "dma_start", out=out, in_=in_,
                       wait=wait, inc=((sem, 16),), **kw)


def c3(ap2d, n=GT):
    return ap2d.rearrange("p (c n) -> p c n", c=NCH)



def chain_p01(ctx, g, row_a, row_b, gates, cast_eng="vector"):
    h = ctx.h
    gsl = slice(g * GT, (g + 1) * GT)
    st = dict(g=g, row_a=row_a, row_b=row_b)
    s1_tok = s2_tok = None
    for c in range(NCH):
        cp = slice((c % 2) * GT, (c % 2 + 1) * GT)
        cwait = list(gates) + [ctx.tok("cs%d" % (c % 2)),
                               ctx.tok("csq%d" % (c % 2))]
        if cast_eng == "vector":
            cast_t = ctx.vcopy(h["castb"][:, cp],
                               c3(h["x"][:, :], T)[:, c, gsl], wait=cwait)
        else:
            cast_t = ctx.act(h["castb"][:, cp],
                             c3(h["x"][:, :], T)[:, c, gsl], AF.Copy,
                             wait=cwait)
        sq_t = ctx.act(h["sqb"][:, cp], h["castb"][:, cp], AF.Square,
                       wait=[cast_t, ctx.tok("sq%d" % (c % 2))])
        ctx.set_tok("csq%d" % (c % 2), sq_t)
        s1_tok = ctx.mm(h["G2"][0:1, 0:GT], h["ones_col"][:, :],
                        h["castb"][:, cp], start=(c == 0), stop=(c == NCH - 1),
                        wait=[cast_t] +
                             ([ctx.guard.get("G2")] if c == 0 else []),
                        inc=(("T", 1),))
        ctx.set_tok("cs%d" % (c % 2), s1_tok)
        s2_tok = ctx.mm(h["G3"][0:1, 0:GT], h["ones_col"][:, :],
                        h["sqb"][:, cp], start=(c == 0), stop=(c == NCH - 1),
                        wait=[sq_t] +
                             ([ctx.guard.get("G3")] if c == 0 else []),
                        inc=(("T", 1),))
        ctx.set_tok("sq%d" % (c % 2), s2_tok)
    ev1 = ctx.bank_read("G2", ctx.act(h["m_row"][:, gsl], h["G2"][0:1, 0:GT],
                                      AF.Copy, scale=1.0 / 1024.0,
                                      wait=[s1_tok, ctx.tok("mrow")]))
    ev2 = ctx.bank_read("G3", ctx.act(h["s2_row"][:, gsl], h["G3"][0:1, 0:GT],
                                      AF.Copy, scale=1.0 / 1024.0,
                                      wait=[s2_tok]))
    m = h["m_row"][:, gsl]
    s2m = h["s2_row"][:, gsl]
    t_m2 = ctx.tt(h["stat_m2"], m, m, ALU.mult, wait=[ev1, ctx.tok("m2")])
    t_var = ctx.tt(s2m, s2m, h["stat_m2"], ALU.subtract, wait=[ev2, t_m2])
    ctx.set_tok("m2", t_var)
    t_ln = ctx.act(h["stat_ln"], s2m, AF.Ln, bias=h["cst_eps"][0:1, 0:1],
                   wait=[t_var, ctx.tok("lnx")])
    t_a = ctx.act(h["ab_r"][row_a][:, gsl], h["stat_ln"], AF.Exp, scale=-0.5,
                  wait=[t_ln])
    ctx.set_tok("lnx", t_a)
    t_b = ctx.tt(h["ab_r"][row_b][:, gsl], m, h["ab_r"][row_a][:, gsl],
                 ALU.mult, wait=[t_a])
    ctx.set_tok("mrow", t_b)
    st["t_a"] = t_a
    st["t_b"] = t_b
    return st


def chain_p2(ctx, st):
    h = ctx.h
    g, row_a, row_b = st["g"], st["row_a"], st["row_b"]
    gsl = slice(g * GT, (g + 1) * GT)
    par = g % 2
    psl = slice(par * GT, (par + 1) * GT)
    mm_a = ctx.mm(h["G2"][:, 0:GT], h["onesr_at"][h["ab_bp"][row_a]],
                  h["ab_r"][row_a][:, gsl], start=True, stop=True,
                  wait=[st["t_a"], ctx.guard.get("G2")], inc=(("T", 1),))
    mm_b = ctx.mm(h["G3"][:, 0:GT], h["onesr_at"][h["ab_bp"][row_b]],
                  h["ab_r"][row_b][:, gsl], start=True, stop=True,
                  wait=[st["t_b"], ctx.guard.get("G3")], inc=(("T", 1),))
    ev_a = ctx.bank_read("G2", ctx.act(h["ab_a"][:, psl], h["G2"][:, 0:GT],
                                       AF.Copy,
                                       wait=[mm_a, ctx.tok("aba%d" % par)]))
    ev_b = ctx.bank_read("G3", ctx.act(h["ab_b"][:, psl], h["G3"][:, 0:GT],
                                       AF.Copy,
                                       wait=[mm_b, ctx.tok("abb%d" % par)]))
    return dict(ev_a=ev_a, ev_b=ev_b, par=par, b_tok=st["t_b"])


def emit_ln_chain(ctx, g, row_a, row_b, gates):
    return chain_p2(ctx, chain_p01(ctx, g, row_a, row_b, gates))


def emit_apply(ctx, g, chain, extra_wait=(), dst=None):
    h = ctx.h
    dst = dst if dst is not None else h["ln"]
    gsl = slice(g * GT, (g + 1) * GT)
    par = chain["par"]
    psl = slice(par * GT, (par + 1) * GT)
    d3 = c3(dst[:, :])
    t1 = ctx.tt(d3, c3(h["x"][:, :], T)[:, :, gsl],
                h["ab_a"][:, psl].unsqueeze(1).broadcast_to([128, NCH, GT]),
                ALU.mult, wait=[chain["ev_a"]] + list(extra_wait))
    tok = ctx.tt(d3, d3,
                 h["ab_b"][:, psl].unsqueeze(1).broadcast_to([128, NCH, GT]),
                 ALU.subtract, wait=[chain["ev_b"], t1])
    ctx.set_tok("aba%d" % par, tok)
    ctx.set_tok("abb%d" % par, tok)
    return tok


def load_weight(ctx, slot, bf_name, wait=()):
    h = ctx.h
    return ctx.dma("sync", out=h[slot][:, :], in_=h[bf_name][:, :],
                   wait=wait, sem="W" + slot[-1])


def cast_weight_sbuf(ctx, slot, hbm_name, wait=()):
    h = ctx.h
    return ctx.dma("gpsimd",
                   out=h[slot][:, :].rearrange("p (c d) -> p c d", c=NCH),
                   in_=h[hbm_name][:, :, :].rearrange("c p d -> p c d"),
                   wait=wait, sem="W" + slot[-1])


def cast_weight_hbm(ctx, bf_name, hbm_name, wait=()):
    h = ctx.h
    return ctx.dma("gpsimd",
                   out=h[bf_name][:, :].rearrange("p (c d) -> p c d", c=NCH),
                   in_=h[hbm_name][:, :, :].rearrange("c p d -> p c d"),
                   wait=wait, sem="WC")


def wtile(ctx, slot, c, dd):
    return ctx.h[slot][:, c * 1024 + dd * 128: c * 1024 + (dd + 1) * 128]


def proj_mms(ctx, slot, w_tok, rhs_buf, rhs_tok, extra_first=()):
    h = ctx.h
    for dd in range(NCH):
        bank = "G0" if dd % 2 == 0 else "G1"
        tok = ctx.mmacc(bank, h[bank][:, 0:GT],
                        [wtile(ctx, slot, c, dd) for c in range(NCH)],
                        [c3(rhs_buf[:, :])[:, c, :] for c in range(NCH)],
                        wait_first=[rhs_tok, w_tok] + list(extra_first))
        yield dd, bank, tok



def new_ffn_state():
    return dict(chains={}, applied={}, diff={}, lnprev={}, mix_r=None,
                mix_k=None)


def ffn_prep(ctx, g, st, extra_wait=()):
    h = ctx.h
    if -1 not in st["lnprev"]:
        st["lnprev"][-1] = ctx.op("vector", "memset",
                                  h["lnprev"][:, NCH:2 * NCH], 0.0,
                                  inc=(("V", 1),))
    st["applied"][g] = emit_apply(ctx, g, st["chains"][g],
                                  extra_wait=list(extra_wait))
    lp_w = slice((g % 2) * NCH, (g % 2) * NCH + NCH)
    st["lnprev"][g] = ctx.vcopy(h["lnprev"][:, lp_w].unsqueeze(2),
                                c3(h["ln"][:, :])[:, :, GT - 1:GT],
                                wait=[st["applied"][g]])
    d1 = ctx.tt(c3(h["diff"][:, :])[:, :, 1:GT],
                c3(h["ln"][:, :])[:, :, 1:GT],
                c3(h["ln"][:, :])[:, :, 0:GT - 1], ALU.subtract,
                wait=[st["applied"][g], ctx.tok("difffree")])
    lp_r = slice(((g - 1) % 2) * NCH, ((g - 1) % 2) * NCH + NCH)
    st["diff"][g] = ctx.tt(c3(h["diff"][:, :])[:, :, 0:1],
                           c3(h["ln"][:, :])[:, :, 0:1],
                           h["lnprev"][:, lp_r].unsqueeze(2), ALU.subtract,
                           wait=[d1, st["lnprev"][g - 1]])


def ffn_mixes(ctx, g, which, guard_tok, st):
    h = ctx.h
    mu = h["mur"] if which == "r" else h["muk"]
    dst = h["xr"] if which == "r" else h["xk"]
    lp_r = slice(((g - 1) % 2) * NCH, ((g - 1) % 2) * NCH + NCH)
    tok = None
    for c in range(NCH):
        ctx.stt(c3(dst[:, :])[:, c, 1:GT],
                c3(h["diff"][:, :])[:, c, 1:GT], mu[:, c:c + 1],
                c3(h["ln"][:, :])[:, c, 0:GT - 1], ALU.mult, ALU.add,
                wait=[st["diff"][g], guard_tok] if c == 0 else ())
        tok = ctx.stt(c3(dst[:, :])[:, c, 0:1],
                      c3(h["diff"][:, :])[:, c, 0:1], mu[:, c:c + 1],
                      h["lnprev"][:, lp_r][:, c:c + 1].unsqueeze(2),
                      ALU.mult, ALU.add, wait=())
    return tok


def emit_ffn(ctx, s, slots, w_toks, out_dma, gates, st=None, tails=None,
             out_xload=None):
    h = ctx.h
    st = st or new_ffn_state()
    tails = tails or {}
    pend = {}

    def chain_full(g):
        if g not in st["chains"]:
            st["chains"][g] = emit_ln_chain(ctx, g, 0, 1, gates[g])
    chain_full(0)
    if "pend1" not in st:
        chain_full(1)
    if 0 not in st["applied"]:
        ffn_prep(ctx, 0, st, extra_wait=[ctx.tok("lnfree")])
    mix_r = st["mix_r"] or ffn_mixes(ctx, 0, "r", ctx.tok("xrfree"), st)
    mix_k = st["mix_k"] or ffn_mixes(ctx, 0, "k", ctx.tok("xkfree"), st)
    res_toks = [None] * NG
    out_toks = [None] * NG
    xload_toks = [None] * NG
    prod_t = ctx.tok("rgfree")
    kact_free = ctx.tok("kactfree")
    last_r = last_k = last_v = None

    def run_tails(g, point):
        for cb in tails.get((g, point), ()):
            cb(res_toks, xload_toks)

    for g in range(NG):
        gsl = slice(g * GT, (g + 1) * GT)
        if 1 <= g < NG - 1:
            ffn_prep(ctx, g + 1, st, extra_wait=[mix_k])
        sig = {}
        for dd, bank, tok in proj_mms(ctx, slots[0], w_toks[0], h["xr"],
                                      mix_r):
            sig[dd] = ctx.bank_read(bank, ctx.act(
                c3(h["rg"][:, :])[:, dd, :], h[bank][:, 0:GT], AF.Sigmoid,
                wait=[tok, prod_t]))
            last_r = tok
        if g == NG - 1:
            ctx.set_tok("xrfree", last_r)
        if g == 0 and "pend1" in st:
            st["chains"][1] = chain_p2(ctx, st.pop("pend1"))
        if g == 0:
            ffn_prep(ctx, 1, st, extra_wait=[mix_k])
        if g + 1 < NG:
            mix_r = ffn_mixes(ctx, g + 1, "r", last_r, st)
        run_tails(g, "mid")
        sq_t = None
        for dd, bank, tok in proj_mms(ctx, slots[1], w_toks[1], h["xk"],
                                      mix_k):
            rtok = ctx.bank_read(bank, ctx.act(
                c3(h["krelu"][:, :])[:, dd, :], h[bank][:, 0:GT], AF.Relu,
                wait=[tok, prod_t]))
            sq_t = ctx.act(c3(h["kact"][:, :])[:, dd, :],
                           c3(h["krelu"][:, :])[:, dd, :], AF.Square,
                           wait=[rtok, kact_free if dd == 0 else None,
                                 last_v if dd == 0 else None])
            last_k = tok
        kact_free = None
        if g == NG - 1:
            ctx.set_tok("xkfree", last_k)
        if g + 2 < NG and g + 2 not in st["chains"]:
            pend[g + 2] = chain_p01(ctx, g + 2, 0, 1, gates[g + 2])
        else:
            run_tails(g, "mid2")
        if g + 1 < NG:
            mix_k = ffn_mixes(ctx, g + 1, "k", last_k, st)
            if g + 1 == NG - 1:
                ctx.set_tok("lnfree", mix_k)
                ctx.set_tok("difffree", mix_k)
        pres = {}
        for dd, bank, tok in proj_mms(ctx, slots[2], w_toks[2], h["kact"],
                                      sq_t):
            vev = ctx.bank_read(bank, ctx.act(
                c3(h["krelu"][:, :])[:, dd, :], h[bank][:, 0:GT], AF.Copy,
                scale=0.5, wait=[tok, sq_t]))
            psl2 = slice((dd % 2) * GT, (dd % 2) * GT + GT)
            ptok = ctx.tt(h["stg"][:, psl2], c3(h["rg"][:, :])[:, dd, :],
                          c3(h["krelu"][:, :])[:, dd, :], ALU.mult,
                          wait=[vev] + ([sig[NCH - 1], prod_t]
                                        if dd == 0 else []) +
                               ([pres[dd - 2]] if dd >= 2 else []))
            pres[dd] = ctx.tt(c3(h["x"][:, :], T)[:, dd, gsl],
                              c3(h["x"][:, :], T)[:, dd, gsl],
                              h["stg"][:, psl2], ALU.add, wait=[ptok])
            last_v = tok
        res_toks[g] = pres[NCH - 1]
        prod_t = res_toks[g]
        if g + 2 in pend:
            st["chains"][g + 2] = chain_p2(ctx, pend.pop(g + 2))
        run_tails(g, "post")
        if out_dma:
            out_toks[g] = ctx.dma(
                "sync",
                out=h["out_hbm"][s, :, :, gsl].rearrange("c p t -> p c t"),
                in_=c3(h["x"][:, :], T)[:, :, gsl],
                wait=[res_toks[g]], sem="XO%d" % g)
            if out_xload is not None:
                ns = out_xload[0]
                xload_toks[g] = ctx.dma(
                    "sync",
                    out=c3(h["x"][:, :], T)[:, :, gsl],
                    in_=h["x_hbm"][ns, :, :, gsl].rearrange("c p t -> p c t"),
                    wait=[out_toks[g]], sem="XI%d" % g)
        run_tails(g, "post2")
    ctx.set_tok("rgfree", prod_t)
    ctx.set_tok("kactfree", last_v)
    return dict(slot_last=(last_r, last_k, last_v), res=res_toks,
                out=out_toks, xloads=xload_toks)



def emit_attn(ctx, s, slots, w_toks, gates, pre_chains=None, pre_applied=None,
              tail_emits=None):
    h = ctx.h
    expk_buf = h["rg"]
    v_buf = h["krelu"]
    kern3_buf = h["kact"]
    ksm_buf = h["ksm_g"]
    k3t_buf = h["k3t_ev"]
    lnb1 = (h["ln"], h["xk"])
    lnbufs = (h["ln"], h["ksm_g"])
    rg_free = ctx.tok("rgfree")

    chains = [None] * NG
    if pre_chains:
        chains[0] = pre_chains[0]
        c1 = pre_chains[1]
        chains[1] = c1 if "ev_a" in c1 else chain_p2(ctx, c1)
    else:
        chains[0] = emit_ln_chain(ctx, 0, 2, 3, gates[0])
        chains[1] = emit_ln_chain(ctx, 1, 2, 3, gates[1])
    applied = pre_applied or emit_apply(ctx, 0, chains[0],
                                        extra_wait=[ctx.tok("lnfree")])
    pend_a = None

    ksm_t = None
    kern3_t = None
    pool_t = ctx.tok("poolfree")
    ksm_dma = [None] * NG
    k3t_dma = [None] * NG
    tpose_done = [None] * NG
    vmm_by = [None] * NG
    last_kmm = None
    last_exp = last_vev = None
    pend_tpose = None
    ln3_t = None

    def do_tpose(gp, k3_tok, spill=True):
        last = None
        ev = None
        evs = []
        prev_spill = k3t_dma[gp - 1] if gp >= 1 else ctx.tok("k3evfree")
        for tl in range(4):
            half = "trA0" if tl % 2 == 0 else "trA1"
            hap = h["trA"][:, (tl % 2) * 1024:(tl % 2) * 1024 + 1024]
            tk = None
            for c in range(NCH):
                tk = ctx.tr(hap[:, c * 128:(c + 1) * 128],
                            c3(kern3_buf[:, :])[:, c, tl * 128:(tl + 1) * 128],
                            wait=([k3_tok, ctx.guard.get(half),
                                   ctx.pre_ident] if c == 0 else ()),
                            inc=((("T", 1),) if c == NCH - 1 else ()))
            ev = ctx.bank_read(half, ctx.act(
                k3t_buf[:, tl * 1024:(tl + 1) * 1024], hap, AF.Copy,
                wait=[tk, prev_spill]))
            evs.append(ev)
            last = tk
        dma = None
        if spill:
            dma = ctx.dma("sync",
                          out=h["k3t_scr"][gp * 4:(gp + 1) * 4, :, :]
                              .rearrange("t p d -> p t d"),
                          in_=k3t_buf[:, :].rearrange("p (t d) -> p t d", t=4),
                          wait=[ev, k3t_dma[gp - 1] if gp >= 1 else None],
                          sem="K3")
        return last, dma, evs

    def rebcast(g):
        par = g % 2
        psl = slice(par * GT, (par + 1) * GT)
        gsl = slice(g * GT, (g + 1) * GT)
        mm_a = ctx.mm(h["G2"][:, 0:GT], h["onesr_at"][h["ab_bp"][2]],
                      h["ab_r"][2][:, gsl], start=True, stop=True,
                      wait=[chains[g]["b_tok"], ctx.guard.get("G2")],
                      inc=(("T", 1),))
        mm_b = ctx.mm(h["G3"][:, 0:GT], h["onesr_at"][h["ab_bp"][3]],
                      h["ab_r"][3][:, gsl], start=True, stop=True,
                      wait=[chains[g]["b_tok"], ctx.guard.get("G3")],
                      inc=(("T", 1),))
        ev_a = ctx.bank_read("G2", ctx.act(h["ab_a"][:, psl], h["G2"][:, 0:GT],
                                           AF.Copy,
                                           wait=[mm_a, ctx.tok("aba%d" % par)]))
        ev_b = ctx.bank_read("G3", ctx.act(h["ab_b"][:, psl], h["G3"][:, 0:GT],
                                           AF.Copy,
                                           wait=[mm_b, ctx.tok("abb%d" % par)]))
        return dict(ev_a=ev_a, ev_b=ev_b, par=par)

    for g in range(NG):
        gsl = slice(g * GT, (g + 1) * GT)
        ln_t = applied
        exp = {}
        for dd, bank, tok in proj_mms(ctx, slots[1], w_toks[1], lnb1[g % 2],
                                      ln_t):
            exp[dd] = ctx.bank_read(bank, ctx.act(
                c3(expk_buf[:, :])[:, dd, :], h[bank][:, 0:GT], AF.Exp,
                bias=h["biask"][:, dd:dd + 1],
                wait=[tok, ksm_t, rg_free if g == 0 else None]))
            last_kmm = tok
        last_exp = exp[NCH - 1]
        if g + 2 < NG:
            pend_a = chain_p01(ctx, g + 2, 2, 3, gates[g + 2])
        vev = {}
        for dd, bank, tok in proj_mms(ctx, slots[2], w_toks[2], lnb1[g % 2],
                                      ln_t):
            vev[dd] = ctx.bank_read(bank, ctx.act(
                c3(v_buf[:, :])[:, dd, :], h[bank][:, 0:GT], AF.Identity,
                bias=h["bv"][:, dd:dd + 1],
                wait=[tok, kern3_t, rg_free if g == 0 else None]))
            vmm_by[g] = tok
        last_vev = vev[NCH - 1]
        sum_mm = ctx.mmacc("G2", h["G2"][0:1, 0:GT],
                           [h["ones_col"][:, :]] * NCH,
                           [c3(expk_buf[:, :])[:, c, :] for c in range(NCH)],
                           wait_first=[last_exp])
        ln_s = ctx.bank_read("G2", ctx.act(h["inv_f0"], h["G2"][0:1, 0:GT],
                                           AF.Ln, wait=[sum_mm]))
        inv_t = ctx.act(h["invb0"], h["inv_f0"], AF.Exp, scale=-1.0,
                        wait=[ln_s])
        invbc_mm = ctx.mm(h["G2"][:, 0:GT], h["onesr_at"][0], h["invb0"],
                          start=True, stop=True,
                          wait=[inv_t, ctx.guard.get("G2")], inc=(("T", 1),))
        invbc_ev = ctx.bank_read("G2", ctx.act(h["invb"][:, :],
                                               h["G2"][:, 0:GT], AF.Copy,
                                               wait=[invbc_mm, ksm_t]))
        if pend_a is not None:
            chains[pend_a["g"]] = chain_p2(ctx, pend_a)
            pend_a = None
        if g == NG - 1:
            ch0 = rebcast(0)
            ln3_t = emit_apply(ctx, 0, ch0, extra_wait=[vmm_by[NG - 2]],
                               dst=lnbufs[0])
        if g + 1 < NG:
            applied = emit_apply(ctx, g + 1, chains[g + 1],
                                 extra_wait=[vmm_by[g - 1] if g >= 1
                                             else ctx.tok("xkfree")],
                                 dst=lnb1[(g + 1) % 2])
        if pend_tpose is not None:
            gp, tokp = pend_tpose
            tpose_done[gp] = do_tpose(gp, tokp)
            k3t_dma[gp] = tpose_done[gp][1]
        ksm_t = ctx.tt(c3(ksm_buf[:, :]), c3(expk_buf[:, :]),
                       h["invb"][:, :].unsqueeze(1).broadcast_to([128, NCH, GT]),
                       ALU.mult,
                       wait=[invbc_ev, last_exp,
                             ksm_dma[g - 1] if g else ctx.tok("ksmfree")])
        kern3_t = ctx.tt(c3(kern3_buf[:, :]), c3(ksm_buf[:, :]),
                         c3(v_buf[:, :]), ALU.mult,
                         wait=[ksm_t, last_vev,
                               tpose_done[g - 1][0] if g
                               else ctx.tok("kactfree")])
        poolr = ctx.red(h["kland_f"][:, :].rearrange("p (c l) -> p c l", c=NCH),
                        ksm_buf[:, :].rearrange("p (c o i) -> p c o i",
                                                c=NCH, i=32),
                        ALU.add, wait=[ksm_t, pool_t])
        pool_t = ctx.vcopy(h["kland"][:, :].rearrange(
                               "p (c l) -> p c l", c=NCH)[:, :, g * 16:(g + 1) * 16],
                           h["kland_f"][:, :].rearrange("p (c l) -> p c l",
                                                        c=NCH),
                           wait=[poolr])
        ksm_dma[g] = ctx.dma("sync",
                             out=h["ksm_scr"][:, :, gsl]
                                 .rearrange("c p t -> p c t"),
                             in_=c3(ksm_buf[:, :]), wait=[ksm_t], sem="KS")
        pend_tpose = (g, kern3_t)
    ctx.set_tok("poolfree", None)
    ctx.set_tok("rgfree", None)

    expq_buf = h["rg"]
    qsm_buf = h["krelu"]
    k1_evs = [None] * NG
    k1tok_by = [None] * NG
    qmm_by = [None] * NG
    qsm_t = None
    rd_toks = [None] * NG
    mid_mm = None

    def emit_k1(gq, qsm_tok):
        gsl = slice(gq * GT, (gq + 1) * GT)
        k1tok_by[gq] = ctx.mmacc(
            "G3", h["G3"][0:NL, 0:GT],
            [h["kland"][:, :].rearrange("p (c l) -> p c l", c=NCH)[:, c, :]
             for c in range(NCH)],
            [c3(qsm_buf[:, :])[:, c, :] for c in range(NCH)],
            wait_first=[qsm_tok, pool_t])
        k1_evs[gq] = ctx.bank_read("G3", ctx.vcopy(h["kern1T"][0:NL, gsl],
                                                   h["G3"][0:NL, 0:GT],
                                                   wait=[k1tok_by[gq]]))

    def emit_rd1(q, extra):
        return ctx.dma("sync",
                       out=c3(h["xk"][:, :]),
                       in_=h["ksm_scr"][:, :, q * GT:(q + 1) * GT]
                           .rearrange("c p t -> p c t"),
                       wait=[ksm_dma[NG - 1]] + extra, sem="RK0")

    def emit_rd2(q, extra):
        return ctx.dma("sync",
                       out=h["kact"][:, :].rearrange("p (t d) -> p t d", t=4),
                       in_=h["k3t_scr"][q * 4:(q + 1) * 4, :, :]
                           .rearrange("t p d -> p t d"),
                       wait=[k3t_dma[q]] + extra, sem="RK1")

    nt_ctr = [0]

    def emit_chunk(q, rd1, rd2, xr_evs=None):
        nonlocal mid_mm
        for tl in range(4):
            nt = nt_ctr[0]
            nt_ctr[0] += 1
            bank = "G0" if tl % 2 == 0 else "G1"
            ktok = ctx.mmacc(bank, h[bank][:, 0:NL],
                             [c3(h["xk"][:, :])[:, c, tl * 128:(tl + 1) * 128]
                              for c in range(NCH)],
                             [h["kland"][:, :].rearrange("p (c l) -> p c l",
                                                         c=NCH)[:, c, :]
                              for c in range(NCH)],
                             wait_first=[rd1, rd2, pool_t])
            k2_ev = ctx.bank_read(bank, ctx.vcopy(
                h["kern2T"][:, (nt % 2) * NL:(nt % 2 + 1) * NL],
                h[bank][:, 0:NL], wait=[ktok]))
            rhs_buf = k3t_buf if xr_evs is not None else h["kact"]
            for hh in range(2):
                mid_mm = ctx.mm(
                    h["M"][:, hh * 512:(hh + 1) * 512],
                    h["kern2T"][:, (nt % 2) * NL:(nt % 2 + 1) * NL],
                    rhs_buf[:, tl * 1024 + hh * 512:tl * 1024 + (hh + 1) * 512],
                    start=(nt == 0), stop=(nt == 15),
                    wait=([k2_ev, ctx.guard.get("M")] +
                          ([xr_evs[tl]] if xr_evs is not None else [])
                          if hh == 0 else ()),
                    inc=(("T", 1),) if hh == 1 else ())

    ln3_next = None
    for g in range(NG):
        if g + 1 < NG:
            ch = rebcast(g + 1)
            ln3_next = emit_apply(ctx, g + 1, ch,
                                  extra_wait=[qmm_by[g - 1] if g >= 1
                                              else ksm_dma[NG - 1]],
                                  dst=lnbufs[(g + 1) % 2])
        exq = {}
        for dd, bank, tok in proj_mms(
                ctx, slots[0], w_toks[0], lnbufs[g % 2], ln3_t,
                extra_first=([last_exp, ksm_t] if g == 0 else ())):
            exq[dd] = ctx.bank_read(bank, ctx.act(
                c3(expq_buf[:, :])[:, dd, :], h[bank][:, 0:GT], AF.Exp,
                bias=h["biasq"][:, dd:dd + 1], wait=[tok, qsm_t]))
            qmm_by[g] = tok
        if g == 0:
            gp, tokp = pend_tpose
            tpose_done[gp] = do_tpose(gp, tokp, spill=False)
            rd1 = emit_rd1(3, [vmm_by[NG - 1]])
            emit_chunk(3, rd1, None, xr_evs=tpose_done[gp][2])
            rd_toks[0] = (emit_rd1(0, [mid_mm]),
                          emit_rd2(0, [tpose_done[gp][0]]))
        else:
            emit_k1(g - 1, qsm_t)
            emit_chunk(g - 1, *rd_toks[g - 1])
            if g < NG - 1:
                rd_toks[g] = (emit_rd1(g, [mid_mm]), emit_rd2(g, [mid_mm]))
        ln3_t = ln3_next
        sum_mm = ctx.mmacc("G2", h["G2"][0:1, 0:GT],
                           [h["ones_col"][:, :]] * NCH,
                           [c3(expq_buf[:, :])[:, c, :] for c in range(NCH)],
                           wait_first=[exq[NCH - 1]])
        ln_s = ctx.bank_read("G2", ctx.act(h["inv_f1"], h["G2"][0:1, 0:GT],
                                           AF.Ln, wait=[sum_mm]))
        inv3 = ctx.act(h["invb1"], h["inv_f1"], AF.Exp, scale=-1.0,
                       wait=[ln_s])
        invbc_mm = ctx.mm(h["G2"][:, 0:GT], h["onesr_at"][0], h["invb1"],
                          start=True, stop=True,
                          wait=[inv3, ctx.guard.get("G2")], inc=(("T", 1),))
        invbc_ev = ctx.bank_read("G2", ctx.act(
            h["invb"][:, :], h["G2"][:, 0:GT], AF.Copy,
            wait=[invbc_mm, qsm_t, ksm_t if g == 0 else None]))
        qsm_t = ctx.tt(c3(qsm_buf[:, :]), c3(expq_buf[:, :]),
                       h["invb"][:, :].unsqueeze(1).broadcast_to([128, NCH, GT]),
                       ALU.mult,
                       wait=[invbc_ev, exq[NCH - 1],
                             k1tok_by[g - 1] if g >= 1 else None,
                             kern3_t if g == 0 else None])
    mid_ev = ctx.bank_read("M", ctx.act(h["mid"][:, :], h["M"][:, :], AF.Copy,
                                        scale=1.0 / 1024.0, wait=[mid_mm]))

    ctx.set_tok("xrfree", mid_mm)
    ctx.set_tok("xkfree", mid_mm)
    ctx.set_tok("lnfree", qmm_by[2])
    ctx.set_tok("difffree", qmm_by[3])
    ctx.set_tok("rgfree", qsm_t)
    ctx.set_tok("kactfree", mid_mm)
    ctx.set_tok("ksmfree", None)
    ctx.set_tok("k3evfree", None)

    res_toks = [None] * NG
    for g in range(NG):
        gsl = slice(g * GT, (g + 1) * GT)
        rtok = None
        for dd in range(NCH):
            bank = "G0" if dd % 2 == 0 else "G1"
            omm = ctx.mm(h[bank][:, 0:GT],
                         h["mid"][:, dd * 128:(dd + 1) * 128],
                         h["kern1T"][0:NL, gsl], start=True, stop=True,
                         wait=[k1_evs[g], mid_ev, ctx.guard.get(bank)],
                         inc=(("T", 1),))
            rtok = ctx.bank_read(bank, ctx.tt(
                c3(h["x"][:, :], T)[:, dd, gsl],
                c3(h["x"][:, :], T)[:, dd, gsl],
                h[bank][:, 0:GT], ALU.add, wait=[omm]))
        res_toks[g] = rtok
        if g == 0:
            emit_k1(NG - 1, qsm_t)
        if tail_emits and g < len(tail_emits):
            for cb in tail_emits[g]:
                cb(res_toks)
    return dict(slot_last=(qmm_by[NG - 1], last_kmm, vmm_by[NG - 1]),
                res=res_toks)



def emit_preamble(ctx):
    h = ctx.h
    ones_t = ctx.op("vector", "memset", h["ones_col"][:, :], 1.0,
                    inc=(("V", 1),))
    ctx.op("vector", "memset", h["ones_row"][:, :], 1.0, inc=(("V", 1),))
    eps_t = ctx.op("vector", "memset", h["cst_eps"][:, :], EPS,
                   inc=(("V", 1),))
    id0 = ctx.op("gpsimd", "memset", h["ident"][:, :], 0.0, inc=(("G", 1),))
    id_t = ctx.op("gpsimd", "affine_select", out=h["ident"][:, :],
                  in_=h["ident"][:, :], compare_op=ALU.not_equal, fill=1.0,
                  base=0, pattern=[[-1, 128]], channel_multiplier=1,
                  wait=[id0], inc=(("G", 1),))
    ctx.pre_ident = id_t
    ctx.dma("gpsimd", out=h["vecs_sb"][:, :], in_=h["vecs_hbm"][:, :],
            sem="PV")
    bfl_t = None
    for i, dst in enumerate((h["bflat0"], h["bflat1"])):
        bfl_t = ctx.dma("gpsimd",
                        out=dst.rearrange("o (c p) -> o c p", c=8),
                        in_=h["vecs_hbm"][:, i * 8:(i + 1) * 8]
                            .rearrange("p (o c) -> o c p", o=1),
                        sem="PV", allow_slow_non_contiguous=True)
    vec_t = bfl_t
    wq_t = cast_weight_sbuf(ctx, "w0", "wq_hbm")
    wk_t = cast_weight_sbuf(ctx, "w1", "wk_hbm")
    spill_q = ctx.dma("sync", out=h["wq_bf"][:, :], in_=h["w0"][:, :],
                      wait=[wq_t], sem="W0")
    spill_k = ctx.dma("sync", out=h["wk_bf"][:, :], in_=h["w1"][:, :],
                      wait=[wk_t], sem="W1")
    c_wr = cast_weight_hbm(ctx, "wr_bf", "wr_hbm")
    c_wkf = cast_weight_hbm(ctx, "wkf_bf", "wkf_hbm", wait=[c_wr])
    c_wvf = cast_weight_hbm(ctx, "wvf_bf", "wvf_hbm", wait=[c_wkf])
    c_wv = cast_weight_hbm(ctx, "wv_bf", "wv_hbm", wait=[c_wvf])

    b_toks = {}
    last_sq = None
    for i, (slot, wt, brow) in enumerate((("w0", wq_t, 0), ("w1", wk_t, 1))):
        sq0 = ctx.act(h["xr"][:, :], h[slot][:, 0:4096], AF.Square,
                      wait=[wt, last_sq])
        sq1 = ctx.act(h["xk"][:, :], h[slot][:, 4096:8192], AF.Square,
                      wait=[wt])
        cn_mm = {}
        for dg in range(2):
            bank = "G2" if dg == 0 else "G3"
            lhs, rhs = [], []
            for c in range(NCH):
                buf = h["xr"] if c < 4 else h["xk"]
                lhs.append(h["ones_col"][:, :])
                rhs.append(buf[:, (c % 4) * 1024 + dg * 512:
                               (c % 4) * 1024 + dg * 512 + 512])
            cn_mm[dg] = ctx.mmacc(bank, h[bank][0:1, 0:512], lhs, rhs,
                                  wait_first=[sq0, sq1, ones_t])
        last_sq = cn_mm[1]
        ev0 = ctx.bank_read("G2", ctx.act(h["m_row"][:, 0:512],
                                          h["G2"][0:1, 0:512], AF.Copy,
                                          wait=[cn_mm[0]]))
        ev1 = ctx.bank_read("G3", ctx.act(h["m_row"][:, 512:1024],
                                          h["G3"][0:1, 0:512], AF.Copy,
                                          wait=[cn_mm[1]]))
        lnt = ctx.act(h["s2_row"][:, 0:1024], h["m_row"][:, 0:1024], AF.Ln,
                      bias=h["cst_eps"][0:1, 0:1], wait=[ev0, ev1, eps_t])
        cnt = ctx.act(h["m_row"][:, 0:1024], h["s2_row"][:, 0:1024], AF.Exp,
                      scale=0.5, wait=[lnt])
        m1 = ctx.red(h["stat_m2"][:, 0:1], h["m_row"][:, 0:1024], ALU.max,
                     wait=[cnt])
        m2 = ctx.red(h["stat_m2"][:, 1:2],
                     h["bflat0"] if brow == 0 else h["bflat1"],
                     ALU.max, apply_absolute_value=True, wait=[bfl_t])
        b1 = ctx.ts(h["stat_m2"][:, 2:3], h["stat_m2"][:, 0:1], 32.0,
                    ALU.mult, wait=[m1])
        b2 = ctx.tt(h["stat_m2"][:, 2:3], h["stat_m2"][:, 2:3],
                    h["stat_m2"][:, 1:2], ALU.add, wait=[b1, m2])
        b3 = ctx.ts(h["stat_m2"][:, 2:3], h["stat_m2"][:, 2:3], 2.0,
                    ALU.add, wait=[b2])
        bf_c = ctx.vcopy(h["stat_bf"][0:1, 0:1], h["stat_m2"][:, 2:3],
                         wait=[b3])
        bmm = ctx.mm(h["G2"][:, 0:1], h["onesr_at"][0],
                     h["stat_bf"][0:1, 0:1], start=True, stop=True,
                     wait=[bf_c, ctx.guard.get("G2")], inc=(("T", 1),))
        bev = ctx.bank_read("G2", ctx.act(h["Bcol"][:, i:i + 1],
                                          h["G2"][:, 0:1], AF.Copy,
                                          wait=[bmm]))
        boff = 0 if i == 0 else 8
        b_toks[i] = ctx.op("vector", "tensor_scalar",
                           h["biasqk"][:, i * 8:(i + 1) * 8],
                           h["vecs_sb"][:, boff:boff + 8],
                           h["Bcol"][:, i:i + 1], None, ALU.subtract,
                           wait=[bev, vec_t], inc=(("V", 1),))
    ctx.set_tok("m2", b_toks[1])
    ctx.set_tok("lnx", b_toks[1])
    ctx.set_tok("mrow", b_toks[1])
    ctx.set_tok("xrfree", last_sq)
    ctx.set_tok("xkfree", last_sq)
    return dict(bias_done=(b_toks[0], b_toks[1]),
                bcomp_last=(last_sq, b_toks[0], b_toks[1]),
                vecs=vec_t, spills=(spill_q, spill_k),
                casts=dict(wr=c_wr, wkf=c_wkf, wvf=c_wvf, wv=c_wv))



def program(ctx):
    h = ctx.h
    pre = emit_preamble(ctx)
    spill_q, spill_k = pre["spills"]
    casts = pre["casts"]

    wr_t = load_weight(ctx, "w0", "wr_bf",
                       wait=list(pre["bcomp_last"]) + [spill_q, casts["wr"]])
    wkf_t = load_weight(ctx, "w1", "wkf_bf",
                        wait=list(pre["bcomp_last"]) + [spill_k, casts["wkf"]])
    wvf_t = load_weight(ctx, "w2", "wvf_bf", wait=[casts["wvf"]])
    ffn_slots = ("w0", "w1", "w2")
    ffn_toks = (wr_t, wkf_t, wvf_t)

    x_loads = None
    f1st = None
    f2 = None
    for s in range(NSEQ):
        if s == 0:
            x_loads = [ctx.dma(
                "sync",
                out=c3(h["x"][:, :], T)[:, :, g * GT:(g + 1) * GT],
                in_=h["x_hbm"][s, :, :, g * GT:(g + 1) * GT]
                    .rearrange("c p t -> p c t"),
                sem="XI%d" % g) for g in range(NG)]
        gates1 = [[x_loads[g]] + (list(pre["bias_done"]) if s == 0 else [])
                  for g in range(NG)]
        attn_pre = {}
        ap = {}

        def a0_p01(res, xl):
            ap["c0"] = chain_p01(ctx, 0, 2, 3, [res[0]])

        def a0_p2(res, xl):
            attn_pre[0] = chain_p2(ctx, ap["c0"])

        def a1_p01(res, xl):
            attn_pre[1] = chain_p01(ctx, 1, 2, 3, [res[1]])

        def a_ap0(res, xl):
            attn_pre["ap0"] = emit_apply(ctx, 0, attn_pre[0],
                                         extra_wait=[ctx.tok("lnfree")])
        tails1 = {(2, "mid2"): [a0_p01],
                  (2, "post"): [a0_p2],
                  (3, "mid2"): [a1_p01, a_ap0]}
        f1 = emit_ffn(ctx, s, ffn_slots, ffn_toks, False, gates1, st=f1st,
                      tails=tails1)
        sl = f1["slot_last"]
        wq_t = load_weight(ctx, ffn_slots[0], "wq_bf", wait=[sl[0], spill_q])
        wka_t = load_weight(ctx, ffn_slots[1], "wk_bf", wait=[sl[1], spill_k])
        wva_t = load_weight(ctx, ffn_slots[2], "wv_bf",
                            wait=[sl[2], casts["wv"]])
        gates2 = [[f1["res"][g]] for g in range(NG)]
        f2st = new_ffn_state()
        f2p = {}

        def f20_p01(res):
            f2p["c0"] = chain_p01(ctx, 0, 0, 1, [res[0]], cast_eng="scalar")

        def f20_p2(res):
            f2st["chains"][0] = chain_p2(ctx, f2p["c0"])

        def f21_p01(res):
            f2st["pend1"] = chain_p01(ctx, 1, 0, 1, [res[1]],
                                      cast_eng="scalar")

        def f2_prep(res):
            ffn_prep(ctx, 0, f2st, extra_wait=[ctx.tok("lnfree")])

        def f2_mix(res):
            f2st["mix_r"] = ffn_mixes(ctx, 0, "r", ctx.tok("xrfree"), f2st)
            f2st["mix_k"] = ffn_mixes(ctx, 0, "k", ctx.tok("xkfree"), f2st)
        tails3b = [[f20_p01], [f20_p2, f2_prep, f21_p01], [f2_mix], []]
        a1 = emit_attn(ctx, s, ffn_slots, (wq_t, wka_t, wva_t), gates2,
                       pre_chains=attn_pre, pre_applied=attn_pre["ap0"],
                       tail_emits=tails3b)
        asl = a1["slot_last"]
        new_slots = (ffn_slots[1], ffn_slots[2], ffn_slots[0])
        wr_t = load_weight(ctx, new_slots[0], "wr_bf",
                           wait=[asl[1], casts["wr"]])
        wkf_t = load_weight(ctx, new_slots[1], "wkf_bf",
                            wait=[asl[2], casts["wkf"]])
        wvf_t = load_weight(ctx, new_slots[2], "wvf_bf",
                            wait=[asl[0], casts["wvf"]])
        ffn_slots = new_slots
        ffn_toks = (wr_t, wkf_t, wvf_t)
        gates3 = [[a1["res"][g]] for g in range(NG)]
        if s + 1 < NSEQ:
            f1st = new_ffn_state()
            nst = f1st
            np_ = {}

            def n0_p01(res, xl):
                np_["c0"] = chain_p01(ctx, 0, 0, 1, [xl[0]])

            def n0_p2(res, xl):
                nst["chains"][0] = chain_p2(ctx, np_["c0"])

            def n_prep(res, xl):
                ffn_prep(ctx, 0, nst, extra_wait=[ctx.tok("lnfree")])

            def n1_p01(res, xl):
                nst["pend1"] = chain_p01(ctx, 1, 0, 1, [xl[1]])

            def n_mix(res, xl):
                nst["mix_r"] = ffn_mixes(ctx, 0, "r", ctx.tok("xrfree"), nst)
                nst["mix_k"] = ffn_mixes(ctx, 0, "k", ctx.tok("xkfree"), nst)
            tails2 = {(2, "mid2"): [n0_p01],
                      (2, "post"): [n0_p2],
                      (3, "mid"): [n_prep],
                      (3, "mid2"): [n1_p01, n_mix]}
            oxl = (s + 1,)
        else:
            f1st = None
            tails2 = None
            oxl = None
        f2 = emit_ffn(ctx, s, ffn_slots, ffn_toks, True, gates3, st=f2st,
                      tails=tails2, out_xload=oxl)
        x_loads = f2["xloads"]

    ctx.op("sync", "nop", wait=list(f2["out"]))



def build_nc():
    import contextlib
    nc = bass.Bass("TRN2", target_bir_lowering=False, debug=False,
                   detect_race_conditions=False)
    h = {}
    h["x_hbm"] = nc.dram_tensor("x_in", [NSEQ, NCH, 128, T], F32,
                                kind="ExternalInput").ap()
    h["out_hbm"] = nc.dram_tensor("out", [NSEQ, NCH, 128, T], F32,
                                  kind="ExternalOutput").ap()
    for nm in ("wq", "wk", "wv", "wr", "wkf", "wvf"):
        h[nm + "_hbm"] = nc.dram_tensor(nm, [NCH, 128, 1024], F32,
                                        kind="ExternalInput").ap()
    h["vecs_hbm"] = nc.dram_tensor("vecs", [128, 40], F32,
                                   kind="ExternalInput").ap()
    h["ksm_scr"] = nc.dram_tensor("ksm_scr", [NCH, 128, T], BF16,
                                  kind="Internal").ap()
    h["k3t_scr"] = nc.dram_tensor("k3t_scr", [16, 128, 1024], BF16,
                                  kind="Internal").ap()
    for nm in ("wq", "wk", "wv", "wr", "wkf", "wvf"):
        h[nm + "_bf"] = nc.dram_tensor(nm + "_bf", [128, NCH * 1024], BF16,
                                       kind="Internal").ap()

    stack = contextlib.ExitStack()

    def sb(name, shape, dtype):
        return stack.enter_context(nc.sbuf_tensor(name, shape, dtype))

    def ps(name, shape, dtype):
        return stack.enter_context(nc.psum_tensor(name, shape, dtype))

    h["x"] = sb("x", [128, NCH * T], F32)
    for w in ("w0", "w1", "w2"):
        h[w] = sb(w, [128, NCH * 1024], BF16)
    for nm in ("ln", "diff", "xr", "xk", "rg", "krelu", "kact"):
        h[nm] = sb(nm, [128, NCH * GT], BF16)
    h["ksm_g"] = h["diff"]
    h["k3t_ev"] = h["xr"]
    h["castb"] = sb("castb", [128, 2 * GT], BF16)
    h["sqb"] = sb("sqb", [128, 2 * GT], BF16)
    h["stg"] = sb("stg", [128, 2 * GT], BF16)
    h["ab_a"] = sb("ab_a", [128, 2 * GT], BF16)
    h["ab_b"] = sb("ab_b", [128, 2 * GT], BF16)
    h["invb"] = sb("invb", [128, GT], BF16)
    h["sums"] = sb("sums", [33, T], F32)
    h["ab"] = sb("ab", [65, 2 * T], BF16)
    h["stat_tmp"] = sb("stat_tmp", [33, GT], F32)
    h["stat_bf"] = sb("stat_bf", [1, 4], BF16)
    h["inv_bb"] = sb("inv_bb", [1, GT], BF16)
    h["kern2T"] = sb("kern2T", [128, 2 * NL], BF16)
    h["mid"] = sb("mid", [NL, 1024], BF16)
    h["kern1T"] = sb("kern1T", [NL, T], BF16)
    h["kland"] = sb("kland", [128, NCH * NL], BF16)
    h["kland_f"] = sb("kland_f", [128, NCH * 16], F32)
    h["vecs_sb"] = sb("vecs_sb", [128, 40], F32)
    h["biasqk"] = sb("biasqk", [128, 16], F32)
    h["Bcol"] = sb("Bcol", [128, 2], F32)
    h["ones_col"] = sb("ones_col", [128, 1], BF16)
    h["ones_row"] = sb("ones_row", [65, 128], BF16)
    h["ident"] = sb("ident", [128, 128], BF16)
    h["lnprev"] = sb("lnprev", [128, 2 * NCH], BF16)
    h["cst_eps"] = sb("cst_eps", [1, 1], F32)

    h["G0"] = ps("G0", [128, 512], F32)
    h["G1"] = ps("G1", [128, 512], F32)
    h["G2"] = ps("G2", [128, 512], F32)
    h["G3"] = ps("G3", [128, 512], F32)
    h["M"] = ps("M", [NL, 1024], F32)
    h["trA"] = ps("trA", [128, 2048], BF16)

    h["m_row"] = h["sums"][0:1, 0:T]
    h["s2_row"] = h["sums"][32:33, 0:T]
    h["ab_r"] = [h["ab"][0:1, 0:T], h["ab"][32:33, 0:T],
                 h["ab"][0:1, T:2 * T], h["ab"][64:65, 0:T]]
    h["ab_bp"] = [0, 32, 0, 64]
    h["bflat0"] = h["sums"][0:1, 1024:2048]
    h["bflat1"] = h["sums"][32:33, 1024:2048]
    h["invb0"] = h["inv_bb"][0:1, 0:GT]
    h["invb1"] = h["inv_bb"][0:1, 0:GT]
    h["inv_f0"] = h["stat_tmp"][0:1, 0:GT]
    h["inv_f1"] = h["stat_tmp"][0:1, 0:GT]
    h["stat_m2"] = h["stat_tmp"][32:33, 0:GT]
    h["stat_ln"] = h["stat_tmp"][0:1, 0:GT]
    h["onesr_at"] = {0: h["ones_row"][0:1, :], 32: h["ones_row"][32:33, :],
                     64: h["ones_row"][64:65, :]}
    h["mur"] = h["vecs_sb"][:, 24:32]
    h["muk"] = h["vecs_sb"][:, 32:40]
    h["bv"] = h["vecs_sb"][:, 16:24]
    h["biasq"] = h["biasqk"][:, 0:8]
    h["biask"] = h["biasqk"][:, 8:16]

    sem_names = ("T", "V", "S", "G", "W0", "W1", "W2", "PV", "WC",
                 "XI0", "XI1", "XI2", "XI3", "XO0", "XO1", "XO2", "XO3",
                 "KS", "K3", "RK0", "RK1")
    sems = {nm: stack.enter_context(nc.semaphore("sem_" + nm))
            for nm in sem_names}

    with nc.Block() as block:
        def run(engine):
            def f(_eng):
                local = {nm: [sems[nm], 0] for nm in sem_names}
                ctx = Ctx(nc, engine, local, h)
                program(ctx)
            return f
        block.sync(run("sync"))
        block.gpsimd(run("gpsimd"))
        block.tensor(run("tensor"))
        block.scalar(run("scalar"))
        block.vector(run("vector"))

    stack.close()
    return nc



_NC_CACHE = {}


def kernel(**inputs):
    x = np.asarray(inputs["L0_attn_hidden_states"], dtype=np.float32)
    B, TT_, D = x.shape
    n_cores = 8
    per = B // n_cores

    xfm = np.ascontiguousarray(
        x.reshape(B, TT_, NCH, 128).transpose(0, 2, 3, 1))

    def warr(w):
        return np.ascontiguousarray(
            np.asarray(w, dtype=np.float32).reshape(NCH, 128, 1024))

    wq = warr(inputs["L0_attn_Wq"])
    wk = warr(inputs["L0_attn_Wk"])
    wv = warr(inputs["L0_attn_Wv"])
    wr = warr(np.asarray(inputs["L0_ffn_W_r"], dtype=np.float32).T)
    wkf = warr(np.asarray(inputs["L0_ffn_W_k"], dtype=np.float32).T)
    wvf = warr(np.asarray(inputs["L0_ffn_W_v"], dtype=np.float32).T)

    def pcol(v):
        return np.ascontiguousarray(
            np.asarray(v, dtype=np.float32).reshape(NCH, 128).T)

    vecs = np.ascontiguousarray(np.concatenate(
        [pcol(inputs["L0_attn_bq"]), pcol(inputs["L0_attn_bk"]),
         pcol(inputs["L0_attn_bv"]), pcol(inputs["L0_ffn_mu_r"]),
         pcol(inputs["L0_ffn_mu_k"])], axis=1))

    if "nc" not in _NC_CACHE:
        _NC_CACHE["nc"] = build_nc()
    nc = _NC_CACHE["nc"]

    in_maps = []
    for c in range(n_cores):
        in_maps.append({
            "x_in": np.ascontiguousarray(xfm[c * per:(c + 1) * per]),
            "wq": wq, "wk": wk, "wv": wv,
            "wr": wr, "wkf": wkf, "wvf": wvf,
            "vecs": vecs,
        })
    res = run_bass_kernel_spmd(nc, in_maps, core_ids=list(range(n_cores)))
    outs = [res.results[c]["out"] for c in range(n_cores)]
    full = np.concatenate(outs, axis=0).reshape(B, NCH, 128, T)
    out = np.ascontiguousarray(full.transpose(0, 3, 1, 2).reshape(B, TT_, D))
    return out


if __name__ == "__main__":
    build_nc()
    print("build OK")


# version 24
# speedup vs baseline: 1.0813x; 1.0407x over previous
import numpy as np
import concourse.bass as bass
import concourse.mybir as mybir
from concourse.bass_utils import run_bass_kernel_spmd

F32 = mybir.dt.float32
BF16 = mybir.dt.bfloat16
AF = mybir.ActivationFunctionType
ALU = mybir.AluOpType
AX = mybir.AxisListType

T = 2048
NSEQ = 4
NCH = 8
NG = 4
GT = 512
NL = 64
EPS = 1e-5


class Ctx:

    def __init__(self, nc, active, sems, h):
        self.nc = nc
        self.active = active
        self.sems = sems
        self.h = h
        self.guard = {}
        self.pre_ident = None
        self._toks = {}

    def tok(self, name):
        return self._toks.get(name)

    def set_tok(self, name, tok):
        self._toks[name] = tok

    def op(self, engine, fn, *args, wait=(), inc=(), **kwargs):
        if self.active == engine:
            eng = getattr(self.nc, engine)
            wmax = {}
            for tk in wait:
                if tk is None:
                    continue
                nm, v = tk
                wmax[nm] = max(wmax.get(nm, 0), v)
            for nm, v in wmax.items():
                eng.wait_ge(self.sems[nm][0], v)
            instr = getattr(eng, fn)(*args, **kwargs)
            for nm, k in inc:
                instr.then_inc(self.sems[nm][0], k)
        toks = []
        for nm, k in inc:
            self.sems[nm][1] += k
            toks.append((nm, self.sems[nm][1]))
        return toks[0] if len(toks) == 1 else (toks if toks else None)

    def mm(self, out, lhsT, rhs, start, stop, wait=(), inc=()):
        return self.op("tensor", "matmul", out, lhsT, rhs,
                       start=start, stop=stop, wait=wait, inc=inc)

    def mmacc(self, bank, out, lhsT_list, rhs_list, wait_first=()):
        n = len(lhsT_list)
        tok = None
        for i in range(n):
            w = (list(wait_first) + [self.guard.get(bank)]) if i == 0 else ()
            tok = self.mm(out, lhsT_list[i], rhs_list[i],
                          start=(i == 0), stop=(i == n - 1),
                          wait=w, inc=((("T", 1),) if i == n - 1 else ()))
        return tok

    def bank_read(self, bank, tok):
        self.guard[bank] = tok
        return tok

    def tr(self, out, in_, wait=(), inc=()):
        return self.op("tensor", "transpose", out, in_, self.h["ident"][:, :],
                       wait=wait, inc=inc)

    def act(self, out, in_, func, wait=(), inc=(("S", 1),), **kw):
        return self.op("scalar", "activation", out, in_, func,
                       wait=wait, inc=inc, **kw)

    def vcopy(self, out, in_, wait=(), inc=(("V", 1),)):
        return self.op("vector", "tensor_copy", out, in_, wait=wait, inc=inc)

    def tt(self, out, in0, in1, alu, wait=(), inc=(("V", 1),)):
        return self.op("vector", "tensor_tensor", out, in0, in1, alu,
                       wait=wait, inc=inc)

    def ts(self, out, in0, scalar, alu, wait=(), inc=(("V", 1),)):
        return self.op("vector", "tensor_single_scalar", out, in0, scalar, alu,
                       wait=wait, inc=inc)

    def stt(self, out, in0, scalar, in1, op0, op1, wait=(), inc=(("V", 1),)):
        return self.op("vector", "scalar_tensor_tensor", out, in0, scalar, in1,
                       op0, op1, wait=wait, inc=inc)

    def red(self, out, in_, alu, wait=(), inc=(("V", 1),), **kw):
        return self.op("vector", "tensor_reduce", out, in_, AX.X, alu,
                       wait=wait, inc=inc, **kw)

    def gcopy(self, out, in_, wait=(), inc=(("G", 1),)):
        return self.op("gpsimd", "tensor_copy", out, in_, wait=wait, inc=inc)

    def gtt(self, out, in0, in1, alu, wait=(), inc=(("G", 1),)):
        return self.op("gpsimd", "tensor_tensor", out, in0, in1, alu,
                       wait=wait, inc=inc)

    def gstt(self, out, in0, scalar, in1, op0, op1, wait=(), inc=(("G", 1),)):
        return self.op("gpsimd", "scalar_tensor_tensor", out, in0, scalar, in1,
                       op0, op1, wait=wait, inc=inc)

    def dma(self, queue, out, in_, wait=(), sem="XO", **kw):
        return self.op(queue, "dma_start", out=out, in_=in_,
                       wait=wait, inc=((sem, 16),), **kw)


def c3(ap2d, n=GT):
    return ap2d.rearrange("p (c n) -> p c n", c=NCH)



def chain_p01(ctx, g, row_a, row_b, gates, cast_eng="vector"):
    h = ctx.h
    gsl = slice(g * GT, (g + 1) * GT)
    st = dict(g=g, row_a=row_a, row_b=row_b)
    s1_tok = s2_tok = None
    for c in range(NCH):
        cp = slice((c % 2) * GT, (c % 2 + 1) * GT)
        cwait = list(gates) + [ctx.tok("cs%d" % (c % 2)),
                               ctx.tok("csq%d" % (c % 2))]
        if cast_eng == "vector":
            cast_t = ctx.vcopy(h["castb"][:, cp],
                               c3(h["x"][:, :], T)[:, c, gsl], wait=cwait)
        else:
            cast_t = ctx.act(h["castb"][:, cp],
                             c3(h["x"][:, :], T)[:, c, gsl], AF.Copy,
                             wait=cwait)
        sq_t = ctx.act(h["sqb"][:, cp], h["castb"][:, cp], AF.Square,
                       wait=[cast_t, ctx.tok("sq%d" % (c % 2))])
        ctx.set_tok("csq%d" % (c % 2), sq_t)
        s1_tok = ctx.mm(h["G2"][0:1, 0:GT], h["ones_col"][:, :],
                        h["castb"][:, cp], start=(c == 0), stop=(c == NCH - 1),
                        wait=[cast_t] +
                             ([ctx.guard.get("G2")] if c == 0 else []),
                        inc=(("T", 1),))
        ctx.set_tok("cs%d" % (c % 2), s1_tok)
        s2_tok = ctx.mm(h["G3"][0:1, 0:GT], h["ones_col"][:, :],
                        h["sqb"][:, cp], start=(c == 0), stop=(c == NCH - 1),
                        wait=[sq_t] +
                             ([ctx.guard.get("G3")] if c == 0 else []),
                        inc=(("T", 1),))
        ctx.set_tok("sq%d" % (c % 2), s2_tok)
    ev1 = ctx.bank_read("G2", ctx.act(h["m_row"][:, gsl], h["G2"][0:1, 0:GT],
                                      AF.Copy, scale=1.0 / 1024.0,
                                      wait=[s1_tok, ctx.tok("mrow")]))
    ev2 = ctx.bank_read("G3", ctx.act(h["s2_row"][:, gsl], h["G3"][0:1, 0:GT],
                                      AF.Copy, scale=1.0 / 1024.0,
                                      wait=[s2_tok]))
    m = h["m_row"][:, gsl]
    s2m = h["s2_row"][:, gsl]
    t_m2 = ctx.tt(h["stat_m2"], m, m, ALU.mult, wait=[ev1, ctx.tok("m2")])
    t_var = ctx.tt(s2m, s2m, h["stat_m2"], ALU.subtract, wait=[ev2, t_m2])
    ctx.set_tok("m2", t_var)
    t_ln = ctx.act(h["stat_ln"], s2m, AF.Ln, bias=h["cst_eps"][0:1, 0:1],
                   wait=[t_var, ctx.tok("lnx")])
    t_a = ctx.act(h["ab_r"][row_a][:, gsl], h["stat_ln"], AF.Exp, scale=-0.5,
                  wait=[t_ln])
    ctx.set_tok("lnx", t_a)
    t_b = ctx.tt(h["ab_r"][row_b][:, gsl], m, h["ab_r"][row_a][:, gsl],
                 ALU.mult, wait=[t_a])
    ctx.set_tok("mrow", t_b)
    st["t_a"] = t_a
    st["t_b"] = t_b
    return st


def chain_p2(ctx, st):
    h = ctx.h
    g, row_a, row_b = st["g"], st["row_a"], st["row_b"]
    gsl = slice(g * GT, (g + 1) * GT)
    par = g % 2
    psl = slice(par * GT, (par + 1) * GT)
    mm_a = ctx.mm(h["G2"][:, 0:GT], h["onesr_at"][h["ab_bp"][row_a]],
                  h["ab_r"][row_a][:, gsl], start=True, stop=True,
                  wait=[st["t_a"], ctx.guard.get("G2")], inc=(("T", 1),))
    mm_b = ctx.mm(h["G3"][:, 0:GT], h["onesr_at"][h["ab_bp"][row_b]],
                  h["ab_r"][row_b][:, gsl], start=True, stop=True,
                  wait=[st["t_b"], ctx.guard.get("G3")], inc=(("T", 1),))
    ev_a = ctx.bank_read("G2", ctx.act(h["ab_a"][:, psl], h["G2"][:, 0:GT],
                                       AF.Copy,
                                       wait=[mm_a, ctx.tok("aba%d" % par)]))
    ev_b = ctx.bank_read("G3", ctx.act(h["ab_b"][:, psl], h["G3"][:, 0:GT],
                                       AF.Copy,
                                       wait=[mm_b, ctx.tok("abb%d" % par)]))
    return dict(ev_a=ev_a, ev_b=ev_b, par=par, b_tok=st["t_b"])


def emit_ln_chain(ctx, g, row_a, row_b, gates):
    return chain_p2(ctx, chain_p01(ctx, g, row_a, row_b, gates))


def emit_apply(ctx, g, chain, extra_wait=(), dst=None):
    h = ctx.h
    dst = dst if dst is not None else h["ln"]
    gsl = slice(g * GT, (g + 1) * GT)
    par = chain["par"]
    psl = slice(par * GT, (par + 1) * GT)
    d3 = c3(dst[:, :])
    t1 = ctx.tt(d3, c3(h["x"][:, :], T)[:, :, gsl],
                h["ab_a"][:, psl].unsqueeze(1).broadcast_to([128, NCH, GT]),
                ALU.mult, wait=[chain["ev_a"]] + list(extra_wait))
    tok = ctx.tt(d3, d3,
                 h["ab_b"][:, psl].unsqueeze(1).broadcast_to([128, NCH, GT]),
                 ALU.subtract, wait=[chain["ev_b"], t1])
    ctx.set_tok("aba%d" % par, tok)
    ctx.set_tok("abb%d" % par, tok)
    return tok


def load_weight(ctx, slot, bf_name, wait=()):
    h = ctx.h
    return ctx.dma("sync", out=h[slot][:, :], in_=h[bf_name][:, :],
                   wait=wait, sem="W" + slot[-1])


def cast_weight_sbuf(ctx, slot, hbm_name, wait=()):
    h = ctx.h
    return ctx.dma("gpsimd",
                   out=h[slot][:, :].rearrange("p (c d) -> p c d", c=NCH),
                   in_=h[hbm_name][:, :, :].rearrange("c p d -> p c d"),
                   wait=wait, sem="W" + slot[-1])


def cast_weight_hbm(ctx, bf_name, hbm_name, wait=()):
    h = ctx.h
    return ctx.dma("gpsimd",
                   out=h[bf_name][:, :].rearrange("p (c d) -> p c d", c=NCH),
                   in_=h[hbm_name][:, :, :].rearrange("c p d -> p c d"),
                   wait=wait, sem="WC")


def wtile(ctx, slot, c, dd):
    return ctx.h[slot][:, c * 1024 + dd * 128: c * 1024 + (dd + 1) * 128]


def proj_mms(ctx, slot, w_tok, rhs_buf, rhs_tok, extra_first=()):
    h = ctx.h
    for dd in range(NCH):
        bank = "G0" if dd % 2 == 0 else "G1"
        tok = ctx.mmacc(bank, h[bank][:, 0:GT],
                        [wtile(ctx, slot, c, dd) for c in range(NCH)],
                        [c3(rhs_buf[:, :])[:, c, :] for c in range(NCH)],
                        wait_first=[rhs_tok, w_tok] + list(extra_first))
        yield dd, bank, tok



def new_ffn_state():
    return dict(chains={}, applied={}, diff={}, lnprev={}, mix_r=None,
                mix_k=None)


def ffn_prep(ctx, g, st, extra_wait=()):
    h = ctx.h
    if -1 not in st["lnprev"]:
        st["lnprev"][-1] = ctx.op("vector", "memset",
                                  h["lnprev"][:, NCH:2 * NCH], 0.0,
                                  inc=(("V", 1),))
    st["applied"][g] = emit_apply(ctx, g, st["chains"][g],
                                  extra_wait=list(extra_wait))
    lp_w = slice((g % 2) * NCH, (g % 2) * NCH + NCH)
    st["lnprev"][g] = ctx.vcopy(h["lnprev"][:, lp_w].unsqueeze(2),
                                c3(h["ln"][:, :])[:, :, GT - 1:GT],
                                wait=[st["applied"][g]])
    d1 = ctx.tt(c3(h["diff"][:, :])[:, :, 1:GT],
                c3(h["ln"][:, :])[:, :, 1:GT],
                c3(h["ln"][:, :])[:, :, 0:GT - 1], ALU.subtract,
                wait=[st["applied"][g], ctx.tok("difffree")])
    lp_r = slice(((g - 1) % 2) * NCH, ((g - 1) % 2) * NCH + NCH)
    st["diff"][g] = ctx.tt(c3(h["diff"][:, :])[:, :, 0:1],
                           c3(h["ln"][:, :])[:, :, 0:1],
                           h["lnprev"][:, lp_r].unsqueeze(2), ALU.subtract,
                           wait=[d1, st["lnprev"][g - 1]])


def ffn_mixes(ctx, g, which, guard_tok, st):
    h = ctx.h
    mu = h["mur"] if which == "r" else h["muk"]
    dst = h["xr"] if which == "r" else h["xk"]
    lp_r = slice(((g - 1) % 2) * NCH, ((g - 1) % 2) * NCH + NCH)
    tok = None
    for c in range(NCH):
        ctx.stt(c3(dst[:, :])[:, c, 1:GT],
                c3(h["diff"][:, :])[:, c, 1:GT], mu[:, c:c + 1],
                c3(h["ln"][:, :])[:, c, 0:GT - 1], ALU.mult, ALU.add,
                wait=[st["diff"][g], guard_tok] if c == 0 else ())
        tok = ctx.stt(c3(dst[:, :])[:, c, 0:1],
                      c3(h["diff"][:, :])[:, c, 0:1], mu[:, c:c + 1],
                      h["lnprev"][:, lp_r][:, c:c + 1].unsqueeze(2),
                      ALU.mult, ALU.add, wait=())
    return tok


def emit_ffn(ctx, s, slots, w_toks, out_dma, gates, st=None, tails=None,
             out_xload=None):
    h = ctx.h
    st = st or new_ffn_state()
    tails = tails or {}
    pend = {}

    def chain_full(g):
        if g not in st["chains"]:
            st["chains"][g] = emit_ln_chain(ctx, g, 0, 1, gates[g])
    chain_full(0)
    if "pend1" in st:
        st["chains"][1] = chain_p2(ctx, st.pop("pend1"))
    chain_full(1)
    if 0 not in st["applied"]:
        ffn_prep(ctx, 0, st, extra_wait=[ctx.tok("lnfree")])
    mix_r = st["mix_r"] or ffn_mixes(ctx, 0, "r", ctx.tok("xrfree"), st)
    mix_k = st["mix_k"] or ffn_mixes(ctx, 0, "k", ctx.tok("xkfree"), st)
    res_toks = [None] * NG
    out_toks = [None] * NG
    xload_toks = [None] * NG
    prod_t = ctx.tok("rgfree")
    kact_free = ctx.tok("kactfree")
    last_r = last_k = last_v = None

    def run_tails(g, point):
        for cb in tails.get((g, point), ()):
            cb(res_toks, xload_toks)

    for g in range(NG):
        gsl = slice(g * GT, (g + 1) * GT)
        if g + 1 < NG:
            ffn_prep(ctx, g + 1, st, extra_wait=[mix_k])
        sig = {}
        for dd, bank, tok in proj_mms(ctx, slots[0], w_toks[0], h["xr"],
                                      mix_r):
            sig[dd] = ctx.bank_read(bank, ctx.act(
                c3(h["rg"][:, :])[:, dd, :], h[bank][:, 0:GT], AF.Sigmoid,
                wait=[tok, prod_t]))
            last_r = tok
        if g == NG - 1:
            ctx.set_tok("xrfree", last_r)
        if g + 1 < NG:
            mix_r = ffn_mixes(ctx, g + 1, "r", last_r, st)
        run_tails(g, "mid")
        sq_t = None
        for dd, bank, tok in proj_mms(ctx, slots[1], w_toks[1], h["xk"],
                                      mix_k):
            rtok = ctx.bank_read(bank, ctx.act(
                c3(h["krelu"][:, :])[:, dd, :], h[bank][:, 0:GT], AF.Relu,
                wait=[tok, prod_t]))
            sq_t = ctx.act(c3(h["kact"][:, :])[:, dd, :],
                           c3(h["krelu"][:, :])[:, dd, :], AF.Square,
                           wait=[rtok, kact_free if dd == 0 else None,
                                 last_v if dd == 0 else None])
            last_k = tok
        kact_free = None
        if g == NG - 1:
            ctx.set_tok("xkfree", last_k)
        if g + 2 < NG and g + 2 not in st["chains"]:
            pend[g + 2] = chain_p01(ctx, g + 2, 0, 1, gates[g + 2])
        else:
            run_tails(g, "mid2")
        if g + 1 < NG:
            mix_k = ffn_mixes(ctx, g + 1, "k", last_k, st)
            if g + 1 == NG - 1:
                ctx.set_tok("lnfree", mix_k)
                ctx.set_tok("difffree", mix_k)
        pres = {}
        for dd, bank, tok in proj_mms(ctx, slots[2], w_toks[2], h["kact"],
                                      sq_t):
            vev = ctx.bank_read(bank, ctx.act(
                c3(h["krelu"][:, :])[:, dd, :], h[bank][:, 0:GT], AF.Copy,
                scale=0.5, wait=[tok, sq_t]))
            psl2 = slice((dd % 2) * GT, (dd % 2) * GT + GT)
            ptok = ctx.tt(h["stg"][:, psl2], c3(h["rg"][:, :])[:, dd, :],
                          c3(h["krelu"][:, :])[:, dd, :], ALU.mult,
                          wait=[vev] + ([sig[NCH - 1], prod_t]
                                        if dd == 0 else []) +
                               ([pres[dd - 2]] if dd >= 2 else []))
            pres[dd] = ctx.tt(c3(h["x"][:, :], T)[:, dd, gsl],
                              c3(h["x"][:, :], T)[:, dd, gsl],
                              h["stg"][:, psl2], ALU.add, wait=[ptok])
            last_v = tok
        res_toks[g] = pres[NCH - 1]
        prod_t = res_toks[g]
        if g + 2 in pend:
            st["chains"][g + 2] = chain_p2(ctx, pend.pop(g + 2))
        run_tails(g, "post")
        if out_dma:
            out_toks[g] = ctx.dma(
                "sync",
                out=h["out_hbm"][s, :, :, gsl].rearrange("c p t -> p c t"),
                in_=c3(h["x"][:, :], T)[:, :, gsl],
                wait=[res_toks[g]], sem="XO%d" % g)
            if out_xload is not None:
                ns = out_xload[0]
                xload_toks[g] = ctx.dma(
                    "sync",
                    out=c3(h["x"][:, :], T)[:, :, gsl],
                    in_=h["x_hbm"][ns, :, :, gsl].rearrange("c p t -> p c t"),
                    wait=[out_toks[g]], sem="XI%d" % g)
        run_tails(g, "post2")
    ctx.set_tok("rgfree", prod_t)
    ctx.set_tok("kactfree", last_v)
    return dict(slot_last=(last_r, last_k, last_v), res=res_toks,
                out=out_toks, xloads=xload_toks)



def emit_attn(ctx, s, slots, w_toks, gates, pre_chains=None, pre_applied=None,
              tail_emits=None):
    h = ctx.h
    expk_buf = h["rg"]
    v_buf = h["krelu"]
    kern3_buf = h["kact"]
    ksm_buf = h["ksm_g"]
    k3t_buf = h["k3t_ev"]
    lnb1 = (h["ln"], h["xk"])
    lnbufs = (h["ln"], h["ksm_g"])
    rg_free = ctx.tok("rgfree")

    chains = [None] * NG
    if pre_chains:
        chains[0] = pre_chains[0]
        c1 = pre_chains[1]
        chains[1] = c1 if "ev_a" in c1 else chain_p2(ctx, c1)
    else:
        chains[0] = emit_ln_chain(ctx, 0, 2, 3, gates[0])
        chains[1] = emit_ln_chain(ctx, 1, 2, 3, gates[1])
    applied = pre_applied or emit_apply(ctx, 0, chains[0],
                                        extra_wait=[ctx.tok("lnfree")])
    pend_a = None

    ksm_t = None
    kern3_t = None
    pool_t = ctx.tok("poolfree")
    ksm_dma = [None] * NG
    k3t_dma = [None] * NG
    tpose_done = [None] * NG
    vmm_by = [None] * NG
    last_kmm = None
    last_exp = last_vev = None
    pend_tpose = None
    ln3_t = None

    def do_tpose(gp, k3_tok, spill=True):
        last = None
        ev = None
        evs = []
        prev_spill = k3t_dma[gp - 1] if gp >= 1 else ctx.tok("k3evfree")
        for tl in range(4):
            half = "trA0" if tl % 2 == 0 else "trA1"
            hap = h["trA"][:, (tl % 2) * 1024:(tl % 2) * 1024 + 1024]
            tk = None
            for c in range(NCH):
                tk = ctx.tr(hap[:, c * 128:(c + 1) * 128],
                            c3(kern3_buf[:, :])[:, c, tl * 128:(tl + 1) * 128],
                            wait=([k3_tok, ctx.guard.get(half),
                                   ctx.pre_ident] if c == 0 else ()),
                            inc=((("T", 1),) if c == NCH - 1 else ()))
            ev = ctx.bank_read(half, ctx.act(
                k3t_buf[:, tl * 1024:(tl + 1) * 1024], hap, AF.Copy,
                wait=[tk, prev_spill]))
            evs.append(ev)
            last = tk
        dma = None
        if spill:
            dma = ctx.dma("sync",
                          out=h["k3t_scr"][gp * 4:(gp + 1) * 4, :, :]
                              .rearrange("t p d -> p t d"),
                          in_=k3t_buf[:, :].rearrange("p (t d) -> p t d", t=4),
                          wait=[ev, k3t_dma[gp - 1] if gp >= 1 else None],
                          sem="K3")
        return last, dma, evs

    def rebcast(g):
        par = g % 2
        psl = slice(par * GT, (par + 1) * GT)
        gsl = slice(g * GT, (g + 1) * GT)
        mm_a = ctx.mm(h["G2"][:, 0:GT], h["onesr_at"][h["ab_bp"][2]],
                      h["ab_r"][2][:, gsl], start=True, stop=True,
                      wait=[chains[g]["b_tok"], ctx.guard.get("G2")],
                      inc=(("T", 1),))
        mm_b = ctx.mm(h["G3"][:, 0:GT], h["onesr_at"][h["ab_bp"][3]],
                      h["ab_r"][3][:, gsl], start=True, stop=True,
                      wait=[chains[g]["b_tok"], ctx.guard.get("G3")],
                      inc=(("T", 1),))
        ev_a = ctx.bank_read("G2", ctx.act(h["ab_a"][:, psl], h["G2"][:, 0:GT],
                                           AF.Copy,
                                           wait=[mm_a, ctx.tok("aba%d" % par)]))
        ev_b = ctx.bank_read("G3", ctx.act(h["ab_b"][:, psl], h["G3"][:, 0:GT],
                                           AF.Copy,
                                           wait=[mm_b, ctx.tok("abb%d" % par)]))
        return dict(ev_a=ev_a, ev_b=ev_b, par=par)

    for g in range(NG):
        gsl = slice(g * GT, (g + 1) * GT)
        ln_t = applied
        exp = {}
        for dd, bank, tok in proj_mms(ctx, slots[1], w_toks[1], lnb1[g % 2],
                                      ln_t):
            exp[dd] = ctx.bank_read(bank, ctx.act(
                c3(expk_buf[:, :])[:, dd, :], h[bank][:, 0:GT], AF.Exp,
                bias=h["biask"][:, dd:dd + 1],
                wait=[tok, ksm_t, rg_free if g == 0 else None]))
            last_kmm = tok
        last_exp = exp[NCH - 1]
        if g + 2 < NG:
            pend_a = chain_p01(ctx, g + 2, 2, 3, gates[g + 2])
        vev = {}
        for dd, bank, tok in proj_mms(ctx, slots[2], w_toks[2], lnb1[g % 2],
                                      ln_t):
            vev[dd] = ctx.bank_read(bank, ctx.act(
                c3(v_buf[:, :])[:, dd, :], h[bank][:, 0:GT], AF.Identity,
                bias=h["bv"][:, dd:dd + 1],
                wait=[tok, kern3_t, rg_free if g == 0 else None]))
            vmm_by[g] = tok
        last_vev = vev[NCH - 1]
        sum_mm = ctx.mmacc("G2", h["G2"][0:1, 0:GT],
                           [h["ones_col"][:, :]] * NCH,
                           [c3(expk_buf[:, :])[:, c, :] for c in range(NCH)],
                           wait_first=[last_exp])
        ln_s = ctx.bank_read("G2", ctx.act(h["inv_f0"], h["G2"][0:1, 0:GT],
                                           AF.Ln, wait=[sum_mm]))
        inv_t = ctx.act(h["invb0"], h["inv_f0"], AF.Exp, scale=-1.0,
                        wait=[ln_s])
        invbc_mm = ctx.mm(h["G2"][:, 0:GT], h["onesr_at"][0], h["invb0"],
                          start=True, stop=True,
                          wait=[inv_t, ctx.guard.get("G2")], inc=(("T", 1),))
        invbc_ev = ctx.bank_read("G2", ctx.act(h["invb"][:, :],
                                               h["G2"][:, 0:GT], AF.Copy,
                                               wait=[invbc_mm, ksm_t]))
        if pend_a is not None:
            chains[pend_a["g"]] = chain_p2(ctx, pend_a)
            pend_a = None
        if g == NG - 1:
            ch0 = rebcast(0)
            ln3_t = emit_apply(ctx, 0, ch0, extra_wait=[vmm_by[NG - 2]],
                               dst=lnbufs[0])
        if g + 1 < NG:
            applied = emit_apply(ctx, g + 1, chains[g + 1],
                                 extra_wait=[vmm_by[g - 1] if g >= 1
                                             else ctx.tok("xkfree")],
                                 dst=lnb1[(g + 1) % 2])
        if pend_tpose is not None:
            gp, tokp = pend_tpose
            tpose_done[gp] = do_tpose(gp, tokp)
            k3t_dma[gp] = tpose_done[gp][1]
        ksm_t = ctx.tt(c3(ksm_buf[:, :]), c3(expk_buf[:, :]),
                       h["invb"][:, :].unsqueeze(1).broadcast_to([128, NCH, GT]),
                       ALU.mult,
                       wait=[invbc_ev, last_exp,
                             ksm_dma[g - 1] if g else ctx.tok("ksmfree")])
        kern3_t = ctx.tt(c3(kern3_buf[:, :]), c3(ksm_buf[:, :]),
                         c3(v_buf[:, :]), ALU.mult,
                         wait=[ksm_t, last_vev,
                               tpose_done[g - 1][0] if g
                               else ctx.tok("kactfree")])
        poolr = ctx.red(h["kland_f"][:, :].rearrange("p (c l) -> p c l", c=NCH),
                        ksm_buf[:, :].rearrange("p (c o i) -> p c o i",
                                                c=NCH, i=32),
                        ALU.add, wait=[ksm_t, pool_t])
        pool_t = ctx.vcopy(h["kland"][:, :].rearrange(
                               "p (c l) -> p c l", c=NCH)[:, :, g * 16:(g + 1) * 16],
                           h["kland_f"][:, :].rearrange("p (c l) -> p c l",
                                                        c=NCH),
                           wait=[poolr])
        ksm_dma[g] = ctx.dma("sync",
                             out=h["ksm_scr"][:, :, gsl]
                                 .rearrange("c p t -> p c t"),
                             in_=c3(ksm_buf[:, :]), wait=[ksm_t], sem="KS")
        pend_tpose = (g, kern3_t)
    ctx.set_tok("poolfree", None)
    ctx.set_tok("rgfree", None)

    expq_buf = h["rg"]
    qsm_buf = h["krelu"]
    k1_evs = [None] * NG
    k1tok_by = [None] * NG
    qmm_by = [None] * NG
    qsm_t = None
    rd_toks = [None] * NG
    mid_mm = None

    def emit_k1(gq, qsm_tok):
        gsl = slice(gq * GT, (gq + 1) * GT)
        k1tok_by[gq] = ctx.mmacc(
            "G3", h["G3"][0:NL, 0:GT],
            [h["kland"][:, :].rearrange("p (c l) -> p c l", c=NCH)[:, c, :]
             for c in range(NCH)],
            [c3(qsm_buf[:, :])[:, c, :] for c in range(NCH)],
            wait_first=[qsm_tok, pool_t])
        k1_evs[gq] = ctx.bank_read("G3", ctx.vcopy(h["kern1T"][0:NL, gsl],
                                                   h["G3"][0:NL, 0:GT],
                                                   wait=[k1tok_by[gq]]))

    def emit_rd1(q, extra):
        return ctx.dma("sync",
                       out=c3(h["xk"][:, :]),
                       in_=h["ksm_scr"][:, :, q * GT:(q + 1) * GT]
                           .rearrange("c p t -> p c t"),
                       wait=[ksm_dma[NG - 1]] + extra, sem="RK0")

    def emit_rd2(q, extra):
        return ctx.dma("sync",
                       out=h["kact"][:, :].rearrange("p (t d) -> p t d", t=4),
                       in_=h["k3t_scr"][q * 4:(q + 1) * 4, :, :]
                           .rearrange("t p d -> p t d"),
                       wait=[k3t_dma[q]] + extra, sem="RK1")

    nt_ctr = [0]

    def emit_chunk(q, rd1, rd2, xr_evs=None):
        nonlocal mid_mm
        for tl in range(4):
            nt = nt_ctr[0]
            nt_ctr[0] += 1
            bank = "G0" if tl % 2 == 0 else "G1"
            ktok = ctx.mmacc(bank, h[bank][:, 0:NL],
                             [c3(h["xk"][:, :])[:, c, tl * 128:(tl + 1) * 128]
                              for c in range(NCH)],
                             [h["kland"][:, :].rearrange("p (c l) -> p c l",
                                                         c=NCH)[:, c, :]
                              for c in range(NCH)],
                             wait_first=[rd1, rd2, pool_t])
            k2_ev = ctx.bank_read(bank, ctx.vcopy(
                h["kern2T"][:, (nt % 2) * NL:(nt % 2 + 1) * NL],
                h[bank][:, 0:NL], wait=[ktok]))
            rhs_buf = k3t_buf if xr_evs is not None else h["kact"]
            for hh in range(2):
                mid_mm = ctx.mm(
                    h["M"][:, hh * 512:(hh + 1) * 512],
                    h["kern2T"][:, (nt % 2) * NL:(nt % 2 + 1) * NL],
                    rhs_buf[:, tl * 1024 + hh * 512:tl * 1024 + (hh + 1) * 512],
                    start=(nt == 0), stop=(nt == 15),
                    wait=([k2_ev, ctx.guard.get("M")] +
                          ([xr_evs[tl]] if xr_evs is not None else [])
                          if hh == 0 else ()),
                    inc=(("T", 1),) if hh == 1 else ())

    ln3_next = None
    mid_ev = None
    for g in range(NG):
        if g == NG - 1:
            emit_chunk(g - 1, *rd_toks[g - 1])
            mid_ev = ctx.bank_read("M", ctx.act(h["mid"][:, :], h["M"][:, :],
                                                AF.Copy, scale=1.0 / 1024.0,
                                                wait=[mid_mm]))
        if g + 1 < NG:
            ch = rebcast(g + 1)
            ln3_next = emit_apply(ctx, g + 1, ch,
                                  extra_wait=[qmm_by[g - 1] if g >= 1
                                              else ksm_dma[NG - 1]],
                                  dst=lnbufs[(g + 1) % 2])
        exq = {}
        for dd, bank, tok in proj_mms(
                ctx, slots[0], w_toks[0], lnbufs[g % 2], ln3_t,
                extra_first=([last_exp, ksm_t] if g == 0 else ())):
            exq[dd] = ctx.bank_read(bank, ctx.act(
                c3(expq_buf[:, :])[:, dd, :], h[bank][:, 0:GT], AF.Exp,
                bias=h["biasq"][:, dd:dd + 1], wait=[tok, qsm_t]))
            qmm_by[g] = tok
        if g == 0:
            gp, tokp = pend_tpose
            tpose_done[gp] = do_tpose(gp, tokp, spill=False)
            rd1 = emit_rd1(3, [vmm_by[NG - 1]])
            emit_chunk(3, rd1, None, xr_evs=tpose_done[gp][2])
            rd_toks[0] = (emit_rd1(0, [mid_mm]),
                          emit_rd2(0, [tpose_done[gp][0]]))
        else:
            emit_k1(g - 1, qsm_t)
            if g < NG - 1:
                emit_chunk(g - 1, *rd_toks[g - 1])
                rd_toks[g] = (emit_rd1(g, [mid_mm]), emit_rd2(g, [mid_mm]))
        ln3_t = ln3_next
        sum_mm = ctx.mmacc("G2", h["G2"][0:1, 0:GT],
                           [h["ones_col"][:, :]] * NCH,
                           [c3(expq_buf[:, :])[:, c, :] for c in range(NCH)],
                           wait_first=[exq[NCH - 1]])
        ln_s = ctx.bank_read("G2", ctx.act(h["inv_f1"], h["G2"][0:1, 0:GT],
                                           AF.Ln, wait=[sum_mm]))
        inv3 = ctx.act(h["invb1"], h["inv_f1"], AF.Exp, scale=-1.0,
                       wait=[ln_s])
        invbc_mm = ctx.mm(h["G2"][:, 0:GT], h["onesr_at"][0], h["invb1"],
                          start=True, stop=True,
                          wait=[inv3, ctx.guard.get("G2")], inc=(("T", 1),))
        invbc_ev = ctx.bank_read("G2", ctx.act(
            h["invb"][:, :], h["G2"][:, 0:GT], AF.Copy,
            wait=[invbc_mm, qsm_t, ksm_t if g == 0 else None]))
        qsm_t = ctx.tt(c3(qsm_buf[:, :]), c3(expq_buf[:, :]),
                       h["invb"][:, :].unsqueeze(1).broadcast_to([128, NCH, GT]),
                       ALU.mult,
                       wait=[invbc_ev, exq[NCH - 1],
                             k1tok_by[g - 1] if g >= 1 else None,
                             kern3_t if g == 0 else None])
    ctx.set_tok("xrfree", mid_mm)
    ctx.set_tok("xkfree", mid_mm)
    ctx.set_tok("lnfree", qmm_by[2])
    ctx.set_tok("difffree", qmm_by[3])
    ctx.set_tok("rgfree", qsm_t)
    ctx.set_tok("kactfree", mid_mm)
    ctx.set_tok("ksmfree", None)
    ctx.set_tok("k3evfree", None)

    res_toks = [None] * NG

    def p3b_group(g):
        gsl2 = slice(g * GT, (g + 1) * GT)
        rtok = None
        for dd in range(NCH):
            bank = "G0" if dd % 2 == 0 else "G1"
            omm = ctx.mm(h[bank][:, 0:GT],
                         h["mid"][:, dd * 128:(dd + 1) * 128],
                         h["kern1T"][0:NL, gsl2], start=True, stop=True,
                         wait=[k1_evs[g], mid_ev, ctx.guard.get(bank)],
                         inc=(("T", 1),))
            rtok = ctx.bank_read(bank, ctx.tt(
                c3(h["x"][:, :], T)[:, dd, gsl2],
                c3(h["x"][:, :], T)[:, dd, gsl2],
                h[bank][:, 0:GT], ALU.add, wait=[omm]))
        res_toks[g] = rtok
        if tail_emits and g < len(tail_emits):
            for cb in tail_emits[g]:
                cb(res_toks)

    p3b_group(0)
    emit_k1(NG - 1, qsm_t)
    for g in range(1, NG):
        p3b_group(g)
    return dict(slot_last=(qmm_by[NG - 1], last_kmm, vmm_by[NG - 1]),
                res=res_toks)



def emit_preamble(ctx):
    h = ctx.h
    ones_t = ctx.op("vector", "memset", h["ones_col"][:, :], 1.0,
                    inc=(("V", 1),))
    ctx.op("vector", "memset", h["ones_row"][:, :], 1.0, inc=(("V", 1),))
    eps_t = ctx.op("vector", "memset", h["cst_eps"][:, :], EPS,
                   inc=(("V", 1),))
    id0 = ctx.op("gpsimd", "memset", h["ident"][:, :], 0.0, inc=(("G", 1),))
    id_t = ctx.op("gpsimd", "affine_select", out=h["ident"][:, :],
                  in_=h["ident"][:, :], compare_op=ALU.not_equal, fill=1.0,
                  base=0, pattern=[[-1, 128]], channel_multiplier=1,
                  wait=[id0], inc=(("G", 1),))
    ctx.pre_ident = id_t
    ctx.dma("gpsimd", out=h["vecs_sb"][:, :], in_=h["vecs_hbm"][:, :],
            sem="PV")
    bfl_t = None
    for i, dst in enumerate((h["bflat0"], h["bflat1"])):
        bfl_t = ctx.dma("gpsimd",
                        out=dst.rearrange("o (c p) -> o c p", c=8),
                        in_=h["vecs_hbm"][:, i * 8:(i + 1) * 8]
                            .rearrange("p (o c) -> o c p", o=1),
                        sem="PV", allow_slow_non_contiguous=True)
    vec_t = bfl_t
    wq_t = cast_weight_sbuf(ctx, "w0", "wq_hbm")
    wk_t = cast_weight_sbuf(ctx, "w1", "wk_hbm")
    spill_q = ctx.dma("sync", out=h["wq_bf"][:, :], in_=h["w0"][:, :],
                      wait=[wq_t], sem="W0")
    spill_k = ctx.dma("sync", out=h["wk_bf"][:, :], in_=h["w1"][:, :],
                      wait=[wk_t], sem="W1")
    c_wr = cast_weight_hbm(ctx, "wr_bf", "wr_hbm")
    c_wkf = cast_weight_hbm(ctx, "wkf_bf", "wkf_hbm", wait=[c_wr])
    c_wvf = cast_weight_hbm(ctx, "wvf_bf", "wvf_hbm", wait=[c_wkf])
    c_wv = cast_weight_hbm(ctx, "wv_bf", "wv_hbm", wait=[c_wvf])

    b_toks = {}
    last_sq = None
    for i, (slot, wt, brow) in enumerate((("w0", wq_t, 0), ("w1", wk_t, 1))):
        sq0 = ctx.act(h["xr"][:, :], h[slot][:, 0:4096], AF.Square,
                      wait=[wt, last_sq])
        sq1 = ctx.act(h["xk"][:, :], h[slot][:, 4096:8192], AF.Square,
                      wait=[wt])
        cn_mm = {}
        for dg in range(2):
            bank = "G2" if dg == 0 else "G3"
            lhs, rhs = [], []
            for c in range(NCH):
                buf = h["xr"] if c < 4 else h["xk"]
                lhs.append(h["ones_col"][:, :])
                rhs.append(buf[:, (c % 4) * 1024 + dg * 512:
                               (c % 4) * 1024 + dg * 512 + 512])
            cn_mm[dg] = ctx.mmacc(bank, h[bank][0:1, 0:512], lhs, rhs,
                                  wait_first=[sq0, sq1, ones_t])
        last_sq = cn_mm[1]
        ev0 = ctx.bank_read("G2", ctx.act(h["m_row"][:, 0:512],
                                          h["G2"][0:1, 0:512], AF.Copy,
                                          wait=[cn_mm[0]]))
        ev1 = ctx.bank_read("G3", ctx.act(h["m_row"][:, 512:1024],
                                          h["G3"][0:1, 0:512], AF.Copy,
                                          wait=[cn_mm[1]]))
        lnt = ctx.act(h["s2_row"][:, 0:1024], h["m_row"][:, 0:1024], AF.Ln,
                      bias=h["cst_eps"][0:1, 0:1], wait=[ev0, ev1, eps_t])
        cnt = ctx.act(h["m_row"][:, 0:1024], h["s2_row"][:, 0:1024], AF.Exp,
                      scale=0.5, wait=[lnt])
        m1 = ctx.red(h["stat_m2"][:, 0:1], h["m_row"][:, 0:1024], ALU.max,
                     wait=[cnt])
        m2 = ctx.red(h["stat_m2"][:, 1:2],
                     h["bflat0"] if brow == 0 else h["bflat1"],
                     ALU.max, apply_absolute_value=True, wait=[bfl_t])
        b1 = ctx.ts(h["stat_m2"][:, 2:3], h["stat_m2"][:, 0:1], 32.0,
                    ALU.mult, wait=[m1])
        b2 = ctx.tt(h["stat_m2"][:, 2:3], h["stat_m2"][:, 2:3],
                    h["stat_m2"][:, 1:2], ALU.add, wait=[b1, m2])
        b3 = ctx.ts(h["stat_m2"][:, 2:3], h["stat_m2"][:, 2:3], 2.0,
                    ALU.add, wait=[b2])
        bf_c = ctx.vcopy(h["stat_bf"][0:1, 0:1], h["stat_m2"][:, 2:3],
                         wait=[b3])
        bmm = ctx.mm(h["G2"][:, 0:1], h["onesr_at"][0],
                     h["stat_bf"][0:1, 0:1], start=True, stop=True,
                     wait=[bf_c, ctx.guard.get("G2")], inc=(("T", 1),))
        bev = ctx.bank_read("G2", ctx.act(h["Bcol"][:, i:i + 1],
                                          h["G2"][:, 0:1], AF.Copy,
                                          wait=[bmm]))
        boff = 0 if i == 0 else 8
        b_toks[i] = ctx.op("vector", "tensor_scalar",
                           h["biasqk"][:, i * 8:(i + 1) * 8],
                           h["vecs_sb"][:, boff:boff + 8],
                           h["Bcol"][:, i:i + 1], None, ALU.subtract,
                           wait=[bev, vec_t], inc=(("V", 1),))
    ctx.set_tok("m2", b_toks[1])
    ctx.set_tok("lnx", b_toks[1])
    ctx.set_tok("mrow", b_toks[1])
    ctx.set_tok("xrfree", last_sq)
    ctx.set_tok("xkfree", last_sq)
    return dict(bias_done=(b_toks[0], b_toks[1]),
                bcomp_last=(last_sq, b_toks[0], b_toks[1]),
                vecs=vec_t, spills=(spill_q, spill_k),
                casts=dict(wr=c_wr, wkf=c_wkf, wvf=c_wvf, wv=c_wv))



def program(ctx):
    h = ctx.h
    pre = emit_preamble(ctx)
    spill_q, spill_k = pre["spills"]
    casts = pre["casts"]

    wr_t = load_weight(ctx, "w0", "wr_bf",
                       wait=list(pre["bcomp_last"]) + [spill_q, casts["wr"]])
    wkf_t = load_weight(ctx, "w1", "wkf_bf",
                        wait=list(pre["bcomp_last"]) + [spill_k, casts["wkf"]])
    wvf_t = load_weight(ctx, "w2", "wvf_bf", wait=[casts["wvf"]])
    ffn_slots = ("w0", "w1", "w2")
    ffn_toks = (wr_t, wkf_t, wvf_t)

    x_loads = None
    f1st = None
    f2 = None
    for s in range(NSEQ):
        if s == 0:
            x_loads = [ctx.dma(
                "sync",
                out=c3(h["x"][:, :], T)[:, :, g * GT:(g + 1) * GT],
                in_=h["x_hbm"][s, :, :, g * GT:(g + 1) * GT]
                    .rearrange("c p t -> p c t"),
                sem="XI%d" % g) for g in range(NG)]
        gates1 = [[x_loads[g]] + (list(pre["bias_done"]) if s == 0 else [])
                  for g in range(NG)]
        attn_pre = {}
        ap = {}

        def a0_p01(res, xl):
            ap["c0"] = chain_p01(ctx, 0, 2, 3, [res[0]])

        def a0_p2(res, xl):
            attn_pre[0] = chain_p2(ctx, ap["c0"])

        def a1_p01(res, xl):
            attn_pre[1] = chain_p01(ctx, 1, 2, 3, [res[1]])

        def a_ap0(res, xl):
            attn_pre["ap0"] = emit_apply(ctx, 0, attn_pre[0],
                                         extra_wait=[ctx.tok("lnfree")])
        tails1 = {(2, "mid2"): [a0_p01],
                  (2, "post"): [a0_p2],
                  (3, "mid2"): [a1_p01, a_ap0]}
        f1 = emit_ffn(ctx, s, ffn_slots, ffn_toks, False, gates1, st=f1st,
                      tails=tails1)
        sl = f1["slot_last"]
        wq_t = load_weight(ctx, ffn_slots[0], "wq_bf", wait=[sl[0], spill_q])
        wka_t = load_weight(ctx, ffn_slots[1], "wk_bf", wait=[sl[1], spill_k])
        wva_t = load_weight(ctx, ffn_slots[2], "wv_bf",
                            wait=[sl[2], casts["wv"]])
        gates2 = [[f1["res"][g]] for g in range(NG)]
        f2st = new_ffn_state()
        f2p = {}

        def f20_p01(res):
            f2p["c0"] = chain_p01(ctx, 0, 0, 1, [res[0]], cast_eng="scalar")

        def f20_p2(res):
            f2st["chains"][0] = chain_p2(ctx, f2p["c0"])

        def f21_p01(res):
            f2st["pend1"] = chain_p01(ctx, 1, 0, 1, [res[1]],
                                      cast_eng="scalar")

        def f2_prep(res):
            ffn_prep(ctx, 0, f2st, extra_wait=[ctx.tok("lnfree")])

        def f2_mix(res):
            f2st["mix_r"] = ffn_mixes(ctx, 0, "r", ctx.tok("xrfree"), f2st)
            f2st["mix_k"] = ffn_mixes(ctx, 0, "k", ctx.tok("xkfree"), f2st)
        tails3b = [[f20_p01], [f20_p2, f2_prep, f21_p01], [f2_mix], []]
        a1 = emit_attn(ctx, s, ffn_slots, (wq_t, wka_t, wva_t), gates2,
                       pre_chains=attn_pre, pre_applied=attn_pre["ap0"],
                       tail_emits=tails3b)
        asl = a1["slot_last"]
        new_slots = (ffn_slots[1], ffn_slots[2], ffn_slots[0])
        wr_t = load_weight(ctx, new_slots[0], "wr_bf",
                           wait=[asl[1], casts["wr"]])
        wkf_t = load_weight(ctx, new_slots[1], "wkf_bf",
                            wait=[asl[2], casts["wkf"]])
        wvf_t = load_weight(ctx, new_slots[2], "wvf_bf",
                            wait=[asl[0], casts["wvf"]])
        ffn_slots = new_slots
        ffn_toks = (wr_t, wkf_t, wvf_t)
        gates3 = [[a1["res"][g]] for g in range(NG)]
        if s + 1 < NSEQ:
            f1st = new_ffn_state()
            nst = f1st
            np_ = {}

            def n0_p01(res, xl):
                np_["c0"] = chain_p01(ctx, 0, 0, 1, [xl[0]])

            def n0_p2(res, xl):
                nst["chains"][0] = chain_p2(ctx, np_["c0"])

            def n_prep(res, xl):
                ffn_prep(ctx, 0, nst, extra_wait=[ctx.tok("lnfree")])

            def n1_p01(res, xl):
                nst["pend1"] = chain_p01(ctx, 1, 0, 1, [xl[1]])

            def n_mix(res, xl):
                nst["mix_r"] = ffn_mixes(ctx, 0, "r", ctx.tok("xrfree"), nst)
                nst["mix_k"] = ffn_mixes(ctx, 0, "k", ctx.tok("xkfree"), nst)
            tails2 = {(2, "mid2"): [n0_p01],
                      (2, "post"): [n0_p2],
                      (3, "mid"): [n_prep],
                      (3, "mid2"): [n1_p01, n_mix]}
            oxl = (s + 1,)
        else:
            f1st = None
            tails2 = None
            oxl = None
        f2 = emit_ffn(ctx, s, ffn_slots, ffn_toks, True, gates3, st=f2st,
                      tails=tails2, out_xload=oxl)
        x_loads = f2["xloads"]

    ctx.op("sync", "nop", wait=list(f2["out"]))



def build_nc():
    import contextlib
    nc = bass.Bass("TRN2", target_bir_lowering=False, debug=False,
                   detect_race_conditions=False)
    h = {}
    h["x_hbm"] = nc.dram_tensor("x_in", [NSEQ, NCH, 128, T], F32,
                                kind="ExternalInput").ap()
    h["out_hbm"] = nc.dram_tensor("out", [NSEQ, NCH, 128, T], F32,
                                  kind="ExternalOutput").ap()
    for nm in ("wq", "wk", "wv", "wr", "wkf", "wvf"):
        h[nm + "_hbm"] = nc.dram_tensor(nm, [NCH, 128, 1024], F32,
                                        kind="ExternalInput").ap()
    h["vecs_hbm"] = nc.dram_tensor("vecs", [128, 40], F32,
                                   kind="ExternalInput").ap()
    h["ksm_scr"] = nc.dram_tensor("ksm_scr", [NCH, 128, T], BF16,
                                  kind="Internal").ap()
    h["k3t_scr"] = nc.dram_tensor("k3t_scr", [16, 128, 1024], BF16,
                                  kind="Internal").ap()
    for nm in ("wq", "wk", "wv", "wr", "wkf", "wvf"):
        h[nm + "_bf"] = nc.dram_tensor(nm + "_bf", [128, NCH * 1024], BF16,
                                       kind="Internal").ap()

    stack = contextlib.ExitStack()

    def sb(name, shape, dtype):
        return stack.enter_context(nc.sbuf_tensor(name, shape, dtype))

    def ps(name, shape, dtype):
        return stack.enter_context(nc.psum_tensor(name, shape, dtype))

    h["x"] = sb("x", [128, NCH * T], F32)
    for w in ("w0", "w1", "w2"):
        h[w] = sb(w, [128, NCH * 1024], BF16)
    for nm in ("ln", "diff", "xr", "xk", "rg", "krelu", "kact"):
        h[nm] = sb(nm, [128, NCH * GT], BF16)
    h["ksm_g"] = h["diff"]
    h["k3t_ev"] = h["xr"]
    h["castb"] = sb("castb", [128, 2 * GT], BF16)
    h["sqb"] = sb("sqb", [128, 2 * GT], BF16)
    h["stg"] = sb("stg", [128, 2 * GT], BF16)
    h["ab_a"] = sb("ab_a", [128, 2 * GT], BF16)
    h["ab_b"] = sb("ab_b", [128, 2 * GT], BF16)
    h["invb"] = sb("invb", [128, GT], BF16)
    h["sums"] = sb("sums", [33, T], F32)
    h["ab"] = sb("ab", [65, 2 * T], BF16)
    h["stat_tmp"] = sb("stat_tmp", [33, GT], F32)
    h["stat_bf"] = sb("stat_bf", [1, 4], BF16)
    h["inv_bb"] = sb("inv_bb", [1, GT], BF16)
    h["kern2T"] = sb("kern2T", [128, 2 * NL], BF16)
    h["mid"] = sb("mid", [NL, 1024], BF16)
    h["kern1T"] = sb("kern1T", [NL, T], BF16)
    h["kland"] = sb("kland", [128, NCH * NL], BF16)
    h["kland_f"] = sb("kland_f", [128, NCH * 16], F32)
    h["vecs_sb"] = sb("vecs_sb", [128, 40], F32)
    h["biasqk"] = sb("biasqk", [128, 16], F32)
    h["Bcol"] = sb("Bcol", [128, 2], F32)
    h["ones_col"] = sb("ones_col", [128, 1], BF16)
    h["ones_row"] = sb("ones_row", [65, 128], BF16)
    h["ident"] = sb("ident", [128, 128], BF16)
    h["lnprev"] = sb("lnprev", [128, 2 * NCH], BF16)
    h["cst_eps"] = sb("cst_eps", [1, 1], F32)

    h["G0"] = ps("G0", [128, 512], F32)
    h["G1"] = ps("G1", [128, 512], F32)
    h["G2"] = ps("G2", [128, 512], F32)
    h["G3"] = ps("G3", [128, 512], F32)
    h["M"] = ps("M", [NL, 1024], F32)
    h["trA"] = ps("trA", [128, 2048], BF16)

    h["m_row"] = h["sums"][0:1, 0:T]
    h["s2_row"] = h["sums"][32:33, 0:T]
    h["ab_r"] = [h["ab"][0:1, 0:T], h["ab"][32:33, 0:T],
                 h["ab"][0:1, T:2 * T], h["ab"][64:65, 0:T]]
    h["ab_bp"] = [0, 32, 0, 64]
    h["bflat0"] = h["sums"][0:1, 1024:2048]
    h["bflat1"] = h["sums"][32:33, 1024:2048]
    h["invb0"] = h["inv_bb"][0:1, 0:GT]
    h["invb1"] = h["inv_bb"][0:1, 0:GT]
    h["inv_f0"] = h["stat_tmp"][0:1, 0:GT]
    h["inv_f1"] = h["stat_tmp"][0:1, 0:GT]
    h["stat_m2"] = h["stat_tmp"][32:33, 0:GT]
    h["stat_ln"] = h["stat_tmp"][0:1, 0:GT]
    h["onesr_at"] = {0: h["ones_row"][0:1, :], 32: h["ones_row"][32:33, :],
                     64: h["ones_row"][64:65, :]}
    h["mur"] = h["vecs_sb"][:, 24:32]
    h["muk"] = h["vecs_sb"][:, 32:40]
    h["bv"] = h["vecs_sb"][:, 16:24]
    h["biasq"] = h["biasqk"][:, 0:8]
    h["biask"] = h["biasqk"][:, 8:16]

    sem_names = ("T", "V", "S", "G", "W0", "W1", "W2", "PV", "WC",
                 "XI0", "XI1", "XI2", "XI3", "XO0", "XO1", "XO2", "XO3",
                 "KS", "K3", "RK0", "RK1")
    sems = {nm: stack.enter_context(nc.semaphore("sem_" + nm))
            for nm in sem_names}

    with nc.Block() as block:
        def run(engine):
            def f(_eng):
                local = {nm: [sems[nm], 0] for nm in sem_names}
                ctx = Ctx(nc, engine, local, h)
                program(ctx)
            return f
        block.sync(run("sync"))
        block.gpsimd(run("gpsimd"))
        block.tensor(run("tensor"))
        block.scalar(run("scalar"))
        block.vector(run("vector"))

    stack.close()
    return nc



_NC_CACHE = {}


def kernel(**inputs):
    x = np.asarray(inputs["L0_attn_hidden_states"], dtype=np.float32)
    B, TT_, D = x.shape
    n_cores = 8
    per = B // n_cores

    xfm = np.ascontiguousarray(
        x.reshape(B, TT_, NCH, 128).transpose(0, 2, 3, 1))

    def warr(w):
        return np.ascontiguousarray(
            np.asarray(w, dtype=np.float32).reshape(NCH, 128, 1024))

    wq = warr(inputs["L0_attn_Wq"])
    wk = warr(inputs["L0_attn_Wk"])
    wv = warr(inputs["L0_attn_Wv"])
    wr = warr(np.asarray(inputs["L0_ffn_W_r"], dtype=np.float32).T)
    wkf = warr(np.asarray(inputs["L0_ffn_W_k"], dtype=np.float32).T)
    wvf = warr(np.asarray(inputs["L0_ffn_W_v"], dtype=np.float32).T)

    def pcol(v):
        return np.ascontiguousarray(
            np.asarray(v, dtype=np.float32).reshape(NCH, 128).T)

    vecs = np.ascontiguousarray(np.concatenate(
        [pcol(inputs["L0_attn_bq"]), pcol(inputs["L0_attn_bk"]),
         pcol(inputs["L0_attn_bv"]), pcol(inputs["L0_ffn_mu_r"]),
         pcol(inputs["L0_ffn_mu_k"])], axis=1))

    if "nc" not in _NC_CACHE:
        _NC_CACHE["nc"] = build_nc()
    nc = _NC_CACHE["nc"]

    in_maps = []
    for c in range(n_cores):
        in_maps.append({
            "x_in": np.ascontiguousarray(xfm[c * per:(c + 1) * per]),
            "wq": wq, "wk": wk, "wv": wv,
            "wr": wr, "wkf": wkf, "wvf": wvf,
            "vecs": vecs,
        })
    res = run_bass_kernel_spmd(nc, in_maps, core_ids=list(range(n_cores)))
    outs = [res.results[c]["out"] for c in range(n_cores)]
    full = np.concatenate(outs, axis=0).reshape(B, NCH, 128, T)
    out = np.ascontiguousarray(full.transpose(0, 3, 1, 2).reshape(B, TT_, D))
    return out


if __name__ == "__main__":
    build_nc()
    print("build OK")


# version 25 (final)
# speedup vs baseline: 1.0832x; 1.0018x over previous
import numpy as np
import concourse.bass as bass
import concourse.mybir as mybir
from concourse.bass_utils import run_bass_kernel_spmd

F32 = mybir.dt.float32
BF16 = mybir.dt.bfloat16
AF = mybir.ActivationFunctionType
ALU = mybir.AluOpType
AX = mybir.AxisListType

T = 2048
NSEQ = 4
NCH = 8
NG = 4
GT = 512
NL = 64
EPS = 1e-5


class Ctx:

    def __init__(self, nc, active, sems, h):
        self.nc = nc
        self.active = active
        self.sems = sems
        self.h = h
        self.guard = {}
        self.pre_ident = None
        self._toks = {}

    def tok(self, name):
        return self._toks.get(name)

    def set_tok(self, name, tok):
        self._toks[name] = tok

    def op(self, engine, fn, *args, wait=(), inc=(), **kwargs):
        if self.active == engine:
            eng = getattr(self.nc, engine)
            wmax = {}
            for tk in wait:
                if tk is None:
                    continue
                nm, v = tk
                wmax[nm] = max(wmax.get(nm, 0), v)
            for nm, v in wmax.items():
                eng.wait_ge(self.sems[nm][0], v)
            instr = getattr(eng, fn)(*args, **kwargs)
            for nm, k in inc:
                instr.then_inc(self.sems[nm][0], k)
        toks = []
        for nm, k in inc:
            self.sems[nm][1] += k
            toks.append((nm, self.sems[nm][1]))
        return toks[0] if len(toks) == 1 else (toks if toks else None)

    def mm(self, out, lhsT, rhs, start, stop, wait=(), inc=()):
        return self.op("tensor", "matmul", out, lhsT, rhs,
                       start=start, stop=stop, wait=wait, inc=inc)

    def mmacc(self, bank, out, lhsT_list, rhs_list, wait_first=()):
        n = len(lhsT_list)
        tok = None
        for i in range(n):
            w = (list(wait_first) + [self.guard.get(bank)]) if i == 0 else ()
            tok = self.mm(out, lhsT_list[i], rhs_list[i],
                          start=(i == 0), stop=(i == n - 1),
                          wait=w, inc=((("T", 1),) if i == n - 1 else ()))
        return tok

    def bank_read(self, bank, tok):
        self.guard[bank] = tok
        return tok

    def tr(self, out, in_, wait=(), inc=()):
        return self.op("tensor", "transpose", out, in_, self.h["ident"][:, :],
                       wait=wait, inc=inc)

    def act(self, out, in_, func, wait=(), inc=(("S", 1),), **kw):
        return self.op("scalar", "activation", out, in_, func,
                       wait=wait, inc=inc, **kw)

    def vcopy(self, out, in_, wait=(), inc=(("V", 1),)):
        return self.op("vector", "tensor_copy", out, in_, wait=wait, inc=inc)

    def tt(self, out, in0, in1, alu, wait=(), inc=(("V", 1),)):
        return self.op("vector", "tensor_tensor", out, in0, in1, alu,
                       wait=wait, inc=inc)

    def ts(self, out, in0, scalar, alu, wait=(), inc=(("V", 1),)):
        return self.op("vector", "tensor_single_scalar", out, in0, scalar, alu,
                       wait=wait, inc=inc)

    def stt(self, out, in0, scalar, in1, op0, op1, wait=(), inc=(("V", 1),)):
        return self.op("vector", "scalar_tensor_tensor", out, in0, scalar, in1,
                       op0, op1, wait=wait, inc=inc)

    def red(self, out, in_, alu, wait=(), inc=(("V", 1),), **kw):
        return self.op("vector", "tensor_reduce", out, in_, AX.X, alu,
                       wait=wait, inc=inc, **kw)

    def gcopy(self, out, in_, wait=(), inc=(("G", 1),)):
        return self.op("gpsimd", "tensor_copy", out, in_, wait=wait, inc=inc)

    def gtt(self, out, in0, in1, alu, wait=(), inc=(("G", 1),)):
        return self.op("gpsimd", "tensor_tensor", out, in0, in1, alu,
                       wait=wait, inc=inc)

    def gstt(self, out, in0, scalar, in1, op0, op1, wait=(), inc=(("G", 1),)):
        return self.op("gpsimd", "scalar_tensor_tensor", out, in0, scalar, in1,
                       op0, op1, wait=wait, inc=inc)

    def dma(self, queue, out, in_, wait=(), sem="XO", **kw):
        return self.op(queue, "dma_start", out=out, in_=in_,
                       wait=wait, inc=((sem, 16),), **kw)


def c3(ap2d, n=GT):
    return ap2d.rearrange("p (c n) -> p c n", c=NCH)



def chain_p01(ctx, g, row_a, row_b, gates, cast_eng="vector"):
    h = ctx.h
    gsl = slice(g * GT, (g + 1) * GT)
    st = dict(g=g, row_a=row_a, row_b=row_b)
    s1_tok = s2_tok = None
    for c in range(NCH):
        cp = slice((c % 2) * GT, (c % 2 + 1) * GT)
        cwait = list(gates) + [ctx.tok("cs%d" % (c % 2)),
                               ctx.tok("csq%d" % (c % 2))]
        if cast_eng == "vector":
            cast_t = ctx.vcopy(h["castb"][:, cp],
                               c3(h["x"][:, :], T)[:, c, gsl], wait=cwait)
        else:
            cast_t = ctx.act(h["castb"][:, cp],
                             c3(h["x"][:, :], T)[:, c, gsl], AF.Copy,
                             wait=cwait)
        sq_t = ctx.act(h["sqb"][:, cp], h["castb"][:, cp], AF.Square,
                       wait=[cast_t, ctx.tok("sq%d" % (c % 2))])
        ctx.set_tok("csq%d" % (c % 2), sq_t)
        s1_tok = ctx.mm(h["G2"][0:1, 0:GT], h["ones_col"][:, :],
                        h["castb"][:, cp], start=(c == 0), stop=(c == NCH - 1),
                        wait=[cast_t] +
                             ([ctx.guard.get("G2")] if c == 0 else []),
                        inc=(("T", 1),))
        ctx.set_tok("cs%d" % (c % 2), s1_tok)
        s2_tok = ctx.mm(h["G3"][0:1, 0:GT], h["ones_col"][:, :],
                        h["sqb"][:, cp], start=(c == 0), stop=(c == NCH - 1),
                        wait=[sq_t] +
                             ([ctx.guard.get("G3")] if c == 0 else []),
                        inc=(("T", 1),))
        ctx.set_tok("sq%d" % (c % 2), s2_tok)
    ev1 = ctx.bank_read("G2", ctx.act(h["m_row"][:, gsl], h["G2"][0:1, 0:GT],
                                      AF.Copy, scale=1.0 / 1024.0,
                                      wait=[s1_tok, ctx.tok("mrow")]))
    ev2 = ctx.bank_read("G3", ctx.act(h["s2_row"][:, gsl], h["G3"][0:1, 0:GT],
                                      AF.Copy, scale=1.0 / 1024.0,
                                      wait=[s2_tok]))
    m = h["m_row"][:, gsl]
    s2m = h["s2_row"][:, gsl]
    t_m2 = ctx.tt(h["stat_m2"], m, m, ALU.mult, wait=[ev1, ctx.tok("m2")])
    t_var = ctx.tt(s2m, s2m, h["stat_m2"], ALU.subtract, wait=[ev2, t_m2])
    ctx.set_tok("m2", t_var)
    t_ln = ctx.act(h["stat_ln"], s2m, AF.Ln, bias=h["cst_eps"][0:1, 0:1],
                   wait=[t_var, ctx.tok("lnx")])
    t_a = ctx.act(h["ab_r"][row_a][:, gsl], h["stat_ln"], AF.Exp, scale=-0.5,
                  wait=[t_ln])
    ctx.set_tok("lnx", t_a)
    t_b = ctx.tt(h["ab_r"][row_b][:, gsl], m, h["ab_r"][row_a][:, gsl],
                 ALU.mult, wait=[t_a])
    ctx.set_tok("mrow", t_b)
    st["t_a"] = t_a
    st["t_b"] = t_b
    return st


def chain_p2(ctx, st):
    h = ctx.h
    g, row_a, row_b = st["g"], st["row_a"], st["row_b"]
    gsl = slice(g * GT, (g + 1) * GT)
    par = g % 2
    psl = slice(par * GT, (par + 1) * GT)
    mm_a = ctx.mm(h["G2"][:, 0:GT], h["onesr_at"][h["ab_bp"][row_a]],
                  h["ab_r"][row_a][:, gsl], start=True, stop=True,
                  wait=[st["t_a"], ctx.guard.get("G2")], inc=(("T", 1),))
    mm_b = ctx.mm(h["G3"][:, 0:GT], h["onesr_at"][h["ab_bp"][row_b]],
                  h["ab_r"][row_b][:, gsl], start=True, stop=True,
                  wait=[st["t_b"], ctx.guard.get("G3")], inc=(("T", 1),))
    ev_a = ctx.bank_read("G2", ctx.act(h["ab_a"][:, psl], h["G2"][:, 0:GT],
                                       AF.Copy,
                                       wait=[mm_a, ctx.tok("aba%d" % par)]))
    ev_b = ctx.bank_read("G3", ctx.act(h["ab_b"][:, psl], h["G3"][:, 0:GT],
                                       AF.Copy,
                                       wait=[mm_b, ctx.tok("abb%d" % par)]))
    return dict(ev_a=ev_a, ev_b=ev_b, par=par, b_tok=st["t_b"])


def emit_ln_chain(ctx, g, row_a, row_b, gates):
    return chain_p2(ctx, chain_p01(ctx, g, row_a, row_b, gates))


def emit_apply(ctx, g, chain, extra_wait=(), dst=None):
    h = ctx.h
    dst = dst if dst is not None else h["ln"]
    gsl = slice(g * GT, (g + 1) * GT)
    par = chain["par"]
    psl = slice(par * GT, (par + 1) * GT)
    d3 = c3(dst[:, :])
    t1 = ctx.tt(d3, c3(h["x"][:, :], T)[:, :, gsl],
                h["ab_a"][:, psl].unsqueeze(1).broadcast_to([128, NCH, GT]),
                ALU.mult, wait=[chain["ev_a"]] + list(extra_wait))
    tok = ctx.tt(d3, d3,
                 h["ab_b"][:, psl].unsqueeze(1).broadcast_to([128, NCH, GT]),
                 ALU.subtract, wait=[chain["ev_b"], t1])
    ctx.set_tok("aba%d" % par, tok)
    ctx.set_tok("abb%d" % par, tok)
    return tok


def load_weight(ctx, slot, bf_name, wait=()):
    h = ctx.h
    return ctx.dma("sync", out=h[slot][:, :], in_=h[bf_name][:, :],
                   wait=wait, sem="W" + slot[-1])


def cast_weight_sbuf(ctx, slot, hbm_name, wait=()):
    h = ctx.h
    return ctx.dma("gpsimd",
                   out=h[slot][:, :].rearrange("p (c d) -> p c d", c=NCH),
                   in_=h[hbm_name][:, :, :].rearrange("c p d -> p c d"),
                   wait=wait, sem="WC")


def cast_weight_hbm(ctx, bf_name, hbm_name, wait=()):
    h = ctx.h
    return ctx.dma("gpsimd",
                   out=h[bf_name][:, :].rearrange("p (c d) -> p c d", c=NCH),
                   in_=h[hbm_name][:, :, :].rearrange("c p d -> p c d"),
                   wait=wait, sem="WC")


def wtile(ctx, slot, c, dd):
    return ctx.h[slot][:, c * 1024 + dd * 128: c * 1024 + (dd + 1) * 128]


def proj_mms(ctx, slot, w_tok, rhs_buf, rhs_tok, extra_first=()):
    h = ctx.h
    for dd in range(NCH):
        bank = "G0" if dd % 2 == 0 else "G1"
        tok = ctx.mmacc(bank, h[bank][:, 0:GT],
                        [wtile(ctx, slot, c, dd) for c in range(NCH)],
                        [c3(rhs_buf[:, :])[:, c, :] for c in range(NCH)],
                        wait_first=[rhs_tok, w_tok] + list(extra_first))
        yield dd, bank, tok



def new_ffn_state():
    return dict(chains={}, applied={}, diff={}, lnprev={}, mix_r=None,
                mix_k=None)


def ffn_prep(ctx, g, st, extra_wait=()):
    h = ctx.h
    if -1 not in st["lnprev"]:
        st["lnprev"][-1] = ctx.op("vector", "memset",
                                  h["lnprev"][:, NCH:2 * NCH], 0.0,
                                  inc=(("V", 1),))
    st["applied"][g] = emit_apply(ctx, g, st["chains"][g],
                                  extra_wait=list(extra_wait))
    lp_w = slice((g % 2) * NCH, (g % 2) * NCH + NCH)
    st["lnprev"][g] = ctx.vcopy(h["lnprev"][:, lp_w].unsqueeze(2),
                                c3(h["ln"][:, :])[:, :, GT - 1:GT],
                                wait=[st["applied"][g]])
    d1 = ctx.tt(c3(h["diff"][:, :])[:, :, 1:GT],
                c3(h["ln"][:, :])[:, :, 1:GT],
                c3(h["ln"][:, :])[:, :, 0:GT - 1], ALU.subtract,
                wait=[st["applied"][g], ctx.tok("difffree")])
    lp_r = slice(((g - 1) % 2) * NCH, ((g - 1) % 2) * NCH + NCH)
    st["diff"][g] = ctx.tt(c3(h["diff"][:, :])[:, :, 0:1],
                           c3(h["ln"][:, :])[:, :, 0:1],
                           h["lnprev"][:, lp_r].unsqueeze(2), ALU.subtract,
                           wait=[d1, st["lnprev"][g - 1]])


def ffn_mixes(ctx, g, which, guard_tok, st):
    h = ctx.h
    mu = h["mur"] if which == "r" else h["muk"]
    dst = h["xr"] if which == "r" else h["xk"]
    lp_r = slice(((g - 1) % 2) * NCH, ((g - 1) % 2) * NCH + NCH)
    tok = None
    for c in range(NCH):
        ctx.stt(c3(dst[:, :])[:, c, 1:GT],
                c3(h["diff"][:, :])[:, c, 1:GT], mu[:, c:c + 1],
                c3(h["ln"][:, :])[:, c, 0:GT - 1], ALU.mult, ALU.add,
                wait=[st["diff"][g], guard_tok] if c == 0 else ())
        tok = ctx.stt(c3(dst[:, :])[:, c, 0:1],
                      c3(h["diff"][:, :])[:, c, 0:1], mu[:, c:c + 1],
                      h["lnprev"][:, lp_r][:, c:c + 1].unsqueeze(2),
                      ALU.mult, ALU.add, wait=())
    return tok


def emit_ffn(ctx, s, slots, w_toks, out_dma, gates, st=None, tails=None,
             out_xload=None):
    h = ctx.h
    st = st or new_ffn_state()
    tails = tails or {}
    pend = {}

    def chain_full(g):
        if g not in st["chains"]:
            st["chains"][g] = emit_ln_chain(ctx, g, 0, 1, gates[g])
    chain_full(0)
    if "pend1" in st:
        st["chains"][1] = chain_p2(ctx, st.pop("pend1"))
    chain_full(1)
    if 0 not in st["applied"]:
        ffn_prep(ctx, 0, st, extra_wait=[ctx.tok("lnfree")])
    mix_r = st["mix_r"] or ffn_mixes(ctx, 0, "r", ctx.tok("xrfree"), st)
    mix_k = st["mix_k"] or ffn_mixes(ctx, 0, "k", ctx.tok("xkfree"), st)
    res_toks = [None] * NG
    out_toks = [None] * NG
    xload_toks = [None] * NG
    prod_t = ctx.tok("rgfree")
    kact_free = ctx.tok("kactfree")
    last_r = last_k = last_v = None

    def run_tails(g, point):
        for cb in tails.get((g, point), ()):
            cb(res_toks, xload_toks)

    for g in range(NG):
        gsl = slice(g * GT, (g + 1) * GT)
        if g + 1 < NG:
            ffn_prep(ctx, g + 1, st, extra_wait=[mix_k])
        sig = {}
        for dd, bank, tok in proj_mms(ctx, slots[0], w_toks[0], h["xr"],
                                      mix_r):
            sig[dd] = ctx.bank_read(bank, ctx.act(
                c3(h["rg"][:, :])[:, dd, :], h[bank][:, 0:GT], AF.Sigmoid,
                wait=[tok, prod_t]))
            last_r = tok
        if g == NG - 1:
            ctx.set_tok("xrfree", last_r)
        if g + 1 < NG:
            mix_r = ffn_mixes(ctx, g + 1, "r", last_r, st)
        run_tails(g, "mid")
        sq_t = None
        for dd, bank, tok in proj_mms(ctx, slots[1], w_toks[1], h["xk"],
                                      mix_k):
            rtok = ctx.bank_read(bank, ctx.act(
                c3(h["krelu"][:, :])[:, dd, :], h[bank][:, 0:GT], AF.Relu,
                wait=[tok, prod_t]))
            sq_t = ctx.act(c3(h["kact"][:, :])[:, dd, :],
                           c3(h["krelu"][:, :])[:, dd, :], AF.Square,
                           wait=[rtok, kact_free if dd == 0 else None,
                                 last_v if dd == 0 else None])
            last_k = tok
        kact_free = None
        if g == NG - 1:
            ctx.set_tok("xkfree", last_k)
        if g + 2 < NG and g + 2 not in st["chains"]:
            pend[g + 2] = chain_p01(ctx, g + 2, 0, 1, gates[g + 2])
        else:
            run_tails(g, "mid2")
        if g + 1 < NG:
            mix_k = ffn_mixes(ctx, g + 1, "k", last_k, st)
            if g + 1 == NG - 1:
                ctx.set_tok("lnfree", mix_k)
                ctx.set_tok("difffree", mix_k)
        pres = {}
        for dd, bank, tok in proj_mms(ctx, slots[2], w_toks[2], h["kact"],
                                      sq_t):
            vev = ctx.bank_read(bank, ctx.act(
                c3(h["krelu"][:, :])[:, dd, :], h[bank][:, 0:GT], AF.Copy,
                scale=0.5, wait=[tok, sq_t]))
            psl2 = slice((dd % 2) * GT, (dd % 2) * GT + GT)
            ptok = ctx.tt(h["stg"][:, psl2], c3(h["rg"][:, :])[:, dd, :],
                          c3(h["krelu"][:, :])[:, dd, :], ALU.mult,
                          wait=[vev] + ([sig[NCH - 1], prod_t]
                                        if dd == 0 else []) +
                               ([pres[dd - 2]] if dd >= 2 else []))
            pres[dd] = ctx.tt(c3(h["x"][:, :], T)[:, dd, gsl],
                              c3(h["x"][:, :], T)[:, dd, gsl],
                              h["stg"][:, psl2], ALU.add, wait=[ptok])
            last_v = tok
        res_toks[g] = pres[NCH - 1]
        prod_t = res_toks[g]
        if g + 2 in pend:
            st["chains"][g + 2] = chain_p2(ctx, pend.pop(g + 2))
        run_tails(g, "post")
        if out_dma:
            out_toks[g] = ctx.dma(
                "sync",
                out=h["out_hbm"][s, :, :, gsl].rearrange("c p t -> p c t"),
                in_=c3(h["x"][:, :], T)[:, :, gsl],
                wait=[res_toks[g]], sem="XO%d" % g)
            if out_xload is not None:
                ns = out_xload[0]
                xload_toks[g] = ctx.dma(
                    "sync",
                    out=c3(h["x"][:, :], T)[:, :, gsl],
                    in_=h["x_hbm"][ns, :, :, gsl].rearrange("c p t -> p c t"),
                    wait=[out_toks[g]], sem="XI%d" % g)
        run_tails(g, "post2")
    ctx.set_tok("rgfree", prod_t)
    ctx.set_tok("kactfree", last_v)
    return dict(slot_last=(last_r, last_k, last_v), res=res_toks,
                out=out_toks, xloads=xload_toks)



def emit_attn(ctx, s, slots, w_toks, gates, pre_chains=None, pre_applied=None,
              tail_emits=None):
    h = ctx.h
    expk_buf = h["rg"]
    v_buf = h["krelu"]
    kern3_buf = h["kact"]
    ksm_buf = h["ksm_g"]
    k3t_buf = h["k3t_ev"]
    lnb1 = (h["ln"], h["xk"])
    lnbufs = (h["ln"], h["ksm_g"])
    rg_free = ctx.tok("rgfree")

    chains = [None] * NG
    if pre_chains:
        chains[0] = pre_chains[0]
        c1 = pre_chains[1]
        chains[1] = c1 if "ev_a" in c1 else chain_p2(ctx, c1)
    else:
        chains[0] = emit_ln_chain(ctx, 0, 2, 3, gates[0])
        chains[1] = emit_ln_chain(ctx, 1, 2, 3, gates[1])
    applied = pre_applied or emit_apply(ctx, 0, chains[0],
                                        extra_wait=[ctx.tok("lnfree")])
    pend_a = None

    ksm_t = None
    kern3_t = None
    pool_t = ctx.tok("poolfree")
    ksm_dma = [None] * NG
    k3t_dma = [None] * NG
    tpose_done = [None] * NG
    vmm_by = [None] * NG
    last_kmm = None
    last_exp = last_vev = None
    pend_tpose = None
    ln3_t = None

    def do_tpose(gp, k3_tok, spill=True):
        last = None
        ev = None
        evs = []
        prev_spill = k3t_dma[gp - 1] if gp >= 1 else ctx.tok("k3evfree")
        for tl in range(4):
            half = "trA0" if tl % 2 == 0 else "trA1"
            hap = h["trA"][:, (tl % 2) * 1024:(tl % 2) * 1024 + 1024]
            tk = None
            for c in range(NCH):
                tk = ctx.tr(hap[:, c * 128:(c + 1) * 128],
                            c3(kern3_buf[:, :])[:, c, tl * 128:(tl + 1) * 128],
                            wait=([k3_tok, ctx.guard.get(half),
                                   ctx.pre_ident] if c == 0 else ()),
                            inc=((("T", 1),) if c == NCH - 1 else ()))
            ev = ctx.bank_read(half, ctx.act(
                k3t_buf[:, tl * 1024:(tl + 1) * 1024], hap, AF.Copy,
                wait=[tk, prev_spill]))
            evs.append(ev)
            last = tk
        dma = None
        if spill:
            dma = ctx.dma("sync",
                          out=h["k3t_scr"][gp * 4:(gp + 1) * 4, :, :]
                              .rearrange("t p d -> p t d"),
                          in_=k3t_buf[:, :].rearrange("p (t d) -> p t d", t=4),
                          wait=[ev, k3t_dma[gp - 1] if gp >= 1 else None],
                          sem="K3")
        return last, dma, evs

    def rebcast(g):
        par = g % 2
        psl = slice(par * GT, (par + 1) * GT)
        gsl = slice(g * GT, (g + 1) * GT)
        mm_a = ctx.mm(h["G2"][:, 0:GT], h["onesr_at"][h["ab_bp"][2]],
                      h["ab_r"][2][:, gsl], start=True, stop=True,
                      wait=[chains[g]["b_tok"], ctx.guard.get("G2")],
                      inc=(("T", 1),))
        mm_b = ctx.mm(h["G3"][:, 0:GT], h["onesr_at"][h["ab_bp"][3]],
                      h["ab_r"][3][:, gsl], start=True, stop=True,
                      wait=[chains[g]["b_tok"], ctx.guard.get("G3")],
                      inc=(("T", 1),))
        ev_a = ctx.bank_read("G2", ctx.act(h["ab_a"][:, psl], h["G2"][:, 0:GT],
                                           AF.Copy,
                                           wait=[mm_a, ctx.tok("aba%d" % par)]))
        ev_b = ctx.bank_read("G3", ctx.act(h["ab_b"][:, psl], h["G3"][:, 0:GT],
                                           AF.Copy,
                                           wait=[mm_b, ctx.tok("abb%d" % par)]))
        return dict(ev_a=ev_a, ev_b=ev_b, par=par)

    for g in range(NG):
        gsl = slice(g * GT, (g + 1) * GT)
        ln_t = applied
        exp = {}
        for dd, bank, tok in proj_mms(ctx, slots[1], w_toks[1], lnb1[g % 2],
                                      ln_t):
            exp[dd] = ctx.bank_read(bank, ctx.act(
                c3(expk_buf[:, :])[:, dd, :], h[bank][:, 0:GT], AF.Exp,
                bias=h["biask"][:, dd:dd + 1],
                wait=[tok, ksm_t, rg_free if g == 0 else None]))
            last_kmm = tok
        last_exp = exp[NCH - 1]
        if g + 2 < NG:
            pend_a = chain_p01(ctx, g + 2, 2, 3, gates[g + 2])
        vev = {}
        for dd, bank, tok in proj_mms(ctx, slots[2], w_toks[2], lnb1[g % 2],
                                      ln_t):
            vev[dd] = ctx.bank_read(bank, ctx.act(
                c3(v_buf[:, :])[:, dd, :], h[bank][:, 0:GT], AF.Identity,
                bias=h["bv"][:, dd:dd + 1],
                wait=[tok, kern3_t, rg_free if g == 0 else None]))
            vmm_by[g] = tok
        last_vev = vev[NCH - 1]
        sum_mm = ctx.mmacc("G2", h["G2"][0:1, 0:GT],
                           [h["ones_col"][:, :]] * NCH,
                           [c3(expk_buf[:, :])[:, c, :] for c in range(NCH)],
                           wait_first=[last_exp])
        ln_s = ctx.bank_read("G2", ctx.act(h["inv_f0"], h["G2"][0:1, 0:GT],
                                           AF.Ln, wait=[sum_mm]))
        inv_t = ctx.act(h["invb0"], h["inv_f0"], AF.Exp, scale=-1.0,
                        wait=[ln_s])
        invbc_mm = ctx.mm(h["G2"][:, 0:GT], h["onesr_at"][0], h["invb0"],
                          start=True, stop=True,
                          wait=[inv_t, ctx.guard.get("G2")], inc=(("T", 1),))
        invbc_ev = ctx.bank_read("G2", ctx.act(h["invb"][:, :],
                                               h["G2"][:, 0:GT], AF.Copy,
                                               wait=[invbc_mm, ksm_t]))
        if pend_a is not None:
            chains[pend_a["g"]] = chain_p2(ctx, pend_a)
            pend_a = None
        if g == NG - 1:
            ch0 = rebcast(0)
            ln3_t = emit_apply(ctx, 0, ch0, extra_wait=[vmm_by[NG - 2]],
                               dst=lnbufs[0])
        if g + 1 < NG:
            applied = emit_apply(ctx, g + 1, chains[g + 1],
                                 extra_wait=[vmm_by[g - 1] if g >= 1
                                             else ctx.tok("xkfree")],
                                 dst=lnb1[(g + 1) % 2])
        if pend_tpose is not None:
            gp, tokp = pend_tpose
            tpose_done[gp] = do_tpose(gp, tokp)
            k3t_dma[gp] = tpose_done[gp][1]
        ksm_t = ctx.tt(c3(ksm_buf[:, :]), c3(expk_buf[:, :]),
                       h["invb"][:, :].unsqueeze(1).broadcast_to([128, NCH, GT]),
                       ALU.mult,
                       wait=[invbc_ev, last_exp,
                             ksm_dma[g - 1] if g else ctx.tok("ksmfree")])
        kern3_t = ctx.tt(c3(kern3_buf[:, :]), c3(ksm_buf[:, :]),
                         c3(v_buf[:, :]), ALU.mult,
                         wait=[ksm_t, last_vev,
                               tpose_done[g - 1][0] if g
                               else ctx.tok("kactfree")])
        poolr = ctx.red(h["kland_f"][:, :].rearrange("p (c l) -> p c l", c=NCH),
                        ksm_buf[:, :].rearrange("p (c o i) -> p c o i",
                                                c=NCH, i=32),
                        ALU.add, wait=[ksm_t, pool_t])
        pool_t = ctx.vcopy(h["kland"][:, :].rearrange(
                               "p (c l) -> p c l", c=NCH)[:, :, g * 16:(g + 1) * 16],
                           h["kland_f"][:, :].rearrange("p (c l) -> p c l",
                                                        c=NCH),
                           wait=[poolr])
        ksm_dma[g] = ctx.dma("sync",
                             out=h["ksm_scr"][:, :, gsl]
                                 .rearrange("c p t -> p c t"),
                             in_=c3(ksm_buf[:, :]), wait=[ksm_t], sem="KS")
        pend_tpose = (g, kern3_t)
    ctx.set_tok("poolfree", None)
    ctx.set_tok("rgfree", None)

    expq_buf = h["rg"]
    qsm_buf = h["krelu"]
    k1_evs = [None] * NG
    k1tok_by = [None] * NG
    qmm_by = [None] * NG
    qsm_t = None
    rd_toks = [None] * NG
    mid_mm = None

    def emit_k1(gq, qsm_tok):
        gsl = slice(gq * GT, (gq + 1) * GT)
        k1tok_by[gq] = ctx.mmacc(
            "G3", h["G3"][0:NL, 0:GT],
            [h["kland"][:, :].rearrange("p (c l) -> p c l", c=NCH)[:, c, :]
             for c in range(NCH)],
            [c3(qsm_buf[:, :])[:, c, :] for c in range(NCH)],
            wait_first=[qsm_tok, pool_t])
        k1_evs[gq] = ctx.bank_read("G3", ctx.vcopy(h["kern1T"][0:NL, gsl],
                                                   h["G3"][0:NL, 0:GT],
                                                   wait=[k1tok_by[gq]]))

    def emit_rd1(q, extra):
        return ctx.dma("sync",
                       out=c3(h["xk"][:, :]),
                       in_=h["ksm_scr"][:, :, q * GT:(q + 1) * GT]
                           .rearrange("c p t -> p c t"),
                       wait=[ksm_dma[NG - 1]] + extra, sem="RK0")

    def emit_rd2(q, extra):
        return ctx.dma("sync",
                       out=h["kact"][:, :].rearrange("p (t d) -> p t d", t=4),
                       in_=h["k3t_scr"][q * 4:(q + 1) * 4, :, :]
                           .rearrange("t p d -> p t d"),
                       wait=[k3t_dma[q]] + extra, sem="RK1")

    nt_ctr = [0]

    def emit_chunk(q, rd1, rd2, xr_evs=None):
        nonlocal mid_mm
        for tl in range(4):
            nt = nt_ctr[0]
            nt_ctr[0] += 1
            bank = "G0" if tl % 2 == 0 else "G1"
            ktok = ctx.mmacc(bank, h[bank][:, 0:NL],
                             [c3(h["xk"][:, :])[:, c, tl * 128:(tl + 1) * 128]
                              for c in range(NCH)],
                             [h["kland"][:, :].rearrange("p (c l) -> p c l",
                                                         c=NCH)[:, c, :]
                              for c in range(NCH)],
                             wait_first=[rd1, rd2, pool_t])
            k2_ev = ctx.bank_read(bank, ctx.vcopy(
                h["kern2T"][:, (nt % 2) * NL:(nt % 2 + 1) * NL],
                h[bank][:, 0:NL], wait=[ktok]))
            rhs_buf = k3t_buf if xr_evs is not None else h["kact"]
            for hh in range(2):
                mid_mm = ctx.mm(
                    h["M"][:, hh * 512:(hh + 1) * 512],
                    h["kern2T"][:, (nt % 2) * NL:(nt % 2 + 1) * NL],
                    rhs_buf[:, tl * 1024 + hh * 512:tl * 1024 + (hh + 1) * 512],
                    start=(nt == 0), stop=(nt == 15),
                    wait=([k2_ev, ctx.guard.get("M")] +
                          ([xr_evs[tl]] if xr_evs is not None else [])
                          if hh == 0 else ()),
                    inc=(("T", 1),) if hh == 1 else ())

    ln3_next = None
    for g in range(NG):
        if g + 1 < NG:
            ch = rebcast(g + 1)
            ln3_next = emit_apply(ctx, g + 1, ch,
                                  extra_wait=[qmm_by[g - 1] if g >= 1
                                              else ksm_dma[NG - 1]],
                                  dst=lnbufs[(g + 1) % 2])
        exq = {}
        for dd, bank, tok in proj_mms(
                ctx, slots[0], w_toks[0], lnbufs[g % 2], ln3_t,
                extra_first=([last_exp, ksm_t] if g == 0 else ())):
            exq[dd] = ctx.bank_read(bank, ctx.act(
                c3(expq_buf[:, :])[:, dd, :], h[bank][:, 0:GT], AF.Exp,
                bias=h["biasq"][:, dd:dd + 1], wait=[tok, qsm_t]))
            qmm_by[g] = tok
        if g == 0:
            gp, tokp = pend_tpose
            tpose_done[gp] = do_tpose(gp, tokp, spill=False)
            rd1 = emit_rd1(3, [vmm_by[NG - 1]])
            emit_chunk(3, rd1, None, xr_evs=tpose_done[gp][2])
            rd_toks[0] = (emit_rd1(0, [mid_mm]),
                          emit_rd2(0, [tpose_done[gp][0]]))
        else:
            emit_k1(g - 1, qsm_t)
            emit_chunk(g - 1, *rd_toks[g - 1])
            if g < NG - 1:
                rd_toks[g] = (emit_rd1(g, [mid_mm]), emit_rd2(g, [mid_mm]))
        ln3_t = ln3_next
        sum_mm = ctx.mmacc("G2", h["G2"][0:1, 0:GT],
                           [h["ones_col"][:, :]] * NCH,
                           [c3(expq_buf[:, :])[:, c, :] for c in range(NCH)],
                           wait_first=[exq[NCH - 1]])
        ln_s = ctx.bank_read("G2", ctx.act(h["inv_f1"], h["G2"][0:1, 0:GT],
                                           AF.Ln, wait=[sum_mm]))
        inv3 = ctx.act(h["invb1"], h["inv_f1"], AF.Exp, scale=-1.0,
                       wait=[ln_s])
        invbc_mm = ctx.mm(h["G2"][:, 0:GT], h["onesr_at"][0], h["invb1"],
                          start=True, stop=True,
                          wait=[inv3, ctx.guard.get("G2")], inc=(("T", 1),))
        invbc_ev = ctx.bank_read("G2", ctx.act(
            h["invb"][:, :], h["G2"][:, 0:GT], AF.Copy,
            wait=[invbc_mm, qsm_t, ksm_t if g == 0 else None]))
        qsm_t = ctx.tt(c3(qsm_buf[:, :]), c3(expq_buf[:, :]),
                       h["invb"][:, :].unsqueeze(1).broadcast_to([128, NCH, GT]),
                       ALU.mult,
                       wait=[invbc_ev, exq[NCH - 1],
                             k1tok_by[g - 1] if g >= 1 else None,
                             kern3_t if g == 0 else None])
    mid_ev = ctx.bank_read("M", ctx.act(h["mid"][:, :], h["M"][:, :], AF.Copy,
                                        scale=1.0 / 1024.0, wait=[mid_mm]))

    ctx.set_tok("xrfree", mid_mm)
    ctx.set_tok("xkfree", mid_mm)
    ctx.set_tok("lnfree", qmm_by[2])
    ctx.set_tok("difffree", qmm_by[3])
    ctx.set_tok("rgfree", qsm_t)
    ctx.set_tok("kactfree", mid_mm)
    ctx.set_tok("ksmfree", None)
    ctx.set_tok("k3evfree", None)

    res_toks = [None] * NG
    for g in range(NG):
        gsl = slice(g * GT, (g + 1) * GT)
        rtok = None
        for dd in range(NCH):
            bank = "G0" if dd % 2 == 0 else "G1"
            omm = ctx.mm(h[bank][:, 0:GT],
                         h["mid"][:, dd * 128:(dd + 1) * 128],
                         h["kern1T"][0:NL, gsl], start=True, stop=True,
                         wait=[k1_evs[g], mid_ev, ctx.guard.get(bank)],
                         inc=(("T", 1),))
            rtok = ctx.bank_read(bank, ctx.tt(
                c3(h["x"][:, :], T)[:, dd, gsl],
                c3(h["x"][:, :], T)[:, dd, gsl],
                h[bank][:, 0:GT], ALU.add, wait=[omm]))
        res_toks[g] = rtok
        if g == 0:
            emit_k1(NG - 1, qsm_t)
        if tail_emits and g < len(tail_emits):
            for cb in tail_emits[g]:
                cb(res_toks)
    return dict(slot_last=(qmm_by[NG - 1], last_kmm, vmm_by[NG - 1]),
                res=res_toks)



def emit_preamble(ctx):
    h = ctx.h
    ones_t = ctx.op("vector", "memset", h["ones_col"][:, :], 1.0,
                    inc=(("V", 1),))
    ctx.op("vector", "memset", h["ones_row"][:, :], 1.0, inc=(("V", 1),))
    eps_t = ctx.op("vector", "memset", h["cst_eps"][:, :], EPS,
                   inc=(("V", 1),))
    id0 = ctx.op("gpsimd", "memset", h["ident"][:, :], 0.0, inc=(("G", 1),))
    id_t = ctx.op("gpsimd", "affine_select", out=h["ident"][:, :],
                  in_=h["ident"][:, :], compare_op=ALU.not_equal, fill=1.0,
                  base=0, pattern=[[-1, 128]], channel_multiplier=1,
                  wait=[id0], inc=(("G", 1),))
    ctx.pre_ident = id_t
    ctx.dma("gpsimd", out=h["vecs_sb"][:, :], in_=h["vecs_hbm"][:, :],
            sem="PV")
    bfl_t = None
    for i, dst in enumerate((h["bflat0"], h["bflat1"])):
        bfl_t = ctx.dma("gpsimd",
                        out=dst.rearrange("o (c p) -> o c p", c=8),
                        in_=h["vecs_hbm"][:, i * 8:(i + 1) * 8]
                            .rearrange("p (o c) -> o c p", o=1),
                        sem="PV", allow_slow_non_contiguous=True)
    vec_t = bfl_t
    wq_t = cast_weight_sbuf(ctx, "w0", "wq_hbm")
    wk_t = cast_weight_sbuf(ctx, "w1", "wk_hbm", wait=[wq_t])
    spill_q = ctx.dma("sync", out=h["wq_bf"][:, :], in_=h["w0"][:, :],
                      wait=[wq_t], sem="W0")
    spill_k = ctx.dma("sync", out=h["wk_bf"][:, :], in_=h["w1"][:, :],
                      wait=[wk_t], sem="W1")
    c_wr = cast_weight_hbm(ctx, "wr_bf", "wr_hbm", wait=[wk_t])
    c_wkf = cast_weight_hbm(ctx, "wkf_bf", "wkf_hbm", wait=[c_wr])
    c_wvf = cast_weight_hbm(ctx, "wvf_bf", "wvf_hbm", wait=[c_wkf])
    c_wv = cast_weight_hbm(ctx, "wv_bf", "wv_hbm", wait=[c_wvf])

    b_toks = {}
    last_sq = None
    for i, (slot, wt, brow) in enumerate((("w0", wq_t, 0), ("w1", wk_t, 1))):
        sq0 = ctx.act(h["xr"][:, :], h[slot][:, 0:4096], AF.Square,
                      wait=[wt, last_sq])
        sq1 = ctx.act(h["xk"][:, :], h[slot][:, 4096:8192], AF.Square,
                      wait=[wt])
        cn_mm = {}
        for dg in range(2):
            bank = "G2" if dg == 0 else "G3"
            lhs, rhs = [], []
            for c in range(NCH):
                buf = h["xr"] if c < 4 else h["xk"]
                lhs.append(h["ones_col"][:, :])
                rhs.append(buf[:, (c % 4) * 1024 + dg * 512:
                               (c % 4) * 1024 + dg * 512 + 512])
            cn_mm[dg] = ctx.mmacc(bank, h[bank][0:1, 0:512], lhs, rhs,
                                  wait_first=[sq0, sq1, ones_t])
        last_sq = cn_mm[1]
        ev0 = ctx.bank_read("G2", ctx.act(h["m_row"][:, 0:512],
                                          h["G2"][0:1, 0:512], AF.Copy,
                                          wait=[cn_mm[0]]))
        ev1 = ctx.bank_read("G3", ctx.act(h["m_row"][:, 512:1024],
                                          h["G3"][0:1, 0:512], AF.Copy,
                                          wait=[cn_mm[1]]))
        lnt = ctx.act(h["s2_row"][:, 0:1024], h["m_row"][:, 0:1024], AF.Ln,
                      bias=h["cst_eps"][0:1, 0:1], wait=[ev0, ev1, eps_t])
        cnt = ctx.act(h["m_row"][:, 0:1024], h["s2_row"][:, 0:1024], AF.Exp,
                      scale=0.5, wait=[lnt])
        m1 = ctx.red(h["stat_m2"][:, 0:1], h["m_row"][:, 0:1024], ALU.max,
                     wait=[cnt])
        m2 = ctx.red(h["stat_m2"][:, 1:2],
                     h["bflat0"] if brow == 0 else h["bflat1"],
                     ALU.max, apply_absolute_value=True, wait=[bfl_t])
        b1 = ctx.ts(h["stat_m2"][:, 2:3], h["stat_m2"][:, 0:1], 32.0,
                    ALU.mult, wait=[m1])
        b2 = ctx.tt(h["stat_m2"][:, 2:3], h["stat_m2"][:, 2:3],
                    h["stat_m2"][:, 1:2], ALU.add, wait=[b1, m2])
        b3 = ctx.ts(h["stat_m2"][:, 2:3], h["stat_m2"][:, 2:3], 2.0,
                    ALU.add, wait=[b2])
        bf_c = ctx.vcopy(h["stat_bf"][0:1, 0:1], h["stat_m2"][:, 2:3],
                         wait=[b3])
        bmm = ctx.mm(h["G2"][:, 0:1], h["onesr_at"][0],
                     h["stat_bf"][0:1, 0:1], start=True, stop=True,
                     wait=[bf_c, ctx.guard.get("G2")], inc=(("T", 1),))
        bev = ctx.bank_read("G2", ctx.act(h["Bcol"][:, i:i + 1],
                                          h["G2"][:, 0:1], AF.Copy,
                                          wait=[bmm]))
        boff = 0 if i == 0 else 8
        b_toks[i] = ctx.op("vector", "tensor_scalar",
                           h["biasqk"][:, i * 8:(i + 1) * 8],
                           h["vecs_sb"][:, boff:boff + 8],
                           h["Bcol"][:, i:i + 1], None, ALU.subtract,
                           wait=[bev, vec_t], inc=(("V", 1),))
    ctx.set_tok("m2", b_toks[1])
    ctx.set_tok("lnx", b_toks[1])
    ctx.set_tok("mrow", b_toks[1])
    ctx.set_tok("xrfree", last_sq)
    ctx.set_tok("xkfree", last_sq)
    return dict(bias_done=(b_toks[0], b_toks[1]),
                bcomp_last=(last_sq, b_toks[0], b_toks[1]),
                vecs=vec_t, spills=(spill_q, spill_k),
                casts=dict(wr=c_wr, wkf=c_wkf, wvf=c_wvf, wv=c_wv))



def program(ctx):
    h = ctx.h
    pre = emit_preamble(ctx)
    spill_q, spill_k = pre["spills"]
    casts = pre["casts"]

    wr_t = load_weight(ctx, "w0", "wr_bf",
                       wait=list(pre["bcomp_last"]) + [spill_q, casts["wr"]])
    wkf_t = load_weight(ctx, "w1", "wkf_bf",
                        wait=list(pre["bcomp_last"]) + [spill_k, casts["wkf"]])
    wvf_t = load_weight(ctx, "w2", "wvf_bf", wait=[casts["wvf"]])
    ffn_slots = ("w0", "w1", "w2")
    ffn_toks = (wr_t, wkf_t, wvf_t)

    x_loads = None
    f1st = None
    f2 = None
    for s in range(NSEQ):
        if s == 0:
            x_loads = [ctx.dma(
                "sync",
                out=c3(h["x"][:, :], T)[:, :, g * GT:(g + 1) * GT],
                in_=h["x_hbm"][s, :, :, g * GT:(g + 1) * GT]
                    .rearrange("c p t -> p c t"),
                sem="XI%d" % g) for g in range(NG)]
        gates1 = [[x_loads[g]] + (list(pre["bias_done"]) if s == 0 else [])
                  for g in range(NG)]
        attn_pre = {}
        ap = {}

        def a0_p01(res, xl):
            ap["c0"] = chain_p01(ctx, 0, 2, 3, [res[0]])

        def a0_p2(res, xl):
            attn_pre[0] = chain_p2(ctx, ap["c0"])

        def a1_p01(res, xl):
            attn_pre[1] = chain_p01(ctx, 1, 2, 3, [res[1]])

        def a_ap0(res, xl):
            attn_pre["ap0"] = emit_apply(ctx, 0, attn_pre[0],
                                         extra_wait=[ctx.tok("lnfree")])
        tails1 = {(2, "mid2"): [a0_p01],
                  (2, "post"): [a0_p2],
                  (3, "mid2"): [a1_p01, a_ap0]}
        f1 = emit_ffn(ctx, s, ffn_slots, ffn_toks, False, gates1, st=f1st,
                      tails=tails1)
        sl = f1["slot_last"]
        wq_t = load_weight(ctx, ffn_slots[0], "wq_bf", wait=[sl[0], spill_q])
        wka_t = load_weight(ctx, ffn_slots[1], "wk_bf", wait=[sl[1], spill_k])
        wva_t = load_weight(ctx, ffn_slots[2], "wv_bf",
                            wait=[sl[2], casts["wv"]])
        gates2 = [[f1["res"][g]] for g in range(NG)]
        f2st = new_ffn_state()
        f2p = {}

        def f20_p01(res):
            f2p["c0"] = chain_p01(ctx, 0, 0, 1, [res[0]], cast_eng="scalar")

        def f20_p2(res):
            f2st["chains"][0] = chain_p2(ctx, f2p["c0"])

        def f21_p01(res):
            f2st["pend1"] = chain_p01(ctx, 1, 0, 1, [res[1]],
                                      cast_eng="scalar")

        def f2_prep(res):
            ffn_prep(ctx, 0, f2st, extra_wait=[ctx.tok("lnfree")])

        def f2_mix(res):
            f2st["mix_r"] = ffn_mixes(ctx, 0, "r", ctx.tok("xrfree"), f2st)
            f2st["mix_k"] = ffn_mixes(ctx, 0, "k", ctx.tok("xkfree"), f2st)
        tails3b = [[f20_p01], [f20_p2, f2_prep, f21_p01], [f2_mix], []]
        a1 = emit_attn(ctx, s, ffn_slots, (wq_t, wka_t, wva_t), gates2,
                       pre_chains=attn_pre, pre_applied=attn_pre["ap0"],
                       tail_emits=tails3b)
        asl = a1["slot_last"]
        new_slots = (ffn_slots[1], ffn_slots[2], ffn_slots[0])
        wr_t = load_weight(ctx, new_slots[0], "wr_bf",
                           wait=[asl[1], casts["wr"]])
        wkf_t = load_weight(ctx, new_slots[1], "wkf_bf",
                            wait=[asl[2], casts["wkf"]])
        wvf_t = load_weight(ctx, new_slots[2], "wvf_bf",
                            wait=[asl[0], casts["wvf"]])
        ffn_slots = new_slots
        ffn_toks = (wr_t, wkf_t, wvf_t)
        gates3 = [[a1["res"][g]] for g in range(NG)]
        if s + 1 < NSEQ:
            f1st = new_ffn_state()
            nst = f1st
            np_ = {}

            def n0_p01(res, xl):
                np_["c0"] = chain_p01(ctx, 0, 0, 1, [xl[0]])

            def n0_p2(res, xl):
                nst["chains"][0] = chain_p2(ctx, np_["c0"])

            def n_prep(res, xl):
                ffn_prep(ctx, 0, nst, extra_wait=[ctx.tok("lnfree")])

            def n1_p01(res, xl):
                nst["pend1"] = chain_p01(ctx, 1, 0, 1, [xl[1]])

            def n_mix(res, xl):
                nst["mix_r"] = ffn_mixes(ctx, 0, "r", ctx.tok("xrfree"), nst)
                nst["mix_k"] = ffn_mixes(ctx, 0, "k", ctx.tok("xkfree"), nst)
            tails2 = {(2, "mid2"): [n0_p01],
                      (2, "post"): [n0_p2],
                      (3, "mid"): [n_prep],
                      (3, "mid2"): [n1_p01, n_mix]}
            oxl = (s + 1,)
        else:
            f1st = None
            tails2 = None
            oxl = None
        f2 = emit_ffn(ctx, s, ffn_slots, ffn_toks, True, gates3, st=f2st,
                      tails=tails2, out_xload=oxl)
        x_loads = f2["xloads"]

    ctx.op("sync", "nop", wait=list(f2["out"]))



def build_nc():
    import contextlib
    nc = bass.Bass("TRN2", target_bir_lowering=False, debug=False,
                   detect_race_conditions=False)
    h = {}
    h["x_hbm"] = nc.dram_tensor("x_in", [NSEQ, NCH, 128, T], F32,
                                kind="ExternalInput").ap()
    h["out_hbm"] = nc.dram_tensor("out", [NSEQ, NCH, 128, T], F32,
                                  kind="ExternalOutput").ap()
    for nm in ("wq", "wk", "wv", "wr", "wkf", "wvf"):
        h[nm + "_hbm"] = nc.dram_tensor(nm, [NCH, 128, 1024], F32,
                                        kind="ExternalInput").ap()
    h["vecs_hbm"] = nc.dram_tensor("vecs", [128, 40], F32,
                                   kind="ExternalInput").ap()
    h["ksm_scr"] = nc.dram_tensor("ksm_scr", [NCH, 128, T], BF16,
                                  kind="Internal").ap()
    h["k3t_scr"] = nc.dram_tensor("k3t_scr", [16, 128, 1024], BF16,
                                  kind="Internal").ap()
    for nm in ("wq", "wk", "wv", "wr", "wkf", "wvf"):
        h[nm + "_bf"] = nc.dram_tensor(nm + "_bf", [128, NCH * 1024], BF16,
                                       kind="Internal").ap()

    stack = contextlib.ExitStack()

    def sb(name, shape, dtype):
        return stack.enter_context(nc.sbuf_tensor(name, shape, dtype))

    def ps(name, shape, dtype):
        return stack.enter_context(nc.psum_tensor(name, shape, dtype))

    h["x"] = sb("x", [128, NCH * T], F32)
    for w in ("w0", "w1", "w2"):
        h[w] = sb(w, [128, NCH * 1024], BF16)
    for nm in ("ln", "diff", "xr", "xk", "rg", "krelu", "kact"):
        h[nm] = sb(nm, [128, NCH * GT], BF16)
    h["ksm_g"] = h["diff"]
    h["k3t_ev"] = h["xr"]
    h["castb"] = sb("castb", [128, 2 * GT], BF16)
    h["sqb"] = sb("sqb", [128, 2 * GT], BF16)
    h["stg"] = sb("stg", [128, 2 * GT], BF16)
    h["ab_a"] = sb("ab_a", [128, 2 * GT], BF16)
    h["ab_b"] = sb("ab_b", [128, 2 * GT], BF16)
    h["invb"] = sb("invb", [128, GT], BF16)
    h["sums"] = sb("sums", [33, T], F32)
    h["ab"] = sb("ab", [65, 2 * T], BF16)
    h["stat_tmp"] = sb("stat_tmp", [33, GT], F32)
    h["stat_bf"] = sb("stat_bf", [1, 4], BF16)
    h["inv_bb"] = sb("inv_bb", [1, GT], BF16)
    h["kern2T"] = sb("kern2T", [128, 2 * NL], BF16)
    h["mid"] = sb("mid", [NL, 1024], BF16)
    h["kern1T"] = sb("kern1T", [NL, T], BF16)
    h["kland"] = sb("kland", [128, NCH * NL], BF16)
    h["kland_f"] = sb("kland_f", [128, NCH * 16], F32)
    h["vecs_sb"] = sb("vecs_sb", [128, 40], F32)
    h["biasqk"] = sb("biasqk", [128, 16], F32)
    h["Bcol"] = sb("Bcol", [128, 2], F32)
    h["ones_col"] = sb("ones_col", [128, 1], BF16)
    h["ones_row"] = sb("ones_row", [65, 128], BF16)
    h["ident"] = sb("ident", [128, 128], BF16)
    h["lnprev"] = sb("lnprev", [128, 2 * NCH], BF16)
    h["cst_eps"] = sb("cst_eps", [1, 1], F32)

    h["G0"] = ps("G0", [128, 512], F32)
    h["G1"] = ps("G1", [128, 512], F32)
    h["G2"] = ps("G2", [128, 512], F32)
    h["G3"] = ps("G3", [128, 512], F32)
    h["M"] = ps("M", [NL, 1024], F32)
    h["trA"] = ps("trA", [128, 2048], BF16)

    h["m_row"] = h["sums"][0:1, 0:T]
    h["s2_row"] = h["sums"][32:33, 0:T]
    h["ab_r"] = [h["ab"][0:1, 0:T], h["ab"][32:33, 0:T],
                 h["ab"][0:1, T:2 * T], h["ab"][64:65, 0:T]]
    h["ab_bp"] = [0, 32, 0, 64]
    h["bflat0"] = h["sums"][0:1, 1024:2048]
    h["bflat1"] = h["sums"][32:33, 1024:2048]
    h["invb0"] = h["inv_bb"][0:1, 0:GT]
    h["invb1"] = h["inv_bb"][0:1, 0:GT]
    h["inv_f0"] = h["stat_tmp"][0:1, 0:GT]
    h["inv_f1"] = h["stat_tmp"][0:1, 0:GT]
    h["stat_m2"] = h["stat_tmp"][32:33, 0:GT]
    h["stat_ln"] = h["stat_tmp"][0:1, 0:GT]
    h["onesr_at"] = {0: h["ones_row"][0:1, :], 32: h["ones_row"][32:33, :],
                     64: h["ones_row"][64:65, :]}
    h["mur"] = h["vecs_sb"][:, 24:32]
    h["muk"] = h["vecs_sb"][:, 32:40]
    h["bv"] = h["vecs_sb"][:, 16:24]
    h["biasq"] = h["biasqk"][:, 0:8]
    h["biask"] = h["biasqk"][:, 8:16]

    sem_names = ("T", "V", "S", "G", "W0", "W1", "W2", "PV", "WC",
                 "XI0", "XI1", "XI2", "XI3", "XO0", "XO1", "XO2", "XO3",
                 "KS", "K3", "RK0", "RK1")
    sems = {nm: stack.enter_context(nc.semaphore("sem_" + nm))
            for nm in sem_names}

    with nc.Block() as block:
        def run(engine):
            def f(_eng):
                local = {nm: [sems[nm], 0] for nm in sem_names}
                ctx = Ctx(nc, engine, local, h)
                program(ctx)
            return f
        block.sync(run("sync"))
        block.gpsimd(run("gpsimd"))
        block.tensor(run("tensor"))
        block.scalar(run("scalar"))
        block.vector(run("vector"))

    stack.close()
    return nc



_NC_CACHE = {}


def kernel(**inputs):
    x = np.asarray(inputs["L0_attn_hidden_states"], dtype=np.float32)
    B, TT_, D = x.shape
    n_cores = 8
    per = B // n_cores

    xfm = np.ascontiguousarray(
        x.reshape(B, TT_, NCH, 128).transpose(0, 2, 3, 1))

    def warr(w):
        return np.ascontiguousarray(
            np.asarray(w, dtype=np.float32).reshape(NCH, 128, 1024))

    wq = warr(inputs["L0_attn_Wq"])
    wk = warr(inputs["L0_attn_Wk"])
    wv = warr(inputs["L0_attn_Wv"])
    wr = warr(np.asarray(inputs["L0_ffn_W_r"], dtype=np.float32).T)
    wkf = warr(np.asarray(inputs["L0_ffn_W_k"], dtype=np.float32).T)
    wvf = warr(np.asarray(inputs["L0_ffn_W_v"], dtype=np.float32).T)

    def pcol(v):
        return np.ascontiguousarray(
            np.asarray(v, dtype=np.float32).reshape(NCH, 128).T)

    vecs = np.ascontiguousarray(np.concatenate(
        [pcol(inputs["L0_attn_bq"]), pcol(inputs["L0_attn_bk"]),
         pcol(inputs["L0_attn_bv"]), pcol(inputs["L0_ffn_mu_r"]),
         pcol(inputs["L0_ffn_mu_k"])], axis=1))

    if "nc" not in _NC_CACHE:
        _NC_CACHE["nc"] = build_nc()
    nc = _NC_CACHE["nc"]

    in_maps = []
    for c in range(n_cores):
        in_maps.append({
            "x_in": np.ascontiguousarray(xfm[c * per:(c + 1) * per]),
            "wq": wq, "wk": wk, "wv": wv,
            "wr": wr, "wkf": wkf, "wvf": wvf,
            "vecs": vecs,
        })
    res = run_bass_kernel_spmd(nc, in_maps, core_ids=list(range(n_cores)))
    outs = [res.results[c]["out"] for c in range(n_cores)]
    full = np.concatenate(outs, axis=0).reshape(B, NCH, 128, T)
    out = np.ascontiguousarray(full.transpose(0, 3, 1, 2).reshape(B, TT_, D))
    return out


if __name__ == "__main__":
    build_nc()
    print("build OK")
